# version 13
# baseline (speedup 1.0000x reference)
import os
os.environ.setdefault("TRN_ALLOW_ICI_ALL_TO_ALL", "1")
import numpy as np

import concourse.bass as bass
import concourse.mybir as mybir
from concourse.bass_utils import run_bass_kernel_spmd
from concourse.bass_core_rust import IndirectOffsetOnAxis

F32 = mybir.dt.float32
BF16 = mybir.dt.bfloat16
I32 = mybir.dt.int32
AF = mybir.ActivationFunctionType
ALU = mybir.AluOpType
AX = mybir.AxisListType

NCORE = 8
B, S, D = 8, 2048, 1024
NH, DH = 16, 64
NTILE = S // 128
ENGS = ("t", "v", "s", "g", "y")


class Sem:
    __slots__ = ("h", "n", "name")

    def __init__(self, h, name):
        self.h, self.n, self.name = h, 0, name

    def tick(self, k=1):
        self.n += k
        return self.n


class Ctx:
    def __init__(self, nc):
        self.nc = nc
        self.tapes = {e: [] for e in ENGS}
        self.sems = {}
        self._sn = 0
        self._alloc = []

    prefix = ""
    scope = None

    def sb(self, name, shape, dtype):
        g = self.nc.sbuf_tensor(self.prefix + name, shape, dtype)
        h = g.__enter__()
        self._alloc.append(g)
        return h

    def ps(self, name, shape, dtype=F32):
        g = self.nc.psum_tensor(self.prefix + name, shape, dtype)
        h = g.__enter__()
        self._alloc.append(g)
        return h

    def il(self, name, arr):
        return self.nc.inline_tensor(np.ascontiguousarray(arr),
                                     self.prefix + name)

    def mark(self):
        return len(self._alloc)

    def free_to(self, m):
        while len(self._alloc) > m:
            self._alloc.pop().__exit__(None, None, None)

    def sem(self, name=None):
        if name is None:
            self._sn += 1
            name = f"_anon{self._sn % 8}"
        if name not in self.sems:
            self.sems[name] = Sem(self.nc.alloc_semaphore(name), name)
        return self.sems[name]

    def breg(self, g, val):
        if not hasattr(self, "_bregs"):
            self._bregs = {}
        if val not in self._bregs:
            r = g.alloc_register(f"bnd{len(self._bregs)}")
            g.reg_mov(r, val)
            self._bregs[val] = r
        return self._bregs[val]

    def emit(self, eng, fn, wait=(), inc=(), dep=False, mark=False):
        wait = list(wait)
        inc = list(inc)
        if not hasattr(self, "_last_inc"):
            self._last_inc = {}
        if dep and eng in self._last_inc:
            sm, val = self._last_inc[eng]
            wait.append((sm, val))
        if mark and not inc:
            s = self.sem("self_" + eng)
            inc.append((s, 1))
            s.tick()
            self._last_inc[eng] = (s, s.n)
        elif inc:
            sm, k = inc[-1]
            self._last_inc[eng] = (sm, sm.n + k)
        assert len(inc) <= 1, "one sem update per instruction"
        self.tapes[eng].append((fn, tuple(wait), tuple(inc),
                                self.scope or self.prefix))

    def dma(self, eng, out, in_, wait=(), inc=(), **kw):
        self.emit(eng, (lambda e, o=out, i=in_, kw=kw: e.dma_start(o, i, **kw)),
                  wait=wait, inc=inc)

    def barrier(self, engs=ENGS, all_sems=False):
        bar = self.sem("bar")
        li = getattr(self, "_last_inc", {})
        if all_sems:
            for sm in list(self.sems.values()):
                if sm.name != "bar" and sm.n > 0:
                    self.emit("y", None, wait=[(sm, sm.n)])
        for e in engs:
            w = [li[e]] if e in li else []
            self.emit(e, lambda en: en.nop(), wait=w, inc=[(bar, 1)])
        bar.tick(len(engs))
        tgt = bar.n
        for e in engs:
            self.emit(e, None, wait=[(bar, tgt)])
        for e in engs:
            self._last_inc[e] = (bar, tgt)

    def check(self):
        bad = []
        for e in ENGS:
            for idx, (fn, waits, incs, _tag) in enumerate(self.tapes[e]):
                for sm, val in waits:
                    if val > sm.n:
                        bad.append((e, idx, sm.name, val, sm.n))
        if bad:
            raise RuntimeError(f"unreachable waits: {bad[:10]}")

    def finish(self, nc):
        self.check()
        with nc.Block() as block:
            def mk(key):
                tape = self.tapes[key]

                def run(engine):
                    cur, sid = None, None
                    for fn, waits, incs, tag in tape:
                        if tag != cur:
                            if cur:
                                nc.leave_named_scope(cur, sid, False)
                            sid = (nc.enter_named_scope(tag, False)[0]
                                   if tag else None)
                            cur = tag
                        for sm, val in waits:
                            engine.wait_ge(sm.h, val)
                        if fn is None:
                            continue
                        ins = fn(engine)
                        for sm, k in incs:
                            ins = ins.then_inc(sm.h, k)
                    if cur:
                        nc.leave_named_scope(cur, sid, False)
                return run
            block.tensor(mk("t"))
            block.vector(mk("v"))
            block.scalar(mk("s"))
            block.gpsimd(mk("g"))
            block.sync(mk("y"))
        self.free_to(0)



def _ctx_il0(self, arr, name):
    return self.nc.inline_tensor(np.ascontiguousarray(arr), self.prefix + name)

Ctx.il0 = _ctx_il0


def np_ident():
    return np.eye(128, dtype=np.float32)


def np_rope():
    inv = 1.0 / (10000.0 ** (np.arange(0, DH, 2, dtype=np.float64) / DH))
    tpos = np.arange(S, dtype=np.float64)
    fr = tpos[None, :] * inv[:, None]
    emb = np.concatenate([fr, fr], 0)
    cos2 = np.concatenate([np.cos(emb), np.cos(emb)], 0).astype(np.float32)
    sin2 = np.concatenate([np.sin(emb), np.sin(emb)], 0).astype(np.float32)
    return cos2, sin2


def np_rotP():
    P = np.zeros((128, 128), np.float32)
    for hh in range(2):
        b0 = hh * 64
        for j in range(32):
            P[b0 + 32 + j, b0 + j] = -1.0
            P[b0 + j, b0 + 32 + j] = 1.0
    return P


def np_decay():
    gam = 1.0 - np.exp2(-5.0 - np.arange(NH, dtype=np.float64))
    ii = np.arange(512)[None, :]
    jj = np.arange(128)[:, None]
    Efull = np.stack([g ** (ii - jj + 128) for g in gam], 0)
    i2 = np.arange(128)[None, :]
    Ediag = np.stack([np.where(i2 >= jj, g ** (i2 - jj), 0.0) for g in gam], 0)
    return Efull.astype(np.float32), Ediag.astype(np.float32), gam


def emit_ln_tile(ctx, xt, st4, hslot, sm, pre_wait=()):
    scr = ctx._lnscr
    eps_ = ctx._eps
    ctx.emit("v", lambda v: v.scalar_tensor_tensor(
        scr[:, :], xt, 1.0, xt, ALU.mult, ALU.mult, accum_out=st4[:, 1:2]),
        wait=pre_wait)
    ctx.emit("v", lambda v: v.reduce_sum(st4[:, 0:1], xt, axis=AX.X),
             inc=[(sm, 1)])
    t0 = sm.tick()
    ctx.emit("s", lambda s: s.mul(st4[:, 2:3], st4[:, 0:1], 1.0 / 1024.0),
             wait=[(sm, t0)], mark=True)
    ctx.emit("s", lambda s: s.mul(st4[:, 0:1], st4[:, 1:2], 1.0 / 1024.0),
             dep=True, mark=True)
    ctx.emit("s", lambda s: s.activation(st4[:, 1:2], st4[:, 2:3], AF.Square),
             dep=True, inc=[(sm, 1)])
    t1 = sm.tick()
    ctx.emit("v", lambda v: v.tensor_tensor(st4[:, 0:1], st4[:, 0:1],
                                            st4[:, 1:2], ALU.subtract),
             wait=[(sm, t1)], inc=[(sm, 1)])
    t2 = sm.tick()
    ctx.emit("s", lambda s: s.activation(st4[:, 0:1], st4[:, 0:1], AF.Sqrt,
                                         bias=eps_[:, :]),
             wait=[(sm, t2)], inc=[(sm, 1)])
    t3 = sm.tick()
    ctx.emit("v", lambda v: v.reciprocal(st4[:, 3:4], st4[:, 0:1]),
             wait=[(sm, t3)], mark=True)
    ctx.emit("v", lambda v: v.tensor_scalar(
        hslot, xt, st4[:, 2:3], st4[:, 3:4], ALU.subtract, ALU.mult),
        dep=True, inc=[(sm, 1)])
    return sm.tick()


def emit_ln_transpose_stream(ctx, x_src_tile, hTs, name, identbf, ident_wait,
                             keep_h=None, keep_mrs=None):
    nc = ctx.nc
    ctx._lnscr = ctx.sb(name + "scr", [128, 1024], BF16)
    if not hasattr(ctx, "_eps") or ctx._eps is None:
        pass
    ctx._eps = getattr(ctx, "_eps_t", None)
    if ctx._eps is None:
        ctx._eps_t = ctx.sb(name + "eps", [128, 1], F32)
        ctx._eps = ctx._eps_t
        _e = ctx._eps_t
        ctx.emit("v", lambda v, _e=_e: v.memset(_e[:, :], 1e-5), mark=True)
    stats = ctx.sb(name + "stats", [128, NTILE, 4], F32)
    if keep_h is None:
        hring = ctx.sb(name + "hring", [128, 3, 1024], BF16)
    sm = ctx.sem(name + "ln")
    smtr = ctx.sem(name + "tr")
    smev = ctx.sem(name + "ev")
    pst = ctx.ps(name + "pst", [128, 2, 1024], BF16)
    for i in range(NTILE):
        xt, xw = x_src_tile(i)
        hslot = keep_h[:, i, :] if keep_h is not None else hring[:, i % 3, :]
        w = list(xw)
        if keep_h is None and i >= 3:
            w.append((smev, smev.n - 16))
        hn = emit_ln_tile(ctx, xt, stats[:, i, :], hslot, sm, pre_wait=w)
        if keep_mrs is not None:
            ctx.emit("v", lambda v, i=i: v.tensor_copy(
                keep_mrs[:, i, :], stats[:, i, 2:4]), dep=True, mark=True)
        for c in range(8):
            k = i * 8 + c
            par = k % 2
            w2 = [(sm, hn)] if c == 0 else []
            if k == 0:
                w2 = w2 + list(ident_wait)
            if k >= 2:
                w2.append((smev, smev.n - 1))
            ctx.emit("t", lambda t, hslot=hslot, c=c, par=par: t.transpose(
                pst[:, par, 0:128], hslot[:, c * 128:(c + 1) * 128], identbf[:, :]),
                wait=w2, inc=[(smtr, 1)])
            smtr.tick()
            ctx.emit("s", lambda s, i=i, c=c, par=par: s.copy(
                hTs[:, c, i * 128:(i + 1) * 128], pst[:, par, 0:128]),
                wait=[(smtr, smtr.n)], inc=[(smev, 1)])
            smev.tick()
    return smev


def emit_wT_load(ctx, w_bf_dram, wT, name, wait=()):
    sm = ctx.sem("wl_" + name[-1])
    for mc in range(D // 128):
        ctx.emit("y", lambda y, mc=mc: y.dma_start(
            wT[:, mc, :], w_bf_dram[:, mc * 128:(mc + 1) * 128], transpose=True),
            wait=wait if mc == 0 else (), inc=[(sm, 16)])
        sm.tick(16)
    return sm


def emit_l0_attn_a(ctx, ext):
    nc = ctx.nc
    ctx._eps_t = None
    _m_all = ctx.mark()
    x = ext["x"]
    win = {k: ext[k] for k in ("wq", "wk", "wv", "wg", "wo")}
    xo = ext["xo"]
    wbf = {k: nc.dram_tensor(ctx.prefix + k + "_bf", [D, D], BF16, kind="Internal").ap()
           for k in win}

    cos2, sin2 = np_rope()
    Efull, Ediag, gam = np_decay()
    c_ident = ctx.il0(np_ident(), "identsrc")
    c_cos = ctx.il0(cos2, "ropecos")
    c_sin = ctx.il0(sin2, "ropesin")
    c_rot = ctx.il0(np_rotP(), "rotP")
    c_ef = ctx.il0(Efull, "Efull")
    c_ed = ctx.il0(Ediag, "Ediag")

    xv = x.rearrange("(n p) d -> p n d", p=128)
    xov = xo.rearrange("(n p) d -> p n d", p=128)

    qT = ctx.sb("qT", [128, 8, S], BF16)
    kT = ctx.sb("kT", [128, 8, S], BF16)
    vtm = ctx.sb("vtm", [128, NTILE, 1024], BF16)
    outT = ctx.sb("outT", [128, 8, S], BF16)
    identbf = ctx.sb("identbf", [128, 128], BF16)
    hTs = ctx.sb("hTs", [128, 8, S], BF16)

    smc = ctx.sem("consts")
    ctx.dma("g", identbf[:, :], c_ident.ap()[:, :], inc=[(smc, 16)])
    ident_ok = smc.tick(16)
    smw = ctx.sem("gc_")
    wcast = {}
    for k in win:
        ctx.dma("g", wbf[k][:, :], win[k][:, :], inc=[(smw, 16)])
        smw.tick(16)
    for k in win:
        wcast[k] = smw.n

    m0 = ctx.mark()
    xring = ctx.sb("xring", [128, 3, 1024], F32)
    smx3 = [ctx.sem(f"lx{j}") for j in range(3)]

    def xsrc(i):
        slot = i % 3
        smx = smx3[slot]
        w = []
        if i >= 3:
            w.append((ctx.sems["lnln"], ctx.sems["lnln"].n - 10))
        ctx.dma("y", xring[:, slot, :], xv[:, i, :], wait=w, inc=[(smx, 16)])
        smx.tick(16)
        return xring[:, slot, :], [(smx, smx.n)]

    trsem = emit_ln_transpose_stream(ctx, xsrc, hTs, "ln", identbf,
                                     [(smc, ident_ok)])
    hT_done = trsem.n
    ctx.barrier()
    ctx.free_to(m0)

    m1 = ctx.mark()
    wT = ctx.sb("wT", [128, 8, D], BF16)
    cosS = ctx.sb("cosS", [128, S], BF16)
    sinS = ctx.sb("sinS", [128, S], BF16)
    rotP = ctx.sb("rotPs", [128, 128], BF16)
    qsb = ctx.sb("ropeqsb", [128, 512], BF16)
    t1f = ctx.sb("ropet1", [128, 512], F32)
    psq = ctx.ps("psq", [128, 2, 512], F32)
    psrot = ctx.ps("psrot", [128, 512], F32)
    ctx.dma("g", cosS[:, :], c_cos.ap()[:, :], inc=[(smc, 16)])
    smc.tick(16)
    ctx.dma("g", sinS[:, :], c_sin.ap()[:, :], inc=[(smc, 16)])
    smc.tick(16)
    ctx.dma("g", rotP[:, :], c_rot.ap()[:, :], inc=[(smc, 16)])
    consts2 = smc.tick(16)

    smmm = ctx.sem("mms")
    smev = ctx.sem("evs")

    def fm_proj(wname, outbuf, mode):
        wl = emit_wT_load(ctx, wbf[wname], wT, wname,
                          wait=[(smw, wcast[wname]), (smev, smev.n)])
        wld = wl.n
        for oc in range(8):
            for tg in range(4):
                it = oc * 4 + tg
                par = it % 2
                w2 = [(wl, wld)]
                if it == 0:
                    w2.append((trsem, hT_done))
                if it >= 2:
                    w2.append((smev, smev.n - 1))
                for mc in range(8):
                    ctx.emit("t", lambda t, oc=oc, tg=tg, mc=mc, par=par: t.matmul(
                        psq[:, par, :], wT[:, mc, oc * 128:(oc + 1) * 128],
                        hTs[:, mc, tg * 512:(tg + 1) * 512],
                        start=(mc == 0), stop=(mc == 7)),
                        wait=w2 if mc == 0 else (),
                        inc=[(smmm, 1)] if mc == 7 else [])
                mmw = smmm.tick()
                if mode == "rope":
                    ctx.emit("v", lambda v, tg=tg, par=par: v.tensor_tensor(
                        t1f[:, :], psq[:, par, :], cosS[:, tg * 512:(tg + 1) * 512],
                        ALU.mult), wait=[(smmm, mmw), (smc, consts2)],
                        dep=True, mark=True)
                    ctx.emit("v", lambda v, par=par: v.tensor_copy(
                        qsb[:, :], psq[:, par, :]), dep=True, mark=True,
                        inc=[(smev, 1)])
                    qsb_ready = smev.tick()
                    ctx.emit("t", lambda t: t.matmul(
                        psrot[:, :], rotP[:, :], qsb[:, :], start=True, stop=True),
                        wait=[(smev, qsb_ready), (smc, consts2)], inc=[(smmm, 1)])
                    rot_done = smmm.tick()
                    ctx.emit("v", lambda v, tg=tg: v.tensor_tensor(
                        psrot[:, :], psrot[:, :], sinS[:, tg * 512:(tg + 1) * 512],
                        ALU.mult), wait=[(smmm, rot_done)], dep=True, mark=True)
                    ctx.emit("v", lambda v, oc=oc, tg=tg: v.tensor_tensor(
                        outbuf[:, oc, tg * 512:(tg + 1) * 512], t1f[:, :],
                        psrot[:, :], ALU.add), dep=True, mark=True,
                        inc=[(smev, 1)])
                    smev.tick()
                else:
                    ctx.emit("s", lambda s, oc=oc, tg=tg, par=par: s.copy(
                        outbuf[:, oc, tg * 512:(tg + 1) * 512], psq[:, par, :]),
                        wait=[(smmm, mmw)], inc=[(smev, 1)])
                    smev.tick()

    fm_proj("wq", qT, "rope")
    fm_proj("wk", kT, "rope")

    wlv = emit_wT_load(ctx, wbf["wv"], wT, "wv",
                       wait=[(smw, wcast["wv"]), (smev, smev.n)])
    wlvd = wlv.n
    for i in range(NTILE):
        for og in range(2):
            it = i * 2 + og
            par = it % 2
            w2 = [(wlv, wlvd)]
            if it >= 2:
                w2.append((smev, smev.n - 1))
            for mc in range(8):
                ctx.emit("t", lambda t, i=i, og=og, mc=mc, par=par: t.matmul(
                    psq[:, par, :], hTs[:, mc, i * 128:(i + 1) * 128],
                    wT[:, mc, og * 512:(og + 1) * 512],
                    start=(mc == 0), stop=(mc == 7)),
                    wait=w2 if mc == 0 else (),
                    inc=[(smmm, 1)] if mc == 7 else [])
            mmw = smmm.tick()
            ctx.emit("s", lambda s, i=i, og=og, par=par: s.copy(
                vtm[:, i, og * 512:(og + 1) * 512], psq[:, par, :]),
                wait=[(smmm, mmw)], inc=[(smev, 1)])
            smev.tick()

    ctx.barrier()
    ctx.free_to(m1)

    m2 = ctx.mark()
    Efs = ctx.sb("Efs", [128, NH, 512], BF16)
    Eds = ctx.sb("Eds", [128, NH, 128], BF16)
    pT = ctx.sb("pT", [128, 4, 512], BF16)
    psS = ctx.ps("psS", [128, 4, 512], F32)
    psO = ctx.ps("psO", [128, 2, 512], F32)
    ctx.dma("g", Efs[:, :, :], c_ef.ap().rearrange("h p i -> p h i"),
            inc=[(smc, 16)])
    smc.tick(16)
    ctx.dma("g", Eds[:, :, :], c_ed.ap().rearrange("h p i -> p h i"),
            inc=[(smc, 16)])
    dec_done = smc.tick(16)

    smrS = ctx.sem("mms")
    smrE = ctx.sem("evs")
    smrO = ctx.sem("mm2")
    smrG = ctx.sem("ev2")
    inv8 = 1.0 / 8.0
    e0r = smrE.n

    for sg in range(4):
        for h in range(NH):
            c = h // 2
            rbase = (h % 2) * 64
            po = psO[rbase:rbase + 64, h % 2, :]
            nch = 4 * sg + 4
            head_seq = sg * NH + h
            po_guard = [(smrG, smrG.n - 1)] if head_seq >= 2 else []
            pend = []
            for tc in range(nch):
                par = smrE.n % 4
                w2 = [(smc, dec_done)]
                if smrE.n - e0r >= 4:
                    w2.append((smrE, smrE.n - 3))
                if len(pend) == 2:
                    pend.pop(0)()
                ctx.emit("t", lambda t, c=c, rbase=rbase, tc=tc, sg=sg, par=par: t.matmul(
                    psS[:, par, :], kT[rbase:rbase + 64, c, tc * 128:(tc + 1) * 128],
                    qT[rbase:rbase + 64, c, sg * 512:(sg + 1) * 512],
                    start=True, stop=True),
                    wait=w2, inc=[(smrS, 1)])
                sS = smrS.tick()
                mdel = 4 * sg - tc
                if mdel >= 1:
                    ctx.emit("v", lambda v, h=h, par=par, mdel=mdel: v.scalar_tensor_tensor(
                        pT[:, par, :], psS[:, par, :],
                        float(inv8 * (gam[h] ** (128 * (mdel - 1)))),
                        Efs[:, h, :], ALU.mult, ALU.mult),
                        wait=[(smrS, sS)], inc=[(smrE, 1)])
                    smrE.tick()
                else:
                    cl = tc - 4 * sg
                    first = True
                    if cl > 0:
                        ctx.emit("v", lambda v, par=par, cl=cl: v.memset(
                            pT[:, par, 0:cl * 128], 0.0), wait=[(smrS, sS)],
                            mark=True)
                        first = False
                    if cl < 3:
                        wdt = 512 - 128 * (cl + 1)
                        ctx.emit("v", lambda v, h=h, par=par, cl=cl, wdt=wdt: v.scalar_tensor_tensor(
                            pT[:, par, (cl + 1) * 128:512],
                            psS[:, par, (cl + 1) * 128:512],
                            inv8, Efs[:, h, 0:wdt], ALU.mult, ALU.mult),
                            wait=[(smrS, sS)] if first else (),
                            dep=not first, mark=True)
                        first = False
                    ctx.emit("v", lambda v, h=h, par=par, cl=cl: v.scalar_tensor_tensor(
                        pT[:, par, cl * 128:(cl + 1) * 128],
                        psS[:, par, cl * 128:(cl + 1) * 128],
                        inv8, Eds[:, h, :], ALU.mult, ALU.mult),
                        wait=[(smrS, sS)] if first else (),
                        dep=not first, inc=[(smrE, 1)])
                    smrE.tick()
                eE = smrE.n
                pend.append(lambda tc=tc, h=h, par=par, po=po, eE=eE,
                            fp=(tc == 0), last=(tc == nch - 1),
                            pg=(po_guard if tc == 0 else []): ctx.emit(
                    "t", lambda t: t.matmul(
                        po, vtm[:, tc, h * 64:(h + 1) * 64], pT[:, par, :],
                        start=fp, stop=last),
                    wait=[(smrE, eE)] + pg,
                    inc=[(smrO, 1)] if last else []))
            for f in pend:
                f()
            oO = smrO.tick()
            ctx.emit("v", lambda v, c=c, rbase=rbase, sg=sg, po=po: v.tensor_copy(
                outT[rbase:rbase + 64, c, sg * 512:(sg + 1) * 512], po),
                wait=[(smrO, oO)], inc=[(smrG, 1)])
            smrG.tick()

    ctx.barrier()
    ctx.free_to(m2)

    m3 = ctx.mark()
    wT2 = ctx.sb("wT2", [128, 8, D], BF16)
    gsl = ctx.sb("gsl", [128, 2, 512], BF16)
    psq2 = ctx.ps("psq2", [128, 2, 512], F32)
    wlg = emit_wT_load(ctx, wbf["wg"], wT2, "wg", wait=[(smw, wcast["wg"])])
    wlgd = wlg.n
    smgm = ctx.sem("mm2")
    smga = ctx.sem("acts")
    smge = ctx.sem("ev2")
    for oc in range(8):
        for tg in range(4):
            it = oc * 4 + tg
            par = it % 2
            w2 = [(wlg, wlgd)]
            if it >= 2:
                w2.append((smga, smga.n - 1))
            for mc in range(8):
                ctx.emit("t", lambda t, oc=oc, tg=tg, mc=mc, par=par: t.matmul(
                    psq2[:, par, :], wT2[:, mc, oc * 128:(oc + 1) * 128],
                    hTs[:, mc, tg * 512:(tg + 1) * 512],
                    start=(mc == 0), stop=(mc == 7)),
                    wait=w2 if mc == 0 else (),
                    inc=[(smgm, 1)] if mc == 7 else [])
            mmw = smgm.tick()
            wga = [(smgm, mmw)]
            if it >= 2:
                wga.append((smge, smge.n - 1))
            ctx.emit("s", lambda s, par=par: s.activation(
                gsl[:, par, :], psq2[:, par, :], AF.Silu),
                wait=wga, inc=[(smga, 1)])
            act_done = smga.tick()
            ctx.emit("v", lambda v, oc=oc, tg=tg, par=par: v.tensor_tensor(
                outT[:, oc, tg * 512:(tg + 1) * 512],
                outT[:, oc, tg * 512:(tg + 1) * 512],
                gsl[:, par, :], ALU.mult),
                wait=[(smga, act_done)], inc=[(smge, 1)])
            smge.tick()
    ctx.barrier()
    ctx.free_to(m3)

    m4 = ctx.mark()
    woT = ctx.sb("woT", [128, 8, D], BF16)
    xin = ctx.sb("xin", [128, 2, 1024], F32)
    psX = ctx.ps("psX", [128, 2, 512], F32)
    wlo = emit_wT_load(ctx, wbf["wo"], woT, "wo", wait=[(smw, wcast["wo"])])
    wlod = wlo.n
    smXm = ctx.sem("mm2")
    smXe = ctx.sem("ev2")
    smxl2 = [ctx.sem(f"lx{j}") for j in range(2)]
    smxs2 = [ctx.sem(f"sx{j}") for j in range(2)]
    for gt in range(NTILE):
        slot = gt % 2
        smxl = smxl2[slot]
        smxs = smxs2[slot]
        wx = [(smxs, smxs.n)] if gt >= 2 else []
        ctx.dma("y", xin[:, slot, :], xv[:, gt, :], wait=wx, inc=[(smxl, 16)])
        xl = smxl.tick(16)
        for oh in range(2):
            it2 = gt * 2 + oh
            par = it2 % 2
            w2 = [(wlo, wlod)]
            if it2 >= 2:
                w2.append((smXe, smXe.n - 1))
            for c2 in range(8):
                ctx.emit("t", lambda t, gt=gt, oh=oh, c2=c2, par=par: t.matmul(
                    psX[:, par, :], outT[:, c2, gt * 128:(gt + 1) * 128],
                    woT[:, c2, oh * 512:(oh + 1) * 512],
                    start=(c2 == 0), stop=(c2 == 7)),
                    wait=w2 if c2 == 0 else (),
                    inc=[(smXm, 1)] if c2 == 7 else [])
            mmw = smXm.tick()
            ctx.emit("v", lambda v, slot=slot, oh=oh, par=par: v.tensor_tensor(
                xin[:, slot, oh * 512:(oh + 1) * 512],
                xin[:, slot, oh * 512:(oh + 1) * 512],
                psX[:, par, :], ALU.add),
                wait=[(smXm, mmw), (smxl, xl)], inc=[(smXe, 1)])
            smXe.tick()
        ctx.dma("g", xov[:, gt, :], xin[:, slot, :],
                wait=[(smXe, smXe.n)], inc=[(smxs, 16)])
        smxs.tick(16)

    ctx.emit("y", None, wait=[(smxs2[0], smxs2[0].n), (smxs2[1], smxs2[1].n)])
    ctx.barrier(all_sems=True)
    ctx.free_to(_m_all)


def emit_l0_ffn(ctx, ext):
    nc = ctx.nc
    ctx._eps_t = None
    _m_all = ctx.mark()
    x = ext["x"]
    Win = ext["Win"]
    bin_ = ext["bin_"]
    W1 = ext["W1"]
    b1 = ext["b1"]
    W2 = ext["W2"]
    b2 = ext["b2"]
    Wout = ext["Wout"]
    bout = ext["bout"]
    xo = ext["xo"]

    c_ident = ctx.il0(np_ident(), "identsrc")

    xv = x.rearrange("(n p) d -> p n d", p=128)
    xov = xo.rearrange("(n p) d -> p n d", p=128)

    xs = ctx.sb("xs", [128, NTILE, 1024], F32)
    hTs = ctx.sb("hTs", [128, 8, S], BF16)
    identbf = ctx.sb("identbf", [128, 128], BF16)
    WinS = ctx.sb("WinS", [128, 8, 64], BF16)
    WoutS = ctx.sb("WoutS", [64, D], BF16)
    binbf = ctx.sb("binbf", [1, 2, 64], BF16)
    boutbf = ctx.sb("boutbf", [1, 2, D], BF16)
    onesbf = ctx.sb("onesbf", [1, 128], BF16)
    onesf = ctx.sb("onesf", [1, 128], F32)
    W1b = ctx.sb("W1b", [128, 1024], F32)
    b1b = ctx.sb("b1b", [128, 1024], F32)
    W2b = ctx.sb("W2b", [128, 1024], F32)
    b2b = ctx.sb("b2b", [128, 64], F32)
    m_pre = ctx.mark()
    psb = ctx.ps("psb", [128, 1024], F32)

    smc = ctx.sem("consts")
    smst = ctx.sem("t2v")
    smb = ctx.sem("bch")
    smbr = ctx.sem("bc2")

    ctx.dma("g", identbf[:, :], c_ident.ap()[:, :], inc=[(smc, 16)])
    smc.tick(16)
    ctx.dma("g", WinS[:, :, :], Win.rearrange("(c p) m -> p c m", p=128),
            inc=[(smc, 16)])
    smc.tick(16)
    ctx.dma("g", WoutS[:, :], Wout[:, :], inc=[(smc, 16)])
    consts_done = smc.tick(16)

    ctx.emit("v", lambda v: v.memset(onesbf[:, :], 1.0))
    ctx.emit("v", lambda v: v.memset(onesf[:, :], 1.0), inc=[(smst, 1)])
    ones_done = smst.tick()

    tmpb = ctx.sb("tmpb", [1, D], F32)

    def hi_lo(row_ap, dst_bf, width):
        ctx.dma("y", tmpb[:, 0:width], row_ap, wait=[(smb, smb.n)],
                inc=[(smb, 16)])
        w0 = smb.tick(16)
        ctx.emit("v", lambda v: v.tensor_copy(dst_bf[:, 0, 0:width],
                                              tmpb[:, 0:width]),
                 wait=[(smb, w0)], mark=True)
        ctx.emit("v", lambda v: v.tensor_tensor(
            tmpb[:, 0:width], tmpb[:, 0:width], dst_bf[:, 0, 0:width],
            ALU.subtract), dep=True, mark=True)
        ctx.emit("v", lambda v: v.tensor_copy(dst_bf[:, 1, 0:width],
                                              tmpb[:, 0:width]),
                 dep=True, inc=[(smb, 1)])
        smb.tick()

    hi_lo(bin_[None, :], binbf, 64)
    hi_lo(bout[None, :], boutbf, D)
    bias_done = smb.n

    rbf = ctx.sb("rbf", [1, 1024], F32)

    def bcast(src_flat, dst, width):
        ctx.dma("y", rbf[:, 0:width], src_flat, wait=[(smbr, smbr.n)],
                inc=[(smbr, 16)])
        w0 = smbr.tick(16)
        for h0 in range(0, width, 512):
            h1 = min(h0 + 512, width)
            ctx.emit("t", lambda t, h0=h0, h1=h1: t.matmul(
                psb[:, h0:h1], onesf[:, :], rbf[:, h0:h1], start=True, stop=True),
                wait=[(smbr, w0), (smst, ones_done)] if h0 == 0 else (),
                inc=[(smbr, 1)] if h1 == width else [])
        w1 = smbr.tick()
        ctx.emit("v", lambda v, width=width: v.tensor_copy(
            dst[:, 0:width], psb[:, 0:width]),
            wait=[(smbr, w1)], inc=[(smbr, 1)])
        smbr.tick()

    bcast(W1.rearrange("m i -> (m i)")[None, :], W1b, 1024)
    bcast(b1.rearrange("m i -> (m i)")[None, :], b1b, 1024)
    bcast(W2.rearrange("m i -> (m i)")[None, :], W2b, 1024)
    bcast(b2[None, :], b2b, 64)
    bc_done = smbr.n

    smx = ctx.sem("lx0")
    ctx.dma("y", xs[:, :, :], xv, inc=[(smx, 16)])
    xs_ok = smx.tick(16)

    trsem = emit_ln_transpose_stream(
        ctx, lambda i: (xs[:, i, :], [(smx, xs_ok)]), hTs, "ln", identbf,
        [(smc, consts_done)], keep_h=None)
    hT_done = trsem.n
    ctx.barrier()
    ctx.free_to(m_pre)

    pss = ctx.ps("pss", [128, 2, 512], F32)
    psx = ctx.ps("psx", [128, 2, 512], F32)
    pstr = ctx.ps("pstr", [128, 2, 1024], BF16)
    sV = ctx.sb("sV", [128, 2, 64], F32)
    t1 = ctx.sb("t1", [128, 1024], F32)
    t2 = ctx.sb("t2", [128, 1024], F32)
    gVb = ctx.sb("gVb", [128, 2, 64], BF16)
    gTt = ctx.sb("gTt", [64, 2, 128], BF16)

    smm = ctx.sem("mms")
    sme = ctx.sem("evs")
    smg = ctx.sem("ev2")
    smt = ctx.sem("t2s")
    smtv = ctx.sem("t2v")
    smo = ctx.sem("mm2")
    smxe = ctx.sem("ev3")
    smxs = ctx.sem("xst")

    for i in range(NTILE):
        par = i % 2
        w2 = [(smc, consts_done), (smb, bias_done), (trsem, hT_done)]
        if i >= 2:
            w2.append((smg, smg.n - 1))
        for mc in range(8):
            ctx.emit("t", lambda t, i=i, mc=mc, par=par: t.matmul(
                pss[:, par, 0:64], hTs[:, mc, i * 128:(i + 1) * 128],
                WinS[:, mc, :], start=(mc == 0), stop=False),
                wait=w2 if mc == 0 else ())
        ctx.emit("t", lambda t, par=par: t.matmul(
            pss[:, par, 0:64], onesbf[:, :], binbf[:, 0, :],
            start=False, stop=False))
        ctx.emit("t", lambda t, par=par: t.matmul(
            pss[:, par, 0:64], onesbf[:, :], binbf[:, 1, :],
            start=False, stop=True), inc=[(smm, 1)])
        mmw = smm.tick()
        ctx.emit("v", lambda v, par=par: v.tensor_copy(
            sV[:, par, :], pss[:, par, 0:64]), wait=[(smm, mmw)],
            inc=[(sme, 1)])
        se = sme.tick()
        ctx.emit("v", lambda v, par=par: v.tensor_tensor(
            t1.rearrange("p (m i) -> p m i", i=16),
            sV[:, par, :].unsqueeze(2).broadcast_to([128, 64, 16]),
            W1b.rearrange("p (m i) -> p m i", i=16), ALU.mult),
            wait=[(sme, se), (smbr, bc_done)], mark=True)
        ctx.emit("v", lambda v: v.tensor_tensor(t2[:, :], t1[:, :], b1b[:, :],
                                                ALU.add), dep=True,
                 inc=[(smtv, 1)])
        tw = smtv.tick()
        ctx.emit("s", lambda s: s.activation(t2[:, :], t2[:, :], AF.Relu),
                 wait=[(smtv, tw)], inc=[(smt, 1)])
        st = smt.tick()
        ctx.emit("v", lambda v: v.tensor_tensor(t1[:, :], t2[:, :], W2b[:, :],
                                                ALU.mult), wait=[(smt, st)],
                 mark=True)
        ctx.emit("v", lambda v, par=par: v.reduce_sum(
            sV[:, par, :], t1.rearrange("p (m i) -> p m i", i=16), axis=AX.X),
            dep=True, mark=True)
        ctx.emit("v", lambda v, par=par: v.tensor_tensor(
            sV[:, par, :], sV[:, par, :], b2b[:, :], ALU.add),
            dep=True, mark=True)
        ctx.emit("v", lambda v, par=par: v.tensor_copy(
            gVb[:, par, :], sV[:, par, :]), dep=True,
            wait=[(smm, smm.n - 3)] if i >= 2 else (), inc=[(smg, 1)])
        sg_ = smg.tick()
        ctx.emit("t", lambda t, par=par: t.transpose(
            pstr[0:64, par, 0:128], gVb[:, par, :], identbf[:, :]),
            wait=[(smg, sg_)] + ([(smt, smt.n - 2)] if i >= 2 else []),
            inc=[(smm, 1)])
        mm2 = smm.tick()
        ctx.emit("s", lambda s, par=par: s.copy(gTt[:, par, :],
                                                pstr[0:64, par, 0:128]),
                 wait=[(smm, mm2)], inc=[(smt, 1)])
        gt_done = smt.tick()
        for oh in range(2):
            pxp = (2 * i + oh) % 2
            wpx = [(smt, gt_done)] if oh == 0 else []
            if 2 * i + oh - 1 >= 1:
                wpx.append((smxe, smxe.n - 1))
            ctx.emit("t", lambda t, pxp=pxp, par=par, oh=oh: t.matmul(
                psx[:, pxp, :], gTt[:, par, :],
                WoutS[:, oh * 512:(oh + 1) * 512], start=True, stop=False),
                wait=wpx)
            ctx.emit("t", lambda t, pxp=pxp, oh=oh: t.matmul(
                psx[:, pxp, :], onesbf[:, :], boutbf[:, 0, oh * 512:(oh + 1) * 512],
                start=False, stop=False))
            ctx.emit("t", lambda t, pxp=pxp, oh=oh: t.matmul(
                psx[:, pxp, :], onesbf[:, :], boutbf[:, 1, oh * 512:(oh + 1) * 512],
                start=False, stop=True), inc=[(smo, 1)])
            so = smo.tick()
            ctx.emit("v", lambda v, i=i, oh=oh, pxp=pxp: v.tensor_tensor(
                xs[:, i, oh * 512:(oh + 1) * 512],
                xs[:, i, oh * 512:(oh + 1) * 512],
                psx[:, pxp, :], ALU.add), wait=[(smo, so)],
                inc=[(smxe, 1)])
            smxe.tick()
        xe = smxe.n
        ctx.dma("g", xov[:, i, :], xs[:, i, :], wait=[(smxe, xe)],
                inc=[(smxs, 16)])
        smxs.tick(16)

    ctx.emit("y", None, wait=[(smxs, smxs.n)])
    ctx.barrier(all_sems=True)
    ctx.free_to(_m_all)




def np_causal_masks():
    n_ = np.arange(128)[:, None]
    m_ = np.arange(512)[None, :]
    M1 = np.stack([np.where(m_ > n_ + 128 * j, -1e9, 0.0) for j in range(4)], 0)
    N2 = np.stack([np.where(m_ < n_ + 128 * j * 0 + 0, 0.0, 0.0) for j in range(1)], 0)
    M2 = np.stack([np.where(m_ - 128 * j < n_, -1e9, 0.0).T * 0.0 for j in range(1)], 0)
    return M1.astype(np.float32)


def emit_l1_attn(ctx, ext, variant):
    nc = ctx.nc
    ctx._eps_t = None
    _m_all = ctx.mark()
    x = ext["x"]
    if variant == "b":
        W_attn = ext["W_attn"]
        b_attn = ext["b_attn"]
        W_proj = ext["W_proj"]
        b_proj = ext["b_proj"]
        wbf = nc.dram_tensor(ctx.prefix + "wbf", [D, 3 * D], BF16, kind="Internal").ap()
        wpbf = nc.dram_tensor(ctx.prefix + "wpbf", [D, D], BF16, kind="Internal").ap()
    else:
        Wq = ext["Wq"]
        Wk = ext["Wk"]
        Wv = ext["Wv"]
        Wproj = ext["Wproj"]
        bproj = ext["bproj"]
        Wpre = ext["Wpre"]
        bpre = ext["bpre"]
        pi_in = ext["pi"]
        wqbf = nc.dram_tensor(ctx.prefix + "wqbf", [D, D], BF16, kind="Internal").ap()
        wkbf = nc.dram_tensor(ctx.prefix + "wkbf", [D, D], BF16, kind="Internal").ap()
        wvbf = nc.dram_tensor(ctx.prefix + "wvbf", [D, D], BF16, kind="Internal").ap()
        wpbf = nc.dram_tensor(ctx.prefix + "wpbf", [D, D], BF16, kind="Internal").ap()
        wrbf = nc.dram_tensor(ctx.prefix + "wrbf", [D, D], BF16, kind="Internal").ap()
        L_d = nc.dram_tensor(ctx.prefix + "L_d", [S, S], BF16, kind="Internal").ap()
    xo = ext["xo"]
    nlrow_d = nc.dram_tensor(ctx.prefix + "nlrow_d", [2, 2, 128, 128], BF16, kind="Internal").ap()

    c_ident = ctx.il0(np_ident(), "identsrc")
    causal = variant == "b"
    if causal:
        n_ = np.arange(128)[:, None]
        m_ = np.arange(512)[None, :]
        M1 = np.stack([np.where(m_ > n_ + 128 * j, 0.0, 1.0)
                       for j in range(4)], 0).astype(np.float32)
        ml = np.arange(128)[:, None]
        nl = np.arange(512)[None, :]
        M2 = np.stack([np.where(nl < 128 * j + ml, 0.0, 1.0)
                       for j in range(4)], 0).astype(np.float32)
        c_m1 = ctx.il0(M1, "M1")
        c_m2 = ctx.il0(M2, "M2")

    xv = x.rearrange("(n p) d -> p n d", p=128)
    xov = xo.rearrange("(n p) d -> p n d", p=128)

    qT = ctx.sb("qT", [128, 8, S], BF16)
    kT = ctx.sb("kT", [128, 8, S], BF16)
    vtm = ctx.sb("vtm", [128, NTILE, 1024], BF16)
    TT = ctx.sb("TT", [128, 8, S], BF16)
    outT = TT
    preT = TT
    identbf = ctx.sb("identbf", [128, 128], BF16)
    mrs = ctx.sb("mrs", [128, NTILE, 2], F32)
    o1b_ = ctx.sb("ones1bf", [1, 128], BF16)
    ctx._ones1bf = o1b_
    ctx.emit("v", lambda v, o1b_=o1b_: v.memset(o1b_[:, :], 1.0), mark=True)

    smc = ctx.sem("consts")
    ctx.dma("g", identbf[:, :], c_ident.ap()[:, :], inc=[(smc, 16)])
    ident_ok = smc.tick(16)
    smw = ctx.sem("gc_")
    if variant == "b":
        ctx.dma("g", wbf[:, :], W_attn[:, :], inc=[(smw, 16)])
        smw.tick(16)
        ctx.dma("g", wpbf[:, :], W_proj[:, :], inc=[(smw, 16)])
        smw.tick(16)
    else:
        for dst, srcw in ((wqbf, Wq), (wkbf, Wk), (wvbf, Wv), (wpbf, Wproj),
                          (wrbf, Wpre)):
            ctx.dma("g", dst[:, :], srcw[:, :], inc=[(smw, 16)])
            smw.tick(16)
    wcast_all = smw.n

    m_ln = ctx.mark()
    hTs = ctx.sb("hTs", [128, 8, S], BF16)
    m0 = ctx.mark()
    xring = ctx.sb("xring", [128, 3, 1024], F32)
    smx3 = [ctx.sem(f"lx{j}") for j in range(3)]

    def xsrc(i):
        slot = i % 3
        smx = smx3[slot]
        w = []
        if i >= 3:
            w.append((ctx.sems["lnln"], ctx.sems["lnln"].n - 10))
        ctx.dma("y", xring[:, slot, :], xv[:, i, :], wait=w, inc=[(smx, 16)])
        smx.tick(16)
        return xring[:, slot, :], [(smx, smx.n)]

    trsem = emit_ln_transpose_stream(ctx, xsrc, hTs, "ln", identbf,
                                     [(smc, ident_ok)], keep_h=None, keep_mrs=mrs)
    hT_done = trsem.n
    ctx.barrier()
    ctx.free_to(m0)

    m1 = ctx.mark()
    wcol = ctx.sb("wcol", [128, 2, 8, 128], BF16)
    bias_c = ctx.sb("bias_c", [128, 32], F32)
    psq = ctx.ps("psq", [128, 2, 512], F32)
    smmm = ctx.sem("mms")
    smev = ctx.sem("evs")
    smbc = ctx.sem("bch")

    piB = None
    btmp = ctx.sb("btmp", [24, 128], F32)
    identf = ctx.sb("identf", [128, 128], F32)
    ctx.dma("g", identf[:, :], c_ident.ap()[:, :], inc=[(smc, 16)])
    smc.tick(16)
    idf_ok = smc.n
    psbt = ctx.ps("psbt", [128, 512], F32)
    nbc = 24 if variant == "b" else 8
    bap = b_attn if variant == "b" else bpre
    ctx.dma("y", btmp[0:nbc, :],
            bap.rearrange("(c p) -> c p", p=128)[0:nbc, :], inc=[(smbc, 16)])
    smbc.tick(16)
    ctx.emit("t", lambda t, nbc=nbc: t.transpose(
        psbt[:, 0:nbc], btmp[0:nbc, :], identf[0:nbc, 0:nbc]),
        wait=[(smbc, smbc.n), (smc, idf_ok)], inc=[(smbc, 1)])
    smbc.tick()
    ctx.emit("v", lambda v, nbc=nbc: v.tensor_copy(
        bias_c[:, 0:nbc], psbt[:, 0:nbc]),
        wait=[(smbc, smbc.n)], inc=[(smbc, 1)])
    smbc.tick()
    if variant == "a":
        piB = ctx.sb("piB", [128, 1], F32)
        pi1 = ctx.sb("pi1", [1, 1], F32)
        onesf1 = ctx.sb("onesf1", [1, 128], F32)
        smpi = ctx.sem("lx2")
        ctx.emit("v", lambda v: v.memset(onesf1[:, :], 1.0), mark=True)
        vv = ctx._last_inc["v"]
        ctx.dma("y", pi1[:, :], pi_in[None, :], inc=[(smpi, 16)])
        smpi.tick(16)
        ctx.emit("t", lambda t: t.matmul(psq[:, 0, 0:1], onesf1[:, :], pi1[:, :],
                                         start=True, stop=True),
                 wait=[(smpi, smpi.n), vv], inc=[(smpi, 1)])
        smpi.tick()
        ctx.emit("v", lambda v: v.tensor_scalar(piB[:, :], psq[:, 0, 0:1],
                                                1.0 / 32.0, None, ALU.mult),
                 wait=[(smpi, smpi.n)], inc=[(smbc, 1)])
        smbc.tick()
    bias_ready = smbc.n
    ctx.barrier()

    def fm_proj2(w_dram_bf, col_off, outbuf, mode, scale=1.0, bcol0=0):
        sm_l2 = [ctx.sem(), ctx.sem()]
        ev_entry = smev.n
        for oc in range(8):
            par = oc % 2
            sm_l = sm_l2[par]
            w0 = [(smw, wcast_all)]
            if oc >= 2:
                w0.append((smev, smev.n - 1))
            else:
                w0.append((smev, ev_entry))
            for mc in range(8):
                ctx.dma("y", wcol[:, par, mc, :],
                        w_dram_bf[mc * 128:(mc + 1) * 128,
                                  col_off + oc * 128: col_off + (oc + 1) * 128],
                        wait=w0 if mc == 0 else (), inc=[(sm_l, 16)])
                sm_l.tick(16)
            for tg in range(4):
                it = oc * 4 + tg
                pp = it % 2
                w2 = [(sm_l, sm_l.n)]
                if it == 0:
                    w2.append((trsem, hT_done))
                if it >= 2:
                    w2.append((smev, smev.n - 1))
                for mc in range(8):
                    ctx.emit("t", lambda t, oc=oc, tg=tg, mc=mc, par=par, pp=pp: t.matmul(
                        psq[:, pp, :], wcol[:, par, mc, :],
                        hTs[:, mc, tg * 512:(tg + 1) * 512],
                        start=(mc == 0), stop=(mc == 7)),
                        wait=w2 if mc == 0 else (),
                        inc=[(smmm, 1)] if mc == 7 else [])
                mmw = smmm.tick()
                if mode == "silu":
                    ctx.emit("s", lambda s, oc=oc, tg=tg, pp=pp, scale=scale, bcol0=bcol0: s.activation(
                        outbuf[:, oc, tg * 512:(tg + 1) * 512], psq[:, pp, :],
                        AF.Silu, bias=bias_c[:, bcol0 + oc:bcol0 + oc + 1],
                        scale=scale),
                        wait=[(smmm, mmw), (smbc, bias_ready)], inc=[(smev, 1)])
                elif mode == "bias":
                    ctx.emit("v", lambda v, oc=oc, tg=tg, pp=pp, scale=scale, bcol0=bcol0: v.tensor_scalar(
                        outbuf[:, oc, tg * 512:(tg + 1) * 512], psq[:, pp, :],
                        bias_c[:, bcol0 + oc:bcol0 + oc + 1], scale,
                        ALU.add, ALU.mult),
                        wait=[(smmm, mmw), (smbc, bias_ready)], inc=[(smev, 1)])
                else:
                    ctx.emit("s", lambda s, oc=oc, tg=tg, pp=pp, scale=scale: s.activation(
                        outbuf[:, oc, tg * 512:(tg + 1) * 512], psq[:, pp, :],
                        AF.Copy, scale=scale),
                        wait=[(smmm, mmw)], inc=[(smev, 1)])
                smev.tick()

    def tm_proj2(w_dram_bf, col_off, brow_bf, outbuf):
        wv_sb = ctx.sb("wv_sb", [128, 8, 1024], BF16)
        sm_l = ctx.sem()
        for mc in range(8):
            ctx.dma("y", wv_sb[:, mc, :],
                    w_dram_bf[mc * 128:(mc + 1) * 128, col_off:col_off + 1024],
                    wait=[(smw, wcast_all), (smev, smev.n)] if mc == 0 else (),
                    inc=[(sm_l, 16)])
            sm_l.tick(16)
        for i in range(NTILE):
            for og in range(2):
                it = i * 2 + og
                pp = it % 2
                w2 = [(sm_l, sm_l.n)]
                if it >= 2:
                    w2.append((smev, smev.n - 1))
                use_b = brow_bf is not None
                for mc in range(8):
                    ctx.emit("t", lambda t, i=i, og=og, mc=mc, pp=pp, use_b=use_b: t.matmul(
                        psq[:, pp, :], hTs[:, mc, i * 128:(i + 1) * 128],
                        wv_sb[:, mc, og * 512:(og + 1) * 512],
                        start=(mc == 0), stop=(mc == 7 and not use_b)),
                        wait=w2 if mc == 0 else (),
                        inc=[(smmm, 1)] if (mc == 7 and not use_b) else [])
                if use_b:
                    ctx.emit("t", lambda t, og=og, pp=pp: t.matmul(
                        psq[:, pp, :], o1b_[:, :],
                        brow_bf[:, 0, og * 512:(og + 1) * 512],
                        start=False, stop=False))
                    ctx.emit("t", lambda t, og=og, pp=pp: t.matmul(
                        psq[:, pp, :], o1b_[:, :],
                        brow_bf[:, 1, og * 512:(og + 1) * 512],
                        start=False, stop=True), inc=[(smmm, 1)])
                mmw = smmm.tick()
                ctx.emit("s", lambda s, i=i, og=og, pp=pp: s.copy(
                    outbuf[:, i, og * 512:(og + 1) * 512], psq[:, pp, :]),
                    wait=[(smmm, mmw)], inc=[(smev, 1)])
                smev.tick()

    smbr = ctx.sem("lx2")
    if variant == "b":
        vbrow = ctx.sb("vbrow", [1, 2, 1024], BF16)
        vbtmp = ctx.sb("vbtmp", [1, 1024], F32)
        ctx.dma("y", vbtmp[:, :], b_attn[None, 2048:3072], inc=[(smbr, 16)])
        smbr.tick(16)
        ctx.emit("v", lambda v: v.tensor_copy(vbrow[:, 0, :], vbtmp[:, :]),
                 wait=[(smbr, smbr.n)], mark=True)
        ctx.emit("v", lambda v: v.tensor_tensor(vbtmp[:, :], vbtmp[:, :],
                                                vbrow[:, 0, :], ALU.subtract),
                 dep=True, mark=True)
        ctx.emit("v", lambda v: v.tensor_copy(vbrow[:, 1, :], vbtmp[:, :]),
                 dep=True, inc=[(smbr, 1)])
        smbr.tick()
        fm_proj2(wbf, 0, qT, "bias", scale=0.125, bcol0=0)
        fm_proj2(wbf, 1024, kT, "bias", scale=1.0, bcol0=8)
        tm_proj2(wbf, 2048, vbrow, vtm)
    else:
        fm_proj2(wqbf, 0, qT, "plain", scale=0.125)
        fm_proj2(wkbf, 0, kT, "plain", scale=1.0)
        fm_proj2(wrbf, 0, preT, "silu", scale=1.0)
        tm_proj2(wvbf, 0, None, vtm)

    ctx.barrier()
    ctx.free_to(m1)
    if variant == "b":
        ctx.free_to(m_ln)
    else:
        ctx.free_to(m_ln)

    m2 = ctx.mark()
    stat = ctx.sb("stat", [128, NH, 8], F32)
    stat2 = ctx.sb("stat2", [128, NH, 8], F32)
    nlC = ctx.sb("nlC", [128, 2, 256], BF16)
    exp_scr = ctx.sb("exp_scr", [128, 512], BF16)
    exp_scrs = ctx.sb("exp_scrs", [128, 4, 512], BF16)
    exp_scr2 = ctx.sb("exp_scr2", [128, 512], BF16)
    if causal:
        M1s = ctx.sb("M1s", [128, 4, 512], BF16)
        ctx.dma("g", M1s[:, :, :], c_m1.ap().rearrange("j p m -> p j m"),
                inc=[(smc, 16)])
        smc.tick(16)
    if variant == "a":
        L1buf = ctx.sb("L1buf", [128, 2, 512], BF16)
        psL = ctx.ps("psL", [128, 2, 512], F32)
    psS = ctx.ps("psS", [128, 4, 512], F32)
    consts_p1 = smc.n

    smS = ctx.sem("mms")
    smE = ctx.sem("ev2")
    smA = ctx.sem("acts")
    smL = ctx.sem("evs")
    smLds = [ctx.sem("sd0"), ctx.sem("sd1")]
    smFz = ctx.sem("lu0")
    e0_p1 = smE.n

    for nb in range(NTILE):
        ngroups = (nb // 4) + 1 if causal else 4
        for mg in range(ngroups):
            if variant == "a":
                lit = smL.n % 2
                smLd = smLds[lit]
                w2 = [(smc, consts_p1), (smbc, bias_ready)]
                if smL.n >= 2:
                    w2.append((smLd, smLd.n))
                    w2.append((smL, smL.n - 1))
                if smE.n - e0_p1 >= 16:
                    w2.append((smE, smE.n - 16))
                for mc in range(8):
                    ctx.emit("t", lambda t, nb=nb, mg=mg, mc=mc, lit=lit: t.matmul(
                        psL[:, lit, :], preT[:, mc, nb * 128:(nb + 1) * 128],
                        preT[:, mc, mg * 512:(mg + 1) * 512],
                        start=(mc == 0), stop=(mc == 7)),
                        wait=w2 if mc == 0 else (),
                        inc=[(smS, 1)] if mc == 7 else [])
                sw = smS.tick()
                ctx.emit("s", lambda s, mg=mg, lit=lit: s.activation(
                    L1buf[:, lit, :], psL[:, lit, :], AF.Exp,
                    scale=piB[:, 0:1]),
                    wait=[(smS, sw)], inc=[(smL, 1)])
                smL.tick()
                ctx.dma("g", L_d[nb * 128:(nb + 1) * 128,
                                 mg * 512:(mg + 1) * 512],
                        L1buf[:, lit, :], wait=[(smL, smL.n)], inc=[(smLd, 16)])
                smLd.tick(16)
                _ = smLd
            for h in range(NH):
                c = h // 2
                rb = (h % 2) * 64
                lit = (smL.n - 1) % 2 if variant == "a" else 0
                it = smE.n % 4
                diag = causal and mg == ngroups - 1
                w2 = [(smc, consts_p1)]
                if variant == "a":
                    w2.append((smL, smL.n))
                if smE.n >= 4:
                    w2.append((smE, smE.n - 3))
                ctx.emit("t", lambda t, c=c, rb=rb, nb=nb, mg=mg, it=it: t.matmul(
                    psS[:, it, :], qT[rb:rb + 64, c, nb * 128:(nb + 1) * 128],
                    kT[rb:rb + 64, c, mg * 512:(mg + 1) * 512],
                    start=True, stop=True),
                    wait=w2, inc=[(smS, 1)])
                sw = smS.tick()
                if variant == "a":
                    mo = L1buf[:, lit, :]
                elif diag:
                    mo = M1s[:, nb % 4, :]
                else:
                    mo = None
                psum = stat[:, h, 4 + mg:5 + mg]
                if mo is None:
                    ctx.emit("s", lambda s, psum=psum, it=it: s.activation(
                        exp_scrs[:, it, :], psS[:, it, :], AF.Exp,
                        accum_out=psum),
                        wait=[(smS, sw), (smFz, smFz.n)], inc=[(smE, 1)])
                    smE.tick()
                else:
                    ctx.emit("s", lambda s, it=it: s.activation(
                        exp_scrs[:, it, :], psS[:, it, :], AF.Exp),
                        wait=[(smS, sw)], inc=[(smA, 1)])
                    smA.tick()
                    ctx.emit("v", lambda v, psum=psum, it=it, mo=mo:
                             v.scalar_tensor_tensor(
                        exp_scr2[:, :], exp_scrs[:, it, :], 1.0, mo,
                        ALU.mult, ALU.mult, accum_out=psum),
                        wait=[(smA, smA.n)], dep=True, inc=[(smE, 1)])
                    smE.tick()
        ngc = ngroups
        ew = [(smE, smE.n)]
        if ngc == 1:
            ctx.emit("v", lambda v: v.tensor_copy(
                stat2[:, :, 5:6], stat[:, :, 4:5]), wait=ew, dep=True,
                mark=True)
        else:
            ctx.emit("v", lambda v, ngc=ngc: v.reduce_sum(
                stat2[:, :, 5:6], stat[:, :, 4:4 + ngc], axis=AX.X),
                wait=ew, dep=True, mark=True)
        vv = ctx._last_inc["v"]
        ctx.emit("s", lambda s: s.activation(
            stat2[:, :, 6:7], stat2[:, :, 5:6], AF.Ln),
            wait=[vv], inc=[(smA, 1)])
        smA.tick()
        nlv0 = nlC[:, 0, :].rearrange("p (h q) -> p h q", q=16)[:, :, nb:nb + 1]
        nlv1 = nlC[:, 1, :].rearrange("p (h q) -> p h q", q=16)[:, :, nb:nb + 1]
        ctx.emit("v", lambda v: v.tensor_scalar(
            stat2[:, :, 7:8], stat2[:, :, 6:7], -1.0, None, ALU.mult),
            wait=[(smA, smA.n)], dep=True, mark=True)
        ctx.emit("v", lambda v, nlv0=nlv0: v.tensor_copy(
            nlv0, stat2[:, :, 7:8]), dep=True, mark=True)
        ctx.emit("v", lambda v, nlv0=nlv0: v.tensor_tensor(
            stat2[:, :, 6:7], stat2[:, :, 7:8], nlv0, ALU.subtract),
            dep=True, mark=True)
        ctx.emit("v", lambda v, nlv1=nlv1: v.tensor_copy(
            nlv1, stat2[:, :, 6:7]), dep=True, inc=[(smFz, 1)])
        smFz.tick()

    ctx.barrier()
    pstn = ctx.ps("pstn", [128, 2, 1024], BF16)
    smtb = ctx.sem("tps")
    smtd = ctx.sem("sd0")
    for hl in range(2):
        for half in range(2):
            k = hl * 2 + half
            ctx.emit("t", lambda t, hl=hl, half=half, k=k: t.transpose(
                pstn[:, k % 2, 0:128], nlC[:, hl, half * 128:(half + 1) * 128],
                identbf[:, :]), wait=[(smtb, smtb.n)], inc=[(smtb, 1)])
            smtb.tick()
            ctx.emit("s", lambda s, k=k: s.copy(
                exp_scr[:, 0:128], pstn[:, k % 2, 0:128]),
                wait=[(smtb, smtb.n), (smtd, smtd.n)], inc=[(smtb, 1)])
            smtb.tick()
            ctx.dma("g", nlrow_d[hl, half, :, :], exp_scr[:, 0:128],
                    wait=[(smtb, smtb.n)], inc=[(smtd, 16)])
            smtd.tick(16)
    nl_bounced = smtd.n
    ctx.barrier()
    ctx.free_to(m2)

    m3 = ctx.mark()
    nlR = ctx.sb("nlR", [2, 2, 16, 512], BF16)
    ones2b = ctx.sb("ones2b", [2, 128], BF16)
    ctx.emit("v", lambda v: v.memset(ones2b[:, :], 1.0), mark=True)
    ctx.emit("t", None, wait=[ctx._last_inc["v"]])
    if causal:
        M2s = ctx.sb("M2s", [128, 4, 512], BF16)
        ctx.dma("g", M2s[:, :, :], c_m2.ap().rearrange("j p m -> p j m"),
                inc=[(smc, 16)])
        smc.tick(16)
    if variant == "a":
        L2col = ctx.sb("L2col", [128, NTILE, 512], BF16)
    pTt = ctx.sb("pTt", [128, 4, 512], BF16)
    pTt2 = ctx.sb("pTt2", [128, 4, 512], BF16)
    psS2 = ctx.ps("psS2", [128, 4, 512], F32)
    psO = ctx.ps("psO", [128, 4, 512], F32)
    consts_p2 = smc.n

    smS2 = ctx.sem("mms")
    smA2 = ctx.sem("acts")
    smA3 = ctx.sem("ev3")
    smO = ctx.sem("mm2")
    smG2 = ctx.sem("ev2")
    smL2 = ctx.sem("lx0")
    smnls = [ctx.sem("lu0"), ctx.sem("lu1")]
    a30_p2 = smA3.n

    for ng in range(4):
        ngs = ng % 2
        smnl = smnls[ngs]
        wnl = [(smtd, nl_bounced)]
        if ng >= 2:
            wnl.append((smS2, smS2.n))
        ctx.dma("y", nlR[:, ngs, :, :].rearrange(
            "hl h (nb p) -> hl h nb p", p=128),
            nlrow_d[:, :, :, :].rearrange(
                "hl half r p -> hl (half r) p").rearrange(
                "hl (h nb) p -> hl h nb p", nb=16)[:, :, 4 * ng:4 * ng + 4, :],
            wait=wnl, inc=[(smnl, 16)])
        smnl.tick(16)
        nl_ok = smnl.n
        if variant == "a":
            for mb2 in range(NTILE):
                wl = []
                if mb2 == 0:
                    wl = [(smA3, smA3.n), (smLds[0], smLds[0].n),
                          (smLds[1], smLds[1].n)]
                ctx.dma("y", L2col[:, mb2, :],
                        L_d[mb2 * 128:(mb2 + 1) * 128,
                            ng * 512:(ng + 1) * 512],
                        wait=wl, inc=[(smL2, 16)])
                smL2.tick(16)
        l2_ok = smL2.n
        for h in range(NH):
            c = h // 2
            rb = (h % 2) * 64
            head_seq = ng * NH + h
            po = psO[rb:rb + 64, h % 4, :]
            po_guard = [(smG2, smG2.n - 3)] if head_seq >= 4 else []
            mb_hi = 4 * (ng + 1) if causal else NTILE
            first = True
            pend = []
            for mb in range(mb_hi):
                it = smA2.n % 4
                diag = causal and mb // 4 == ng
                w2 = [(smc, consts_p2), (smnl, nl_ok)]
                if smA2.n >= 4:
                    w2.append((smA2, smA2.n - 3))
                if len(pend) == 3:
                    pend.pop(0)()
                ctx.emit("t", lambda t, c=c, rb=rb, ng=ng, mb=mb, it=it: t.matmul(
                    psS2[:, it, :], kT[rb:rb + 64, c, mb * 128:(mb + 1) * 128],
                    qT[rb:rb + 64, c, ng * 512:(ng + 1) * 512],
                    start=True, stop=False),
                    wait=w2)
                ctx.emit("t", lambda t, h=h, ngs=ngs, it=it: t.matmul(
                    psS2[:, it, :], ones2b[:, :],
                    nlR[:, ngs, h, :], start=False, stop=True),
                    inc=[(smS2, 1)])
                sw = smS2.tick()
                if variant == "a":
                    mo2 = L2col[:, mb, :]
                    mow = [(smL2, l2_ok)]
                elif diag:
                    mo2 = M2s[:, mb % 4, :]
                    mow = [(smc, consts_p2)]
                else:
                    mo2 = None
                if mo2 is None:
                    ctx.emit("s", lambda s, it=it: s.activation(
                        pTt[:, it, :], psS2[:, it, :], AF.Exp),
                        wait=[(smS2, sw)], inc=[(smA2, 1)])
                    aw = smA2.tick()
                    pvw = [(smA2, aw)]
                else:
                    if variant == "a":
                        w_p2 = ([(smA3, smA3.n - 3)]
                                if smA3.n - a30_p2 >= 4 else [])
                    else:
                        w_p2 = [(smA3, smA3.n)]
                    ctx.emit("s", lambda s, it=it: s.activation(
                        pTt2[:, it, :], psS2[:, it, :], AF.Exp),
                        wait=[(smS2, sw)] + w_p2, inc=[(smA2, 1)])
                    aw = smA2.tick()
                    ctx.emit("v", lambda v, it=it, mo2=mo2: v.tensor_tensor(
                        pTt[:, it, :], pTt2[:, it, :], mo2, ALU.mult),
                        wait=[(smA2, aw)] + mow, dep=True, inc=[(smA3, 1)])
                    a3w = smA3.tick()
                    pvw = [(smA3, a3w)]
                pend.append(lambda mb=mb, h=h, it=it, po=po, fp=first,
                            last=(mb == mb_hi - 1), pvw=pvw,
                            pg=(po_guard if first else []): ctx.emit(
                    "t", lambda t: t.matmul(
                        po, vtm[:, mb, h * 64:(h + 1) * 64], pTt[:, it, :],
                        start=fp, stop=last),
                    wait=pvw + pg,
                    inc=[(smO, 1)] if last else []))
                first = False
            for f in pend:
                f()
            ow = smO.tick()
            ctx.emit("v", lambda v, c=c, rb=rb, ng=ng, po=po: v.tensor_copy(
                outT[rb:rb + 64, c, ng * 512:(ng + 1) * 512], po),
                wait=[(smO, ow)], inc=[(smG2, 1)])
            smG2.tick()

    ctx.barrier()
    ctx.free_to(m3)

    m4 = ctx.mark()
    wpS = ctx.sb("wpS", [128, 8, 1024], BF16)
    pbrow = ctx.sb("pbrow", [1, 2, 1024], BF16)
    pbtmp = ctx.sb("pbtmp", [1, 1024], F32)
    xin = ctx.sb("xin", [128, 2, 1024], F32)
    psX = ctx.ps("psX", [128, 2, 512], F32)
    if variant == "a":
        aff = ctx.sb("aff", [128, NTILE, 2], F32)
        ctx.emit("v", lambda v: v.tensor_scalar(
            aff[:, :, 0], mrs[:, :, 1], 1.0, None, ALU.add), mark=True)
        ctx.emit("v", lambda v: v.tensor_tensor(
            aff[:, :, 1], mrs[:, :, 0], mrs[:, :, 1], ALU.mult),
            dep=True, mark=True)
        ctx.emit("v", lambda v: v.tensor_scalar(
            aff[:, :, 1], aff[:, :, 1], -1.0, None, ALU.mult),
            dep=True, mark=True)
    smwp = ctx.sem("wl_")
    for mc in range(8):
        ctx.dma("y", wpS[:, mc, :], wpbf[mc * 128:(mc + 1) * 128, :],
                inc=[(smwp, 16)])
        smwp.tick(16)
    bsrc = b_proj if variant == "b" else bproj
    ctx.dma("y", pbtmp[:, :], bsrc[None, :], inc=[(smwp, 16)])
    smwp.tick(16)
    wp_done = smwp.n
    ctx.emit("v", lambda v: v.tensor_copy(pbrow[:, 0, :], pbtmp[:, :]),
             wait=[(smwp, wp_done)], mark=True)
    ctx.emit("v", lambda v: v.tensor_tensor(pbtmp[:, :], pbtmp[:, :],
                                            pbrow[:, 0, :], ALU.subtract),
             dep=True, mark=True)
    ctx.emit("v", lambda v: v.tensor_copy(pbrow[:, 1, :], pbtmp[:, :]),
             dep=True, inc=[(smwp, 1)])
    smwp.tick()
    pb_done = smwp.n
    aff_done = ctx._last_inc["v"]

    smXm = ctx.sem("mm2")
    smXe = ctx.sem("ev2")
    smxl2 = [ctx.sem(f"lx{j}") for j in range(2)]
    smxs2 = [ctx.sem(f"sx{j}") for j in range(2)]
    for gt in range(NTILE):
        slot = gt % 2
        smxl = smxl2[slot]
        smxs = smxs2[slot]
        wx = [(smxs, smxs.n)] if gt >= 2 else []
        ctx.dma("y", xin[:, slot, :], xv[:, gt, :], wait=wx, inc=[(smxl, 16)])
        xl = smxl.tick(16)
        for oh in range(2):
            it2 = gt * 2 + oh
            par = it2 % 2
            w2 = [(smwp, pb_done)]
            if it2 >= 2:
                w2.append((smXe, smXe.n - 1))
            for c2 in range(8):
                ctx.emit("t", lambda t, gt=gt, oh=oh, c2=c2, par=par: t.matmul(
                    psX[:, par, :], outT[:, c2, gt * 128:(gt + 1) * 128],
                    wpS[:, c2, oh * 512:(oh + 1) * 512],
                    start=(c2 == 0), stop=False),
                    wait=w2 if c2 == 0 else ())
            ctx.emit("t", lambda t, oh=oh, par=par: t.matmul(
                psX[:, par, :], o1b_[:, :],
                pbrow[:, 0, oh * 512:(oh + 1) * 512], start=False, stop=False))
            ctx.emit("t", lambda t, oh=oh, par=par: t.matmul(
                psX[:, par, :], o1b_[:, :],
                pbrow[:, 1, oh * 512:(oh + 1) * 512], start=False, stop=True),
                inc=[(smXm, 1)])
            mmw = smXm.tick()
            if variant == "a":
                ctx.emit("v", lambda v, slot=slot, gt=gt, oh=oh: v.tensor_scalar(
                    xin[:, slot, oh * 512:(oh + 1) * 512],
                    xin[:, slot, oh * 512:(oh + 1) * 512],
                    aff[:, gt, 0:1], aff[:, gt, 1:2], ALU.mult, ALU.add),
                    wait=[(smXm, mmw), (smxl, xl), aff_done], dep=True, mark=True)
                ctx.emit("v", lambda v, slot=slot, oh=oh, par=par: v.tensor_tensor(
                    xin[:, slot, oh * 512:(oh + 1) * 512],
                    xin[:, slot, oh * 512:(oh + 1) * 512],
                    psX[:, par, :], ALU.add), dep=True, inc=[(smXe, 1)])
            else:
                ctx.emit("v", lambda v, slot=slot, oh=oh, par=par: v.tensor_tensor(
                    xin[:, slot, oh * 512:(oh + 1) * 512],
                    xin[:, slot, oh * 512:(oh + 1) * 512],
                    psX[:, par, :], ALU.add),
                    wait=[(smXm, mmw), (smxl, xl)], inc=[(smXe, 1)])
            smXe.tick()
        ctx.dma("g", xov[:, gt, :], xin[:, slot, :],
                wait=[(smXe, smXe.n)], inc=[(smxs, 16)])
        smxs.tick(16)

    ctx.emit("y", None, wait=[(smxs2[0], smxs2[0].n), (smxs2[1], smxs2[1].n)])
    ctx.barrier(all_sems=True)
    ctx.free_to(_m_all)




def emit_l0_attn_b(ctx, ext):
    nc = ctx.nc
    ctx._eps_t = None
    _m_all = ctx.mark()
    x = ext["x"]
    W_qkvb = ext["W_qkvb"]
    W_o = ext["W_o"]
    ln_g = ext["ln_g"]
    ln_b = ext["ln_b"]
    xo = ext["xo"]
    wbf = nc.dram_tensor(ctx.prefix + "wbf", [3088, D], BF16, kind="Internal").ap()
    wobf = nc.dram_tensor(ctx.prefix + "wobf", [D, D], BF16, kind="Internal").ap()
    qkvb_d = nc.dram_tensor(ctx.prefix + "qkvb_d", [2, 8, 128, 3088], BF16, kind="Internal").ap()
    c_ident = ctx.il0(np_ident(), "identsrc")

    xv = x.rearrange("(n p) d -> p n d", p=128)
    xov = xo.rearrange("(n p) d -> p n d", p=128)
    NQ = 3088

    identbf = ctx.sb("identbf", [128, 128], BF16)
    mrs = ctx.sb("mrs", [128, NTILE, 2], F32)
    OUT = ctx.sb("OUT", [128, NTILE, 1024], BF16)
    ctx._eps_t = ctx.sb("eps0", [128, 1], F32)
    _e0 = ctx._eps_t
    ctx.emit("v", lambda v, _e0=_e0: v.memset(_e0[:, :], 1e-5), mark=True)
    smc = ctx.sem("consts")
    ctx.dma("g", identbf[:, :], c_ident.ap()[:, :], inc=[(smc, 16)])
    ident_ok = smc.tick(16)
    smw = ctx.sem("gc_")
    ctx.dma("g", wbf[:, :], W_qkvb[:, :], inc=[(smw, 16)])
    smw.tick(16)
    ctx.dma("g", wobf[:, :], W_o[:, :], inc=[(smw, 16)])
    smw.tick(16)
    wcast_all = smw.n

    m_ln = ctx.mark()
    hTs = ctx.sb("hTs", [128, 8, S], BF16)
    m0 = ctx.mark()
    xring = ctx.sb("xring", [128, 3, 1024], F32)
    smx3 = [ctx.sem(f"lx{j}") for j in range(3)]

    def xsrc(i):
        slot = i % 3
        smx = smx3[slot]
        w = []
        if i >= 3:
            w.append((ctx.sems["lnln"], ctx.sems["lnln"].n - 10))
        ctx.dma("y", xring[:, slot, :], xv[:, i, :], wait=w, inc=[(smx, 16)])
        smx.tick(16)
        return xring[:, slot, :], [(smx, smx.n)]

    trsem = emit_ln_transpose_stream(ctx, xsrc, hTs, "ln", identbf,
                                     [(smc, ident_ok)], keep_h=None, keep_mrs=mrs)
    hT_done = trsem.n
    ctx.barrier()
    ctx.free_to(m0)

    m1 = ctx.mark()
    wT = ctx.sb("wT", [128, 8, NQ], BF16)
    qbuf = ctx.sb("qbuf", [128, 2, 3088], BF16)
    psq = ctx.ps("psq", [128, 2, 512], F32)
    smwl = ctx.sem("wl_")
    for mc in range(8):
        ctx.emit("y", lambda y, mc=mc: y.dma_start(
            wT[:, mc, :], wbf[:, mc * 128:(mc + 1) * 128], transpose=True),
            wait=[(smw, wcast_all)] if mc == 0 else (), inc=[(smwl, 16)])
        smwl.tick(16)
    wl_done = smwl.n
    smmm = ctx.sem("mms")
    smev = ctx.sem("evs")
    smqd = [ctx.sem("sd0"), ctx.sem("sd1")]
    GRP = [(0, 512), (512, 512), (1024, 512), (1536, 512), (2048, 512),
           (2560, 512), (3072, 16)]
    for i in range(NTILE):
        slot = i % 2
        smq = smqd[slot]
        for gi, (off, wd) in enumerate(GRP):
            it = smev.n % 2
            w2 = [(smwl, wl_done)]
            if i >= 2 and gi == 0:
                w2.append((smq, smq.n))
            if smev.n >= 2:
                w2.append((smev, smev.n - 1))
            for mc in range(8):
                ctx.emit("t", lambda t, i=i, off=off, wd=wd, mc=mc, it=it: t.matmul(
                    psq[:, it, 0:wd], hTs[:, mc, i * 128:(i + 1) * 128],
                    wT[:, mc, off:off + wd], start=(mc == 0), stop=(mc == 7)),
                    wait=w2 if mc == 0 else (),
                    inc=[(smmm, 1)] if mc == 7 else [])
            mmw = smmm.tick()
            ctx.emit("s", lambda s, slot=slot, off=off, wd=wd, it=it: s.copy(
                qbuf[:, slot, off:off + wd], psq[:, it, 0:wd]),
                wait=[(smmm, mmw)], inc=[(smev, 1)])
            smev.tick()
        ctx.dma("g", qkvb_d[i % 2 if False else (i // 8 if False else 0), 0, 0, 0]
                if False else
                qkvb_d.rearrange("ph s p f -> (ph s) p f")[i, :, :],
                qbuf[:, slot, :], wait=[(smev, smev.n)], inc=[(smq, 16)])
        smq.tick(16)
    qkvb_stored = (smqd[0].n, smqd[1].n)
    ctx.barrier()
    ctx.free_to(m_ln)

    qk_flat = qkvb_d.rearrange("ph s p f -> (ph s) p f")

    kh = ctx.sb("kh", [128, 8, 16, 256], BF16)
    qh = ctx.sb("qh", [128, 16, 256], BF16)
    ytmp = ctx.sb("ytmp", [128, 16, 128], BF16)
    qkv_s = ctx.sb("qkv_s", [128, 8, NQ], BF16)
    scr = ctx.sb("scrD", [128, 8, 256], BF16)
    Abuf = ctx.sb("Abuf", [128, 28, 16], F32)
    Sbuf = ctx.sb("Sbuf", [128, 36, 16], F32)
    kn = ctx.sb("kn", [128, 8, 16], F32)
    qn = ctx.sb("qn", [128, 8, 16], F32)
    kd = ctx.sb("kd", [128, 8, 16], F32)
    fT = ctx.sb("fT", [128, 8, 16], F32)
    bP = ctx.sb("bP", [128, 8, 16], F32)
    dn = ctx.sb("dn", [128, 8, 16], F32)
    U = ctx.sb("U", [128, 8, 16, 64], F32)
    acc = ctx.sb("accB", [128, 8, 64], F32)
    tmp64 = ctx.sb("tmp64", [128, 8, 64], F32)

    smld = [ctx.sem("lq0"), ctx.sem("lq1")]
    smD = ctx.sem("acts")
    APAIR = [(t, r) for t in range(1, 8) for r in range(t)]
    SPAIR = [(s, t) for s in range(8) for t in range(s + 1)]

    for ph in range(2):
        smq = smld[ph % 2]
        wq0 = [(smqd[0], qkvb_stored[0]), (smqd[1], qkvb_stored[1])]
        if ph == 1:
            wq0.append((ctx._last_inc["v"][0], ctx._last_inc["v"][1]))
            wq0.append((smD, smD.n))
        ctx.dma("y", qkv_s[:, :, :],
                qk_flat[:, :, :].rearrange("i p f -> p i f")[:, ph::2, :],
                wait=wq0, inc=[(smq, 16)])
        smq.tick(16)
        qk_ok = smq.n

        for s in range(8):
            koff = 64
            wv_ = [(smq, qk_ok)]
            if s > 0:
                wv_.append(ctx._last_inc["v"])
            ctx.emit("s", lambda s_, s2=s: s_.activation(
                ytmp[:, :, 0:64],
                qkv_s[:, s2, :].rearrange("p (h f) -> p h f", f=193)[:, :, 64:128],
                AF.Relu), wait=wv_, inc=[(smD, 1)])
            smD.tick()
            ctx.emit("s", lambda s_, s2=s: s_.activation(
                ytmp[:, :, 64:128],
                qkv_s[:, s2, :].rearrange("p (h f) -> p h f", f=193)[:, :, 64:128],
                AF.Relu, scale=-1.0), inc=[(smD, 1)])
            smD.tick()
            yready = smD.n
            for rv in (1, 2):
                half0 = (rv - 1) * 128
                ctx.emit("v", lambda v, s2=s, rv=rv, half0=half0: v.tensor_tensor(
                    kh[:, s2, :, half0 + rv:half0 + 128],
                    ytmp[:, :, rv:128], ytmp[:, :, 0:128 - rv], ALU.mult),
                    wait=[(smD, yready)], mark=True)
                ctx.emit("v", lambda v, s2=s, rv=rv, half0=half0: v.tensor_tensor(
                    kh[:, s2, :, half0:half0 + rv],
                    ytmp[:, :, 0:rv], ytmp[:, :, 128 - rv:128], ALU.mult),
                    dep=True, mark=True)
            ctx.emit("v", lambda v, s2=s: v.reduce_sum(
                kn[:, s2, :], kh[:, s2, :, :], axis=AX.X), dep=True, mark=True)

        Ar = Abuf.rearrange("p a (h one) -> p a h one", one=1)
        Sr = Sbuf.rearrange("p a (h one) -> p a h one", one=1)
        for pi_, (t, r) in enumerate(APAIR):
            for hb in range(2):
                h0 = hb * 8
                ctx.emit("v", lambda v, t=t, r=r, h0=h0: v.tensor_tensor(
                    scr[:, :, :], kh[:, t, h0:h0 + 8, :],
                    kh[:, r, h0:h0 + 8, :], ALU.mult),
                    dep=True, mark=True)
                ctx.emit("v", lambda v, pi_=pi_, h0=h0: v.reduce_sum(
                    Ar[:, pi_, h0:h0 + 8, :], scr[:, :, :], axis=AX.X),
                    dep=True, mark=True)
        for s in range(8):
            ctx.emit("s", lambda s_, s2=s: s_.activation(
                ytmp[:, :, 0:64],
                qkv_s[:, s2, :].rearrange("p (h f) -> p h f", f=193)[:, :, 0:64],
                AF.Relu), wait=[(ctx._last_inc["v"][0], ctx._last_inc["v"][1])],
                inc=[(smD, 1)])
            smD.tick()
            ctx.emit("s", lambda s_, s2=s: s_.activation(
                ytmp[:, :, 64:128],
                qkv_s[:, s2, :].rearrange("p (h f) -> p h f", f=193)[:, :, 0:64],
                AF.Relu, scale=-1.0), inc=[(smD, 1)])
            smD.tick()
            yready = smD.n
            for rv in (1, 2):
                half0 = (rv - 1) * 128
                ctx.emit("v", lambda v, rv=rv, half0=half0: v.tensor_tensor(
                    qh[:, :, half0 + rv:half0 + 128],
                    ytmp[:, :, rv:128], ytmp[:, :, 0:128 - rv], ALU.mult),
                    wait=[(smD, yready)], mark=True)
                ctx.emit("v", lambda v, rv=rv, half0=half0: v.tensor_tensor(
                    qh[:, :, half0:half0 + rv],
                    ytmp[:, :, 0:rv], ytmp[:, :, 128 - rv:128], ALU.mult),
                    dep=True, mark=True)
            ctx.emit("v", lambda v, s2=s: v.reduce_sum(
                qn[:, s2, :], qh[:, :, :], axis=AX.X), dep=True, mark=True)
            for pi_, (s_p, t) in enumerate(SPAIR):
                if s_p != s:
                    continue
                for hb in range(2):
                    h0 = hb * 8
                    ctx.emit("v", lambda v, t=t, h0=h0: v.tensor_tensor(
                        scr[:, :, :], qh[:, h0:h0 + 8, :],
                        kh[:, t, h0:h0 + 8, :], ALU.mult),
                        dep=True, mark=True)
                    ctx.emit("v", lambda v, pi_=pi_, h0=h0: v.reduce_sum(
                        Sr[:, pi_, h0:h0 + 8, :], scr[:, :, :], axis=AX.X),
                        dep=True, mark=True)

        ctx.emit("v", lambda v: v.engine_nop(), dep=True, mark=True)
        vsync = ctx._last_inc["v"]
        ctx.emit("v", lambda v: v.reciprocal(kn.rearrange("p s h -> p (s h)"),
                                             kn.rearrange("p s h -> p (s h)")),
                 wait=[vsync], dep=True, mark=True)
        ctx.emit("v", lambda v: v.reciprocal(qn.rearrange("p s h -> p (s h)"),
                                             qn.rearrange("p s h -> p (s h)")),
                 dep=True, mark=True)
        for pi_, (t, r) in enumerate(APAIR):
            ctx.emit("v", lambda v, pi_=pi_, t=t: v.tensor_tensor(
                Abuf[:, pi_, :], Abuf[:, pi_, :], kn[:, t, :], ALU.mult),
                dep=True, mark=True)
            ctx.emit("v", lambda v, pi_=pi_, r=r: v.tensor_tensor(
                Abuf[:, pi_, :], Abuf[:, pi_, :], kn[:, r, :], ALU.mult),
                dep=True, mark=True)
        for pi_, (s, t) in enumerate(SPAIR):
            ctx.emit("v", lambda v, pi_=pi_, s=s: v.tensor_tensor(
                Sbuf[:, pi_, :], Sbuf[:, pi_, :], qn[:, s, :], ALU.mult),
                dep=True, mark=True)
            ctx.emit("v", lambda v, pi_=pi_, t=t: v.tensor_tensor(
                Sbuf[:, pi_, :], Sbuf[:, pi_, :], kn[:, t, :], ALU.mult),
                dep=True, mark=True)
            if t == 0:
                ctx.emit("v", lambda v, pi_=pi_, s=s: v.tensor_copy(
                    dn[:, s, :], Sbuf[:, pi_, :]), dep=True, mark=True)
            else:
                ctx.emit("v", lambda v, pi_=pi_, s=s: v.tensor_tensor(
                    dn[:, s, :], dn[:, s, :], Sbuf[:, pi_, :], ALU.add),
                    dep=True, mark=True)
        ctx.emit("v", lambda v: v.memset(kd[:, 0, :], 1.0), dep=True, mark=True)
        for t in range(1, 8):
            for r in range(t):
                pi_ = APAIR.index((t, r))
                if r == 0:
                    ctx.emit("v", lambda v, t=t, pi_=pi_: v.tensor_copy(
                        kd[:, t, :], Abuf[:, pi_, :]), dep=True, mark=True)
                else:
                    ctx.emit("v", lambda v, t=t, pi_=pi_: v.tensor_tensor(
                        kd[:, t, :], kd[:, t, :], Abuf[:, pi_, :], ALU.add),
                        dep=True, mark=True)
        ctx.emit("v", lambda v: v.tensor_scalar(
            fT.rearrange("p s h -> p (s h)"), kd.rearrange("p s h -> p (s h)"),
            1e-5, None, ALU.add), dep=True, mark=True)
        ctx.emit("v", lambda v: v.reciprocal(
            fT.rearrange("p s h -> p (s h)"), fT.rearrange("p s h -> p (s h)")),
            dep=True, mark=True)
        vsync = ctx._last_inc["v"]
        for s in range(8):
            ctx.emit("s", lambda s_, s2=s, ph=ph: s_.activation(
                bP[:, s2, :],
                qkv_s[:, s2, :].rearrange("p (h f) -> p h f", f=193)[:, :, 192],
                AF.Sigmoid), wait=[vsync] if s == 0 else (), inc=[(smD, 1)])
            smD.tick()
        sig_done = smD.n
        ctx.emit("v", lambda v: v.tensor_tensor(
            bP.rearrange("p s h -> p (s h)"), bP.rearrange("p s h -> p (s h)"),
            kd.rearrange("p s h -> p (s h)"), ALU.mult),
            wait=[(smD, sig_done)], dep=True, mark=True)
        for pi_, (t, r) in enumerate(APAIR):
            ctx.emit("v", lambda v, pi_=pi_, t=t: v.tensor_tensor(
                Abuf[:, pi_, :], Abuf[:, pi_, :], fT[:, t, :], ALU.mult),
                dep=True, mark=True)
            ctx.emit("v", lambda v, pi_=pi_, r=r: v.tensor_tensor(
                Abuf[:, pi_, :], Abuf[:, pi_, :], fT[:, r, :], ALU.mult),
                dep=True, mark=True)
        for pi_, (s, t) in enumerate(SPAIR):
            ctx.emit("v", lambda v, pi_=pi_, t=t: v.tensor_tensor(
                Sbuf[:, pi_, :], Sbuf[:, pi_, :], fT[:, t, :], ALU.mult),
                dep=True, mark=True)
        ctx.emit("v", lambda v: v.tensor_scalar(
            dn.rearrange("p s h -> p (s h)"), dn.rearrange("p s h -> p (s h)"),
            1e-5, None, ALU.add), dep=True, mark=True)
        ctx.emit("v", lambda v: v.reciprocal(
            dn.rearrange("p s h -> p (s h)"), dn.rearrange("p s h -> p (s h)")),
            dep=True, mark=True)
        ctx.emit("v", lambda v: v.tensor_scalar(
            dn.rearrange("p s h -> p (s h)"), dn.rearrange("p s h -> p (s h)"),
            0.125, None, ALU.mult), dep=True, mark=True)

        def vall(t, h0):
            return qkv_s[:, t, :].rearrange("p (h f) -> p h f",
                                            f=193)[:, h0:h0 + 8, 128:192]

        def bcast(buf2d, h0):
            return buf2d[:, h0:h0 + 8].rearrange(
                "p (h one) -> p h one", one=1).broadcast_to((128, 8, 64))

        for h0 in (0, 8):
            for t in range(8):
                if t == 0:
                    ctx.emit("v", lambda v, h0=h0: v.tensor_tensor(
                        U[:, 0, h0:h0 + 8, :], vall(0, h0),
                        bcast(bP[:, 0, :], h0), ALU.mult),
                        dep=True, mark=True)
                else:
                    for r in range(t):
                        pi_ = APAIR.index((t, r))
                        if r == 0:
                            ctx.emit("v", lambda v, pi_=pi_, h0=h0: v.tensor_tensor(
                                acc[:, :, :], U[:, 0, h0:h0 + 8, :],
                                bcast(Abuf[:, pi_, :], h0), ALU.mult),
                                dep=True, mark=True)
                        else:
                            ctx.emit("v", lambda v, r=r, pi_=pi_, h0=h0: v.tensor_tensor(
                                tmp64[:, :, :], U[:, r, h0:h0 + 8, :],
                                bcast(Abuf[:, pi_, :], h0), ALU.mult),
                                dep=True, mark=True)
                            ctx.emit("v", lambda v: v.tensor_tensor(
                                acc[:, :, :], acc[:, :, :], tmp64[:, :, :],
                                ALU.add), dep=True, mark=True)
                    ctx.emit("v", lambda v, t=t, h0=h0: v.tensor_tensor(
                        tmp64[:, :, :], vall(t, h0), acc[:, :, :],
                        ALU.subtract), dep=True, mark=True)
                    ctx.emit("v", lambda v, t=t, h0=h0: v.tensor_tensor(
                        U[:, t, h0:h0 + 8, :], tmp64[:, :, :],
                        bcast(bP[:, t, :], h0), ALU.mult),
                        dep=True, mark=True)
            for s in range(8):
                for t in range(s + 1):
                    pi_ = SPAIR.index((s, t))
                    if t == 0:
                        ctx.emit("v", lambda v, pi_=pi_, h0=h0: v.tensor_tensor(
                            acc[:, :, :], U[:, 0, h0:h0 + 8, :],
                            bcast(Sbuf[:, pi_, :], h0), ALU.mult),
                            dep=True, mark=True)
                    else:
                        ctx.emit("v", lambda v, t=t, pi_=pi_, h0=h0: v.tensor_tensor(
                            tmp64[:, :, :], U[:, t, h0:h0 + 8, :],
                            bcast(Sbuf[:, pi_, :], h0), ALU.mult),
                            dep=True, mark=True)
                        ctx.emit("v", lambda v: v.tensor_tensor(
                            acc[:, :, :], acc[:, :, :], tmp64[:, :, :],
                            ALU.add), dep=True, mark=True)
                ctx.emit("v", lambda v, s=s, ph=ph, h0=h0: v.tensor_tensor(
                    OUT[:, 2 * s + ph, :].rearrange(
                        "p (h f) -> p h f", f=64)[:, h0:h0 + 8, :],
                    acc[:, :, :], bcast(dn[:, s, :], h0), ALU.mult),
                    dep=True, mark=True)

    ctx.barrier()
    ctx.free_to(m_ln)

    m4 = ctx.mark()
    loT = ctx.sb("loT", [128, 8, S], BF16)
    woT = ctx.sb("woT", [128, 8, D], BF16)
    gB = ctx.sb("gB", [128, 1024], F32)
    bB = ctx.sb("bB", [128, 1024], F32)
    grow = ctx.sb("grow", [1, 1024], F32)
    onesf1 = ctx.sb("onesf1", [1, 128], F32)
    xin = ctx.sb("xin", [128, 2, 1024], F32)
    xx = ctx.sb("xx", [128, 2, 1024], F32)
    st8 = ctx.sb("st8", [128, 8], F32)
    scrq = ctx.sb("scrq", [128, 1024], BF16)
    psT = ctx.ps("psT", [128, 2, 1024], BF16)
    psX = ctx.ps("psX", [128, 2, 512], F32)
    psB = ctx.ps("psB", [128, 1024], F32)

    smtr2 = ctx.sem("tps")
    smwo = ctx.sem("wl_")
    smbb = ctx.sem("bc2")
    for mc in range(8):
        ctx.emit("y", lambda y, mc=mc: y.dma_start(
            woT[:, mc, :], wobf[:, mc * 128:(mc + 1) * 128], transpose=True),
            wait=[(smw, wcast_all)] if mc == 0 else (), inc=[(smwo, 16)])
        smwo.tick(16)
    wo_done = smwo.n
    ctx.emit("v", lambda v: v.memset(onesf1[:, :], 1.0), mark=True)
    ones_ok = ctx._last_inc["v"]
    for name, srcap, dstb in (("g", ln_g, gB), ("b", ln_b, bB)):
        ctx.dma("y", grow[:, :], srcap[None, :], wait=[(smbb, smbb.n)],
                inc=[(smbb, 16)])
        smbb.tick(16)
        for hf in range(2):
            ctx.emit("t", lambda t, hf=hf: t.matmul(
                psB[:, hf * 512:(hf + 1) * 512], onesf1[:, :],
                grow[:, hf * 512:(hf + 1) * 512], start=True, stop=True),
                wait=[(smbb, smbb.n), ones_ok] if hf == 0 else (),
                inc=[(smbb, 1)] if hf == 1 else [])
        smbb.tick()
        ctx.emit("v", lambda v, dstb=dstb: v.tensor_copy(dstb[:, :], psB[:, :]),
                 wait=[(smbb, smbb.n)], inc=[(smbb, 1)])
        smbb.tick()
    bcast_done = smbb.n

    for i in range(NTILE):
        for cch in range(8):
            k = i * 8 + cch
            par = k % 2
            w = []
            if k >= 2:
                w.append((smtr2, smtr2.n - 1))
            ctx.emit("t", lambda t, i=i, cch=cch, par=par: t.transpose(
                psT[:, par, 0:128], OUT[:, i, cch * 128:(cch + 1) * 128],
                identbf[:, :]), wait=w, inc=[(smtr2, 1)])
            smtr2.tick()
            ctx.emit("s", lambda s, i=i, cch=cch, par=par: s.copy(
                loT[:, cch, i * 128:(i + 1) * 128], psT[:, par, 0:128]),
                wait=[(smtr2, smtr2.n)], inc=[(smtr2, 1)])
            smtr2.tick()
    tr2_done = smtr2.n

    smXm = ctx.sem("mm2")
    smXe = ctx.sem("ev2")
    smst = ctx.sem("t2s")
    smxl2 = [ctx.sem(f"lx{j}") for j in range(2)]
    smxs2 = [ctx.sem(f"sx{j}") for j in range(2)]
    for gt in range(NTILE):
        slot = gt % 2
        smxl = smxl2[slot]
        smxs = smxs2[slot]
        wx = [(smxs, smxs.n)] if gt >= 2 else []
        ctx.dma("y", xin[:, slot, :], xv[:, gt, :], wait=wx, inc=[(smxl, 16)])
        xl = smxl.tick(16)
        for oh in range(2):
            it2 = gt * 2 + oh
            par = it2 % 2
            w2 = [(smwo, wo_done), (smtr2, tr2_done)]
            if it2 >= 2:
                w2.append((smXe, smXe.n - 1))
            for c2 in range(8):
                ctx.emit("t", lambda t, gt=gt, oh=oh, c2=c2, par=par: t.matmul(
                    psX[:, par, :], loT[:, c2, gt * 128:(gt + 1) * 128],
                    woT[:, c2, oh * 512:(oh + 1) * 512],
                    start=(c2 == 0), stop=(c2 == 7)),
                    wait=w2 if c2 == 0 else (),
                    inc=[(smXm, 1)] if c2 == 7 else [])
            mmw = smXm.tick()
            ctx.emit("v", lambda v, gt=gt: v.tensor_tensor(
                st8[:, 0:1], mrs[:, gt, 0:1], mrs[:, gt, 1:2], ALU.mult),
                wait=[(smXm, mmw), (smxl, xl)], dep=(it2 > 0), mark=True)
            ctx.emit("v", lambda v: v.tensor_scalar(
                st8[:, 0:1], st8[:, 0:1], -1.0, None, ALU.mult),
                dep=True, mark=True)
            ctx.emit("v", lambda v, slot=slot, gt=gt, oh=oh: v.tensor_scalar(
                xx[:, slot, oh * 512:(oh + 1) * 512],
                xin[:, slot, oh * 512:(oh + 1) * 512],
                mrs[:, gt, 1:2], st8[:, 0:1], ALU.mult, ALU.add),
                dep=True, mark=True)
            ctx.emit("v", lambda v, slot=slot, oh=oh, par=par: v.tensor_tensor(
                xx[:, slot, oh * 512:(oh + 1) * 512],
                xx[:, slot, oh * 512:(oh + 1) * 512],
                psX[:, par, :], ALU.add), dep=True, inc=[(smXe, 1)])
            smXe.tick()
        ctx.emit("v", lambda v, slot=slot: v.scalar_tensor_tensor(
            scrq[:, :], xx[:, slot, :], 1.0,
            xx[:, slot, :], ALU.mult, ALU.mult, accum_out=st8[:, 1:2]),
            dep=True, mark=True)
        ctx.emit("v", lambda v, slot=slot: v.reduce_sum(
            st8[:, 2:3], xx[:, slot, :], axis=AX.X), dep=True, inc=[(smst, 1)])
        t0 = smst.tick()
        ctx.emit("s", lambda s: s.mul(st8[:, 3:4], st8[:, 2:3], 1.0 / 1024.0),
                 wait=[(smst, t0)], mark=True)
        ctx.emit("s", lambda s: s.mul(st8[:, 4:5], st8[:, 1:2], 1.0 / 1024.0),
                 dep=True, mark=True)
        ctx.emit("s", lambda s: s.activation(st8[:, 5:6], st8[:, 3:4], AF.Square),
                 dep=True, inc=[(smst, 1)])
        t1 = smst.tick()
        ctx.emit("v", lambda v: v.tensor_tensor(st8[:, 4:5], st8[:, 4:5],
                                                st8[:, 5:6], ALU.subtract),
                 wait=[(smst, t1)], inc=[(smst, 1)])
        t2 = smst.tick()
        ctx.emit("s", lambda s, _e=ctx._eps: s.activation(
            st8[:, 4:5], st8[:, 4:5], AF.Sqrt, bias=_e[:, :]),
                 wait=[(smst, t2)], inc=[(smst, 1)])
        t3 = smst.tick()
        ctx.emit("v", lambda v: v.reciprocal(st8[:, 5:6], st8[:, 4:5]),
                 wait=[(smst, t3)], mark=True)
        ctx.emit("v", lambda v, slot=slot: v.tensor_scalar(
            xx[:, slot, :], xx[:, slot, :], st8[:, 3:4], st8[:, 5:6],
            ALU.subtract, ALU.mult), dep=True, mark=True)
        ctx.emit("v", lambda v, slot=slot: v.tensor_tensor(
            xx[:, slot, :], xx[:, slot, :], gB[:, :], ALU.mult),
            dep=True, mark=True)
        ctx.emit("v", lambda v, slot=slot: v.tensor_tensor(
            xx[:, slot, :], xx[:, slot, :], bB[:, :], ALU.add),
            dep=True, mark=True)
        ctx.emit("v", lambda v, slot=slot: v.tensor_tensor(
            xin[:, slot, :], xin[:, slot, :], xx[:, slot, :], ALU.add),
            dep=True, inc=[(smXe, 1)])
        smXe.tick()
        ctx.dma("g", xov[:, gt, :], xin[:, slot, :],
                wait=[(smXe, smXe.n)], inc=[(smxs, 16)])
        smxs.tick(16)

    ctx.emit("y", None, wait=[(smxs2[0], smxs2[0].n), (smxs2[1], smxs2[1].n)])
    ctx.barrier(all_sems=True)
    ctx.free_to(_m_all)




ECAP = 5121
EPAD = 5248
NEXP = 8


def emit_moe_route(ctx, ext):
    nc = ctx.nc
    ctx._eps_t = None
    _m_all = ctx.mark()
    x = ext["x"]
    rw = ext["rw"]
    cid = ext["cid"]
    h_out = ext["h_out"]
    einfo = ext["einfo"]
    counts = ext["counts"]
    idxp = ext["idxp"]

    c_ident = ctx.il0(np_ident(), "identsrc")
    c_iota8 = ctx.il0(
        np.tile(np.arange(8, dtype=np.float32)[None, :], (128, 1)), "iota8c")
    c_ustrict = ctx.il0(
        np.triu(np.ones((128, 128), np.float32), 1).T * 0 +
        np.tril(np.ones((128, 128), np.float32), -1).T, "ustrictc")
    c_iotap = ctx.il0(np.arange(128, dtype=np.float32)[:, None], "iotapc")

    xv = x.rearrange("(n p) d -> p n d", p=128)
    hv = h_out.rearrange("(n p) d -> p n d", p=128)
    ev = einfo.rearrange("(n p) e -> p n e", p=128)

    identf = ctx.sb("identf", [128, 128], F32)
    iota8 = ctx.sb("iota8", [128, 8], F32)
    iotap = ctx.sb("iotap", [128, 1], F32)
    ustr = ctx.sb("ustr", [128, 128], BF16)
    rws = ctx.sb("rws", [8, D], F32)
    rwT = ctx.sb("rwT", [128, 8, 8], F32)
    cid1 = ctx.sb("cid1", [1, 1], F32)
    cidB = ctx.sb("cidB", [128, 1], F32)
    onesf1 = ctx.sb("onesf1", [1, 128], F32)
    onesb = ctx.sb("onesb", [1, 128], BF16)
    onescol = ctx.sb("onescol", [128, 1], BF16)
    xs = ctx.sb("xs", [128, NTILE, 1024], F32)
    xT = ctx.sb("xT", [128, 8, 128], F32)
    st = ctx.sb("stx", [128, 8], F32)
    probs = ctx.sb("probs", [128, 8], F32)
    mx8 = ctx.sb("mx8", [128, 8], F32)
    oh0 = ctx.sb("oh0", [128, 8], F32)
    oh1 = ctx.sb("oh1", [128, 8], F32)
    ohc = ctx.sb("ohc", [128, 8], BF16)
    einf = ctx.sb("einf", [128, NTILE, 8], F32)
    offrow = ctx.sb("offrow", [1, 8], F32)
    offhl = ctx.sb("offhl", [1, 2, 8], BF16)
    cu = ctx.sb("cu", [128, 8], F32)
    dst0 = ctx.sb("dst0", [128, 2], I32)
    gidc = ctx.sb("gidc", [128, 1], F32)
    zz = ctx.sb("zz", [128, 1024], F32)
    hbuf = ctx.sb("hbuf", [128, 1024], BF16)
    psT = ctx.ps("psT", [128, 2, 512], F32)
    psL = ctx.ps("psL", [128, 512], F32)
    psC = ctx.ps("psC", [128, 512], F32)

    ctx._eps0 = ctx.sb("eps0", [128, 1], F32)
    _e0 = ctx._eps0
    ctx.emit("v", lambda v, _e0=_e0: v.memset(_e0[:, :], 1e-5), mark=True)
    smc = ctx.sem("consts")
    smx3 = [ctx.sem(f"lx{j}") for j in range(3)]
    smt = ctx.sem("tps")
    smm = ctx.sem("mms")
    sma = ctx.sem("acts")
    smsc = ctx.sem("scat")
    smst = ctx.sem("sx0")

    ctx.dma("g", identf[:, :], c_ident.ap()[:, :], inc=[(smc, 16)])
    smc.tick(16)
    ctx.dma("g", iota8[:, :], c_iota8.ap()[:, :], inc=[(smc, 16)])
    smc.tick(16)
    ctx.dma("g", iotap[:, :], c_iotap.ap()[:, :], inc=[(smc, 16)])
    smc.tick(16)
    ctx.dma("g", ustr[:, :], c_ustrict.ap()[:, :], inc=[(smc, 16)])
    smc.tick(16)
    ctx.dma("g", rws[:, :], rw[:, :], inc=[(smc, 16)])
    smc.tick(16)
    ctx.dma("g", cid1[:, :], cid[None, :], inc=[(smc, 16)])
    smc.tick(16)
    ctx.dma("y", xs[:, :, :], xv, inc=[(smx3[0], 16)])
    xs_ok = smx3[0].tick(16)
    consts_ok = smc.n

    ctx.emit("v", lambda v: v.memset(onesf1[:, :], 1.0), mark=True)
    ctx.emit("v", lambda v: v.memset(onesb[:, :], 1.0), dep=True, mark=True)
    ctx.emit("v", lambda v: v.memset(onescol[:, :], 1.0), dep=True, mark=True)
    ctx.emit("v", lambda v: v.memset(einf.rearrange("p n e -> p (n e)"), 0.0),
             dep=True, mark=True)
    ctx.emit("v", lambda v: v.memset(offrow[:, :], 0.0), dep=True, mark=True)
    ctx.emit("v", lambda v: v.memset(zz[:, :], 0.0), dep=True, mark=True)
    ones_ok = ctx._last_inc["v"]

    for mc in range(8):
        ctx.emit("t", lambda t, mc=mc: t.transpose(
            psT[:, mc % 2, 0:8], rws[:, mc * 128:(mc + 1) * 128], identf[0:8, 0:8]),
            wait=[(smc, consts_ok), (smt, smt.n - 1)] if mc >= 2 else
            [(smc, consts_ok)], inc=[(smt, 1)])
        smt.tick()
        ctx.emit("v", lambda v, mc=mc: v.tensor_copy(rwT[:, mc, :],
                                                     psT[:, mc % 2, 0:8]),
                 wait=[(smt, smt.n)], inc=[(smt, 1)])
        smt.tick()
    ctx.emit("t", lambda t: t.matmul(psT[:, 0, 0:1], onesf1[:, :], cid1[:, :],
                                     start=True, stop=True),
             wait=[(smc, consts_ok), (smt, smt.n)], inc=[(smt, 1)])
    smt.tick()
    ctx.emit("v", lambda v: v.tensor_copy(cidB[:, :], psT[:, 0, 0:1]),
             wait=[(smt, smt.n)], inc=[(smt, 1)])
    smt.tick()
    prep_ok = smt.n

    idxf = idxp.rearrange("e n -> (e n)")
    total = NEXP * (EPAD + 4096)
    off0 = 0
    nz = 0
    while off0 < total:
        wdt = min(128 * 1024, total - off0)
        if wdt % 128 == 0:
            ctx.dma("g", idxf[off0:off0 + wdt].rearrange("(p a) -> p a", p=128),
                    zz[:, 0:wdt // 128], wait=[ones_ok], inc=[(smsc, 16)])
        else:
            ctx.dma("g", idxf[off0:off0 + wdt][None, :], zz[0:1, 0:wdt],
                    wait=[ones_ok], inc=[(smsc, 16)])
        smsc.tick(16)
        nz += 1
        off0 += wdt
    zeroed = smsc.n

    h_hook = ext.get("h_hook")
    sst0 = smst.n
    for i in range(NTILE):
        st4 = st
        xt = xs[:, i, :]
        ctx.emit("v", lambda v, xt=xt: v.scalar_tensor_tensor(
            hbuf[:, :], xt, 1.0, xt, ALU.mult, ALU.mult, accum_out=st[:, 1:2]),
            wait=[(smx3[0], xs_ok), (smst, smst.n)], dep=True, mark=True)
        ctx.emit("v", lambda v, xt=xt: v.reduce_sum(st[:, 0:1], xt, axis=AX.X),
                 dep=True, inc=[(sma, 1)])
        t0 = sma.tick()
        ctx.emit("s", lambda s: s.mul(st[:, 2:3], st[:, 0:1], 1.0 / 1024.0),
                 wait=[(sma, t0)], mark=True)
        ctx.emit("s", lambda s: s.mul(st[:, 0:1], st[:, 1:2], 1.0 / 1024.0),
                 dep=True, mark=True)
        ctx.emit("s", lambda s: s.activation(st[:, 1:2], st[:, 2:3], AF.Square),
                 dep=True, inc=[(sma, 1)])
        t1 = sma.tick()
        ctx.emit("v", lambda v: v.tensor_tensor(st[:, 0:1], st[:, 0:1],
                                                st[:, 1:2], ALU.subtract),
                 wait=[(sma, t1)], inc=[(sma, 1)])
        t2 = sma.tick()
        ctx.emit("s", lambda s: s.activation(st[:, 0:1], st[:, 0:1], AF.Sqrt,
                                             bias=_e0[:, :]),
                 wait=[(sma, t2)], inc=[(sma, 1)])
        t3 = sma.tick()
        ctx.emit("v", lambda v: v.reciprocal(st[:, 3:4], st[:, 0:1]),
                 wait=[(sma, t3)], mark=True)
        ctx.emit("v", lambda v, xt=xt: v.tensor_scalar(
            xt, xt, st[:, 2:3], None, ALU.subtract), dep=True, mark=True)
        ctx.emit("v", lambda v, xt=xt: v.tensor_scalar(
            hbuf[:, :], xt, st[:, 3:4], None, ALU.mult), dep=True,
            inc=[(sma, 1)])
        h_ok = sma.tick()
        ctx.dma("g", hv[:, i, :], hbuf[:, :], wait=[(sma, h_ok)],
                inc=[(smst, 16)])
        smst.tick(16)
        for cch in range(8):
            k = i * 8 + cch
            par = k % 2
            w = [(sma, h_ok)]
            if k >= 2:
                w.append((smt, smt.n - 1))
            ctx.emit("t", lambda t, xt=xt, cch=cch, par=par: t.transpose(
                psT[:, par, 0:128], xt[:, cch * 128:(cch + 1) * 128],
                identf[:, :]), wait=w, inc=[(smt, 1)])
            smt.tick()
            ctx.emit("v", lambda v, cch=cch, par=par: v.tensor_copy(
                xT[:, cch, :], psT[:, par, 0:128]),
                wait=[(smt, smt.n)], inc=[(smt, 1)])
            smt.tick()
        tr_ok = smt.n
        for mc in range(8):
            ctx.emit("t", lambda t, mc=mc: t.matmul(
                psL[:, 0:8], xT[:, mc, :], rwT[:, mc, :],
                start=(mc == 0), stop=(mc == 7)),
                wait=[(smt, tr_ok), (smt if False else sma, h_ok)] if mc == 0 else (),
                inc=[(smm, 1)] if mc == 7 else [])
        mmw = smm.tick()
        ctx.emit("v", lambda v: v.tensor_scalar(
            probs[:, :], psL[:, 0:8], st[:, 3:4], None, ALU.mult),
            wait=[(smm, mmw)], dep=True, mark=True)
        ctx.emit("v", lambda v: v.reduce_max(st[:, 4:5], probs[:, :],
                                             axis=AX.X, negate=True),
                 dep=True, inc=[(sma, 1)])
        t4 = sma.tick()
        ctx.emit("s", lambda s: s.activation(probs[:, :], probs[:, :], AF.Exp,
                                             bias=st[:, 4:5],
                                             accum_out=st[:, 5:6]),
                 wait=[(sma, t4)], inc=[(sma, 1)])
        t5 = sma.tick()
        ctx.emit("v", lambda v: v.reciprocal(st[:, 6:7], st[:, 5:6]),
                 wait=[(sma, t5)], mark=True)
        ctx.emit("v", lambda v: v.tensor_scalar(
            probs[:, :], probs[:, :], st[:, 6:7], None, ALU.mult),
            dep=True, mark=True)
        ctx.emit("v", lambda v: v.max(mx8[:, :], probs[:, :]), dep=True,
                 mark=True)
        ctx.emit("v", lambda v: v.tensor_scalar(
            oh0[:, :], probs[:, :], mx8[:, 0:1], None, ALU.is_equal),
            dep=True, mark=True)
        ctx.emit("v", lambda v: v.tensor_scalar(
            oh1[:, :], probs[:, :], mx8[:, 1:2], None, ALU.is_equal),
            dep=True, mark=True)
        ctx.emit("v", lambda v: v.tensor_tensor(
            st[:, 7:8], mx8[:, 0:1], mx8[:, 1:2], ALU.add), dep=True, mark=True)
        ctx.emit("v", lambda v: v.reciprocal(st[:, 4:5], st[:, 7:8]),
                 dep=True, mark=True)
        ctx.emit("v", lambda v, i=i: v.tensor_tensor(
            einf[:, i, 2:3], mx8[:, 0:1], st[:, 4:5], ALU.mult),
            dep=True, mark=True)
        ctx.emit("v", lambda v, i=i: v.tensor_tensor(
            einf[:, i, 3:4], mx8[:, 1:2], st[:, 4:5], ALU.mult),
            dep=True, mark=True)
        ctx.emit("v", lambda v: v.tensor_tensor(cu[:, :], oh0[:, :],
                                                iota8[:, :], ALU.mult),
                 dep=True, mark=True)
        ctx.emit("v", lambda v, i=i: v.reduce_sum(einf[:, i, 0:1], cu[:, :],
                                                  axis=AX.X), dep=True, mark=True)
        ctx.emit("v", lambda v: v.tensor_tensor(cu[:, :], oh1[:, :],
                                                iota8[:, :], ALU.mult),
                 dep=True, mark=True)
        ctx.emit("v", lambda v, i=i: v.reduce_sum(einf[:, i, 1:2], cu[:, :],
                                                  axis=AX.X), dep=True, mark=True)
        ctx.emit("v", lambda v: v.tensor_tensor(ohc[:, :], oh0[:, :],
                                                oh1[:, :], ALU.add),
                 dep=True, inc=[(sma, 1)])
        oh_ok = sma.tick()
        ctx.emit("v", lambda v: v.tensor_copy(offhl[:, 0, :], offrow[:, :]),
                 dep=True, mark=True)
        ctx.emit("v", lambda v: v.tensor_tensor(
            st[0:1, 0:8], offrow[:, :], offhl[:, 0, :], ALU.subtract),
            dep=True, mark=True)
        ctx.emit("v", lambda v: v.tensor_copy(offhl[:, 1, :], st[0:1, 0:8]),
                 dep=True, inc=[(sma, 1)])
        ohl_ok = sma.tick()
        ctx.emit("t", lambda t: t.matmul(psC[:, 0:8], ustr[:, :], ohc[:, :],
                                         start=True, stop=False),
                 wait=[(sma, oh_ok), (sma, ohl_ok)])
        ctx.emit("t", lambda t: t.matmul(psC[:, 0:8], onesb[:, :],
                                         offhl[:, 0, :], start=False, stop=False))
        ctx.emit("t", lambda t: t.matmul(psC[:, 0:8], onesb[:, :],
                                         offhl[:, 1, :], start=False, stop=True),
                 inc=[(smm, 1)])
        mm2 = smm.tick()
        ctx.emit("v", lambda v: v.tensor_tensor(cu[:, :], psC[:, 0:8],
                                                oh0[:, :], ALU.mult),
                 wait=[(smm, mm2)], mark=True)
        ctx.emit("v", lambda v, i=i: v.reduce_sum(einf[:, i, 4:5], cu[:, :],
                                                  axis=AX.X), dep=True, mark=True)
        ctx.emit("v", lambda v: v.tensor_tensor(cu[:, :], psC[:, 0:8],
                                                oh1[:, :], ALU.mult),
                 dep=True, mark=True)
        ctx.emit("v", lambda v, i=i: v.reduce_sum(einf[:, i, 5:6], cu[:, :],
                                                  axis=AX.X), dep=True, mark=True)
        ctx.emit("t", lambda t: t.matmul(psC[0:1, 8:16], onescol[:, :], ohc[:, :],
                                         start=True, stop=True),
                 inc=[(smm, 1)])
        mm3 = smm.tick()
        ctx.emit("v", lambda v: v.tensor_tensor(
            offrow[:, :], offrow[:, :], psC[0:1, 8:16], ALU.add),
            wait=[(smm, mm3)], dep=True, mark=True)
        ctx.emit("v", lambda v, i=i: v.tensor_scalar(
            gidc[:, :], cidB[:, :], 2048.0, float(i * 128 + 1), ALU.mult,
            ALU.add),
            wait=([(smt, prep_ok)] if i == 0 else []) + [(smsc, smsc.n)],
            dep=True, mark=True)
        ctx.emit("v", lambda v: v.tensor_tensor(gidc[:, :], gidc[:, :],
                                                iotap[:, :], ALU.add),
                 dep=True, mark=True)
        for kk in range(2):
            ctx.emit("v", lambda v, i=i, kk=kk: v.tensor_scalar(
                cu[:, 0:1], einf[:, i, kk:kk + 1], float(EPAD + 4096),
                None, ALU.mult), dep=True, mark=True)
            ctx.emit("v", lambda v, i=i, kk=kk: v.tensor_tensor(
                cu[:, 0:1], cu[:, 0:1], einf[:, i, 4 + kk:5 + kk], ALU.add),
                dep=True, mark=True)
            ctx.emit("v", lambda v, kk=kk: v.tensor_copy(
                dst0[:, kk:kk + 1], cu[:, 0:1]), dep=True, inc=[(sma, 1)])
            dsts = sma.tick()
            ctx.emit("g", lambda g, kk=kk: g.indirect_dma_start(
                idxf.rearrange("(n one) -> n one", one=1),
                IndirectOffsetOnAxis(dst0[:, kk:kk + 1], 0),
                gidc[:, :], None,
                bounds_check=ctx.breg(g, total - 1), oob_is_err=False),
                wait=[(sma, dsts), (smsc, zeroed)], inc=[(smsc, 16)])
            smsc.tick(16)
        ctx.dma("g", ev[:, i, :], einf[:, i, :],
                wait=[(sma, sma.n), (ctx._last_inc["v"][0], ctx._last_inc["v"][1])],
                inc=[(smst, 16)])
        smst.tick(16)

    if h_hook is not None:
        h_hook([(smst, smst.n)])
    ctx.dma("g", counts[None, :], offrow[:, :],
            wait=[(ctx._last_inc["v"][0], ctx._last_inc["v"][1])],
            inc=[(smst, 16)])
    smst.tick(16)
    ctx.emit("y", None, wait=[(smst, smst.n), (smsc, smsc.n)])
    ctx.barrier(all_sems=True)
    ctx.free_to(_m_all)




def emit_moe_expert(ctx, ext, fused=False):
    nc = ctx.nc
    ctx._eps_t = None
    _m_all = ctx.mark()
    h_all = ext["h_all"]
    idxp_all = ext["idxp_all"]
    counts_all = ext["counts_all"]
    cid = ext["cid"]
    w1 = ext["w1"]
    w2 = ext["w2"]
    out_d = ext["out_d"]
    w1bf = nc.dram_tensor(ctx.prefix + "w1bf", [D, 4096], BF16, kind="Internal").ap()
    w2bf = nc.dram_tensor(ctx.prefix + "w2bf", [4096, D], BF16, kind="Internal").ap()
    c_ident = ctx.il0(np_ident(), "identsrc")
    c_iotac = ctx.il0(np.arange(8, dtype=np.float32)[:, None], "iota8cc")
    c_iotap = ctx.il0(np.arange(128, dtype=np.float32)[:, None], "iotapc")

    PART = EPAD + 4096
    PSTRIDE = (NEXP * PART) if fused else PART

    identf = ctx.sb("identf", [128, 128], F32)
    identb = ctx.sb("identb", [128, 128], BF16)
    cnt = ctx.sb("cnt", [8, 8], F32)
    cntT = ctx.sb("cntT", [8, 8], F32)
    ohE = ctx.sb("ohE", [8, 1], F32)
    iota8c = ctx.sb("iota8c", [8, 1], F32)
    iotap = ctx.sb("iotap", [128, 1], F32)
    ones8 = ctx.sb("ones8", [1, 8], F32)
    onesf1 = ctx.sb("onesf1", [1, 128], F32)
    cid1 = ctx.sb("cid1", [1, 1], F32)
    prefB = ctx.sb("prefB", [128, 9], F32)
    pref = ctx.sb("pref", [1, 9], F32)
    srci = ctx.sb("srci", [128, 41], F32)
    srcii = ctx.sb("srcii", [128, 41], I32)
    gidx = ctx.sb("gidx", [128, 41], F32)
    gidxi = ctx.sb("gidxi", [128, 41], I32)
    scr41 = ctx.sb("scr41", [128, 41], F32)
    scr41b = ctx.sb("scr41b", [128, 41], F32)
    idxm = ctx.sb("idxm", [128, 41], F32)
    w1S = ctx.sb("w1S", [128, 8, 4096], BF16)
    w2S = ctx.sb("w2S", [128, 32, 1024], BF16)
    gath = ctx.sb("gath", [128, 4, 1024], BF16)
    bufT = ctx.sb("bufT", [128, 8, 512], BF16)
    h1sb = ctx.sb("h1sb", [128, 32, 512], BF16)
    orow = ctx.sb("orow", [128, 2, 512], BF16)
    psq = ctx.ps("psqE", [128, 2, 512], F32)
    psT = ctx.ps("psTE", [128, 2, 1024], BF16)
    psO = ctx.ps("psOE", [128, 2, 512], F32)

    smc = ctx.sem("consts")
    smw = ctx.sem("gc_")
    smwl = ctx.sem("wl_")
    smix = ctx.sem("scat")
    smt = ctx.sem("tps")
    smm = ctx.sem("mms")
    sma = ctx.sem("acts")
    smo = ctx.sem("ev3")
    smg4 = [ctx.sem(f"gt{j}") for j in range(4)]
    smst2 = [ctx.sem(f"sx{j}") for j in range(2)]

    for dst, srcw in ((identf, c_ident.ap()), (iota8c, c_iotac.ap()),
                      (iotap, c_iotap.ap())):
        ctx.dma("g", dst[:, :], srcw[:, :], inc=[(smc, 16)])
        smc.tick(16)
    ctx.dma("g", identb[:, :], c_ident.ap()[:, :], inc=[(smc, 16)])
    smc.tick(16)
    smcy = ctx.sem("lx2")
    ctx.dma("y", cnt[:, :], counts_all[:, :], inc=[(smcy, 16)])
    smcy.tick(16)
    ctx.dma("y", cid1[:, :], cid[None, :], inc=[(smcy, 16)])
    smcy.tick(16)
    consts_ok = smc.n
    ctx.dma("g", w1bf[:, :], w1[:, :], inc=[(smw, 16)])
    smw.tick(16)
    ctx.dma("g", w2bf[:, :], w2[:, :], inc=[(smw, 16)])
    smw.tick(16)
    for mc in range(8):
        ctx.dma("y", w1S[:, mc, :], w1bf[mc * 128:(mc + 1) * 128, :],
                wait=[(smw, smw.n)] if mc == 0 else (), inc=[(smwl, 16)])
        smwl.tick(16)
    for fc in range(32):
        ctx.dma("y", w2S[:, fc, :], w2bf[fc * 128:(fc + 1) * 128, :],
                inc=[(smwl, 16)])
        smwl.tick(16)
    w_ok = smwl.n

    ctx.emit("v", lambda v: v.memset(ones8[:, :], 1.0), mark=True)
    ctx.emit("v", lambda v: v.memset(onesf1[:, :], 1.0), dep=True, mark=True)
    ones_ok = ctx._last_inc["v"]

    ctx.emit("t", lambda t: t.transpose(psq[0:8, 0, 0:8], cnt[:, :],
                                        identf[0:8, 0:8]),
             wait=[(smc, consts_ok), (smcy, smcy.n)], inc=[(smt, 1)])
    smt.tick()
    ctx.emit("v", lambda v: v.tensor_copy(cntT[:, :], psq[0:8, 0, 0:8]),
             wait=[(smt, smt.n)], inc=[(smt, 1)])
    smt.tick()
    ctx.emit("t", lambda t: t.matmul(psq[0:8, 1, 0:1], ones8[:, :], cid1[:, :],
                                     start=True, stop=True),
             wait=[ones_ok, (smt, smt.n)], inc=[(smt, 1)])
    smt.tick()
    ctx.emit("v", lambda v: v.tensor_copy(pref[:, 0:1], psq[0:1, 1, 0:1]),
             wait=[(smt, smt.n)], mark=True)
    ctx.emit("v", lambda v: v.tensor_scalar(
        ohE[:, :], iota8c[:, :], psq[0:8, 1, 0:1], None, ALU.is_equal),
        dep=True, inc=[(smt, 1)])
    smt.tick()
    ctx.emit("t", lambda t: t.matmul(psq[0:1, 0, 0:8], ohE[:, :], cntT[:, :],
                                     start=True, stop=True),
             wait=[(smt, smt.n)], inc=[(smt, 1)])
    smt.tick()
    ctx.emit("v", lambda v: v.tensor_copy(pref[:, 1:9], psq[0:1, 0, 0:8]),
             wait=[(smt, smt.n)], mark=True)
    ctx.emit("v", lambda v: v.memset(pref[:, 0:1], 0.0), dep=True, mark=True)
    for c in range(1, 8):
        ctx.emit("v", lambda v, c=c: v.tensor_tensor(
            pref[:, c:c + 1], pref[:, c - 1:c], pref[:, c:c + 1], ALU.add),
            dep=True, mark=True)
    ctx.emit("v", lambda v: v.engine_nop(), dep=True, inc=[(sma, 1)])
    pref_ok = sma.tick()
    ctx.emit("t", lambda t: t.matmul(psq[:, 0, 0:9], onesf1[:, :], pref[:, 0:9],
                                     start=True, stop=True),
             wait=[(sma, pref_ok)], inc=[(smt, 1)])
    smt.tick()
    ctx.emit("v", lambda v: v.tensor_copy(prefB[:, :], psq[:, 0, 0:9]),
             wait=[(smt, smt.n)], inc=[(sma, 1)])
    prefb_ok = sma.tick()

    c_col = ctx.il0(
        (128.0 * np.arange(41, dtype=np.float32))[None, :].repeat(128, 0), "colcc")
    colc = ctx.sb("colc", [128, 41], F32)
    smc5 = ctx.sem("colcs")
    ctx.dma("g", colc[:, :], c_col.ap()[:, :], inc=[(smc5, 16)])
    smc5.tick(16)
    cst3 = smc5.n
    ctx.emit("v", lambda v: v.tensor_scalar(
        scr41[:, :], colc[:, :], iotap[:, 0:1], None, ALU.add),
        wait=[(smc5, cst3), (sma, prefb_ok)], dep=True, mark=True)
    ctx.emit("v", lambda v: v.memset(srci[:, :], 0.0), dep=True, mark=True)
    ctx.emit("v", lambda v: v.memset(gidx[:, :], 0.0), dep=True, mark=True)
    for c in range(1, 8):
        ctx.emit("v", lambda v, c=c: v.tensor_scalar(
            scr41b[:, :], scr41[:, :], prefB[:, c:c + 1], None, ALU.is_ge),
            dep=True, mark=True)
        ctx.emit("v", lambda v: v.scalar_tensor_tensor(
            srci[:, :], scr41b[:, :], float(PSTRIDE), srci[:, :],
            ALU.mult, ALU.add),
            dep=True, mark=True)
        ctx.emit("v", lambda v, c=c: v.tensor_tensor(
            scr41b[:, 0:1] if False else prefB[:, 8:9], prefB[:, c:c + 1],
            prefB[:, c - 1:c], ALU.subtract), dep=True, mark=True)
        ctx.emit("v", lambda v, c=c: v.scalar_tensor_tensor(
            gidx[:, :], scr41b[:, :], prefB[:, 8:9], gidx[:, :],
            ALU.mult, ALU.add), dep=True, mark=True)
    ctx.emit("v", lambda v: v.tensor_tensor(srci[:, :], srci[:, :],
                                            scr41[:, :], ALU.add),
             dep=True, mark=True)
    if fused:
        cidB128 = ctx.sb("cidB128", [128, 1], F32)
        ctx.emit("t", lambda t: t.matmul(psq[:, 1, 0:1], onesf1[:, :],
                                         cid1[:, :], start=True, stop=True),
                 wait=[(ctx._last_inc["v"][0], ctx._last_inc["v"][1])],
                 inc=[(smt, 1)])
        smt.tick()
        ctx.emit("v", lambda v: v.tensor_copy(cidB128[:, :], psq[:, 1, 0:1]),
                 wait=[(smt, smt.n)], dep=True, mark=True)
        ctx.emit("v", lambda v: v.tensor_scalar(
            cidB128[:, :], cidB128[:, :], float(PART), None, ALU.mult),
            dep=True, mark=True)
        ctx.emit("v", lambda v: v.tensor_scalar(
            srci[:, :], srci[:, :], cidB128[:, 0:1], None, ALU.add),
            dep=True, mark=True)
    ctx.emit("v", lambda v: v.tensor_tensor(srci[:, :], srci[:, :],
                                            gidx[:, :], ALU.subtract),
             dep=True, mark=True)
    ctx.emit("v", lambda v: v.tensor_copy(srcii[:, :], srci[:, :]),
             dep=True, inc=[(sma, 1)])
    srci_ok = sma.tick()

    idxp_flat = idxp_all.rearrange("c n -> (c n)")
    ctx._bndreg = [None, None]

    def bndreg(g, which, val):
        if ctx._bndreg[which] is None:
            r = g.alloc_register(f"bnd{which}")
            g.reg_mov(r, val)
            ctx._bndreg[which] = r
        return ctx._bndreg[which]

    for j in range(41):
        ctx.emit("g", lambda g, j=j: g.indirect_dma_start(
            idxm[:, j:j + 1], None,
            idxp_flat.rearrange("(n one) -> n one", one=1),
            IndirectOffsetOnAxis(srcii[:, j:j + 1], 0),
            bounds_check=ctx.breg(g, NCORE * PSTRIDE - 1), oob_is_err=False),
            wait=[(sma, srci_ok)] if j == 0 else (), inc=[(smix, 16)])
        smix.tick(16)
    idxm_ok = smix.n
    ctx.emit("v", lambda v: v.tensor_scalar(
        gidx[:, :], idxm[:, :], 0.5, None, ALU.is_lt),
        wait=[(smix, idxm_ok)], dep=True, mark=True)
    ctx.emit("v", lambda v: v.tensor_scalar(
        gidx[:, :], gidx[:, :], 1.0e7, None, ALU.mult), dep=True, mark=True)
    ctx.emit("v", lambda v: v.tensor_tensor(gidx[:, :], gidx[:, :],
                                            idxm[:, :], ALU.add),
             dep=True, mark=True)
    ctx.emit("v", lambda v: v.tensor_scalar(
        gidx[:, :], gidx[:, :], 1.0, None, ALU.subtract), dep=True, mark=True)
    ctx.emit("v", lambda v: v.tensor_copy(gidxi[:, :], gidx[:, :]),
             dep=True, inc=[(sma, 1)])
    gidx_ok = sma.tick()

    groups = [(g * 4, 4) for g in range(10)] + [(40, 1)]
    o_hook = ext.get("o_hook")
    _ck = {}
    for gi, (j0, tg) in enumerate(groups):
        for jj in range(tg):
            j = j0 + jj
            slot = jj
            wz = [(sma, gidx_ok)]
            if gi >= 1:
                wz.append((smm, smm.n))
            ctx.emit("v", lambda v, slot=slot: v.memset(gath[:, slot, :], 0.0),
                     wait=wz, dep=True, inc=[(sma, 1)])
            gz = sma.tick()
            ctx.emit("g", lambda g, j=j, slot=slot: g.indirect_dma_start(
                gath[:, slot, :], None, h_all[:, :],
                IndirectOffsetOnAxis(gidxi[:, j:j + 1], 0),
                bounds_check=ctx.breg(g, B * S - 1), oob_is_err=False),
                wait=[(sma, gz)], inc=[(smg4[slot], 16)])
            smg4[slot].tick(16)
        for jj in range(tg):
            for cch in range(8):
                k = jj * 8 + cch
                par = k % 2
                w = [(smg4[jj], smg4[jj].n)]
                if smt.n >= 2:
                    w.append((smt, smt.n - 1))
                ctx.emit("t", lambda t, jj=jj, cch=cch, par=par: t.transpose(
                    psT[:, par, 0:128], gath[:, jj, cch * 128:(cch + 1) * 128],
                    identb[:, :]), wait=w, inc=[(smt, 1)])
                smt.tick()
                ctx.emit("s", lambda s, jj=jj, cch=cch, par=par: s.copy(
                    bufT[:, cch, jj * 128:(jj + 1) * 128], psT[:, par, 0:128]),
                    wait=[(smt, smt.n)], inc=[(smt, 1)])
                smt.tick()
        tr_ok = smt.n
        wd = tg * 128
        for fc in range(32):
            it = fc % 2
            w2_ = [(smt, tr_ok), (smwl, w_ok)]
            if fc >= 2:
                w2_.append((sma, sma.n - 1))
            for mc in range(8):
                ctx.emit("t", lambda t, fc=fc, mc=mc, it=it, wd=wd: t.matmul(
                    psq[:, it, 0:wd], w1S[:, mc, fc * 128:(fc + 1) * 128],
                    bufT[:, mc, 0:wd], start=(mc == 0), stop=(mc == 7)),
                    wait=w2_ if mc == 0 else (),
                    inc=[(smm, 1)] if mc == 7 else [])
            mmw = smm.tick()
            ctx.emit("s", lambda s, fc=fc, it=it, wd=wd: s.activation(
                h1sb[:, fc, 0:wd], psq[:, it, 0:wd], AF.Silu),
                wait=[(smm, mmw)], inc=[(sma, 1)])
            sma.tick()
        h1_ok = sma.n
        for jj in range(tg):
            for oh in range(2):
                it2 = jj * 2 + oh
                par = it2 % 2
                w2_ = [(sma, h1_ok)]
                if smo.n >= 2:
                    w2_.append((smo, smo.n - 1))
                for fc in range(32):
                    ctx.emit("t", lambda t, jj=jj, oh=oh, fc=fc, par=par: t.matmul(
                        psO[:, par, :], h1sb[:, fc, jj * 128:(jj + 1) * 128],
                        w2S[:, fc, oh * 512:(oh + 1) * 512],
                        start=(fc == 0), stop=(fc == 31)),
                        wait=w2_ if fc == 0 else (),
                        inc=[(smm, 1)] if fc == 31 else [])
                mmw = smm.tick()
                sslot = it2 % 2
                ctx.emit("v", lambda v, sslot=sslot, oh=oh, par=par: v.tensor_copy(
                    orow[:, sslot, :], psO[:, par, :]),
                    wait=[(smm, mmw), (smst2[sslot], smst2[sslot].n)],
                    inc=[(smo, 1)])
                ow = smo.tick()
                j = j0 + jj
                if isinstance(out_d, list):
                    ck = 0 if j < 16 else (1 if j < 32 else 2)
                    odt, jl = out_d[ck], j - (0, 16, 32)[ck]
                else:
                    odt, jl = out_d, j
                ctx.dma("g", odt.rearrange("(a p) d -> p a d", p=128)[
                    :, jl, oh * 512:(oh + 1) * 512], orow[:, sslot, :],
                    wait=[(smo, ow)], inc=[(smst2[sslot], 16)])
                smst2[sslot].tick(16)
        if o_hook is not None:
            if gi in (3, 7):
                _ck[gi] = [(smst2[0], smst2[0].n), (smst2[1], smst2[1].n)]
            elif gi == 5:
                o_hook(0, _ck[3])
            elif gi == 9:
                o_hook(1, _ck[7])

    if o_hook is not None:
        o_hook(2, [(smst2[0], smst2[0].n), (smst2[1], smst2[1].n)])
    ctx.emit("y", None, wait=[(smst2[0], smst2[0].n), (smst2[1], smst2[1].n)])
    ctx.barrier(all_sems=True)
    ctx.free_to(_m_all)


def emit_moe_combine(ctx, ext):
    nc = ctx.nc
    ctx._eps_t = None
    _m_all = ctx.mark()
    x = ext["x"]
    h_in = ext["h_in"]
    einfo = ext["einfo"]
    counts_all = ext["counts_all"]
    cid = ext["cid"]
    out_all = ext["out_all"]
    xo = ext["xo"]
    c_ident = ctx.il0(np_ident(), "identsrc")
    c_iota8 = ctx.il0(
        np.tile(np.arange(8, dtype=np.float32)[None, :], (128, 1)), "iota8c")

    xv = x.rearrange("(n p) d -> p n d", p=128)
    hv = h_in.rearrange("(n p) d -> p n d", p=128)
    ev = einfo.rearrange("(n p) e -> p n e", p=128)
    xov = xo.rearrange("(n p) d -> p n d", p=128)

    identf = ctx.sb("identf", [128, 128], F32)
    iota8 = ctx.sb("iota8", [128, 8], F32)
    cnt = ctx.sb("cnt", [8, 8], F32)
    myoff = ctx.sb("myoff", [1, 8], F32)
    myoffB = ctx.sb("myoffB", [128, 8], F32)
    cid1 = ctx.sb("cid1", [1, 1], F32)
    onesf1 = ctx.sb("onesf1", [1, 128], F32)
    xt = ctx.sb("xt", [128, 2, 1024], F32)
    ht = ctx.sb("ht", [128, 2, 1024], BF16)
    et = ctx.sb("et", [128, 2, 8], F32)
    g0b = ctx.sb("g0b", [128, 2, 1024], BF16)
    g1b = ctx.sb("g1b", [128, 2, 1024], BF16)
    gcol = ctx.sb("gcol", [128, 4], F32)
    gcoli = ctx.sb("gcoli", [128, 2], I32)
    oh_ = ctx.sb("ohc2", [128, 8], F32)
    scr8 = ctx.sb("scr8", [128, 8], F32)
    psq = ctx.ps("psqC", [128, 2, 512], F32)

    smc = ctx.sem("consts")
    smt = ctx.sem("tps")
    sma = ctx.sem("acts")
    smld = [ctx.sem(f"lx{j}") for j in range(2)]
    smg = [ctx.sem(f"gt{j}") for j in range(2)]
    smst = [ctx.sem(f"sx{j}") for j in range(2)]
    smxe = ctx.sem("ev2")

    ctx.dma("g", identf[:, :], c_ident.ap()[:, :], inc=[(smc, 16)])
    smc.tick(16)
    ctx.dma("g", iota8[:, :], c_iota8.ap()[:, :], inc=[(smc, 16)])
    smc.tick(16)
    smcy = ctx.sem("lx2")
    ctx.dma("y", cnt[:, :], counts_all[:, :], inc=[(smcy, 16)])
    smcy.tick(16)
    ctx.dma("y", cid1[:, :], cid[None, :], inc=[(smcy, 16)])
    smcy.tick(16)
    consts_ok = smc.n
    ctx.emit("v", lambda v: v.memset(onesf1[:, :], 1.0), mark=True)
    ones_ok = ctx._last_inc["v"]

    iota8c2 = ctx.sb("iota8c2", [8, 1], F32)
    rowsel = ctx.sb("rowsel", [8, 1], BF16)
    c_iotac2 = ctx.il0(np.arange(8, dtype=np.float32)[:, None], "iota8cc2")
    ctx.dma("g", iota8c2[:, :], c_iotac2.ap()[:, :], inc=[(smc, 16)])
    smc.tick(16)
    cst2 = smc.n
    ctx.emit("t", lambda t: t.matmul(psq[0:8, 0, 0:1], onesf1[0:1, 0:8],
                                     cid1[:, :], start=True, stop=True),
             wait=[(smc, cst2), (smcy, smcy.n), ones_ok], inc=[(smt, 1)])
    smt.tick()
    ctx.emit("v", lambda v: v.tensor_scalar(
        rowsel[:, :], iota8c2[:, :], psq[0:8, 0, 0:1], None, ALU.is_lt),
        wait=[(smt, smt.n)], inc=[(smt, 1)])
    smt.tick()
    rowself = ctx.sb("rowself", [8, 1], F32)
    ctx.emit("v", lambda v: v.tensor_copy(rowself[:, :], rowsel[:, :]),
             wait=[(smt, smt.n)], inc=[(smt, 1)])
    smt.tick()
    ctx.emit("t", lambda t: t.matmul(psq[0:1, 1, 0:8], rowself[:, :], cnt[:, :],
                                     start=True, stop=True),
             wait=[(smt, smt.n)], inc=[(smt, 1)])
    smt.tick()
    ctx.emit("v", lambda v: v.tensor_copy(myoff[:, :], psq[0:1, 1, 0:8]),
             wait=[(smt, smt.n)], inc=[(smt, 1)])
    smt.tick()
    ctx.emit("t", lambda t: t.matmul(psq[:, 0, 0:8], onesf1[:, :], myoff[:, :],
                                     start=True, stop=True),
             wait=[(smt, smt.n)], inc=[(smt, 1)])
    smt.tick()
    ctx.emit("v", lambda v: v.tensor_copy(myoffB[:, :], psq[:, 0, 0:8]),
             wait=[(smt, smt.n)], inc=[(sma, 1)])
    off_ok = sma.tick()

    for i in range(NTILE):
        slot = i % 2
        wl = [(smst[slot], smst[slot].n)] if i >= 2 else []
        ctx.dma("y", xt[:, slot, :], xv[:, i, :], wait=wl, inc=[(smld[slot], 16)])
        smld[slot].tick(16)
        ctx.dma("y", ht[:, slot, :], hv[:, i, :], inc=[(smld[slot], 16)])
        smld[slot].tick(16)
        ctx.dma("y", et[:, slot, :], ev[:, i, :], inc=[(smld[slot], 16)])
        ld_ok = smld[slot].tick(16)
        for kk in range(2):
            ctx.emit("v", lambda v, slot=slot, kk=kk: v.tensor_scalar(
                oh_[:, :], iota8[:, :], et[:, slot, kk:kk + 1], None,
                ALU.is_equal),
                wait=[(smld[slot], ld_ok), (sma, off_ok)], dep=True, mark=True)
            ctx.emit("v", lambda v: v.tensor_tensor(
                oh_[:, :], oh_[:, :], myoffB[:, :], ALU.mult),
                dep=True, mark=True)
            ctx.emit("v", lambda v, kk=kk: v.reduce_sum(
                gcol[:, kk:kk + 1], oh_[:, :], axis=AX.X), dep=True, mark=True)
            ctx.emit("v", lambda v, slot=slot, kk=kk: v.tensor_tensor(
                gcol[:, kk:kk + 1], gcol[:, kk:kk + 1],
                et[:, slot, 4 + kk:5 + kk], ALU.add), dep=True, mark=True)
            ctx.emit("v", lambda v, kk=kk: v.tensor_scalar(
                gcol[:, 2 + kk:3 + kk], gcol[:, kk:kk + 1], float(ECAP), None,
                ALU.is_ge), dep=True, mark=True)
            ctx.emit("v", lambda v, slot=slot, kk=kk: v.scalar_tensor_tensor(
                gcol[:, kk:kk + 1], et[:, slot, kk:kk + 1], float(EPAD),
                gcol[:, kk:kk + 1], ALU.mult, ALU.add), dep=True, mark=True)
            ctx.emit("v", lambda v, kk=kk: v.scalar_tensor_tensor(
                gcol[:, kk:kk + 1], gcol[:, 2 + kk:3 + kk], 1.0e7,
                gcol[:, kk:kk + 1], ALU.mult, ALU.add), dep=True, mark=True)
            ctx.emit("v", lambda v, kk=kk: v.tensor_copy(
                gcoli[:, kk:kk + 1], gcol[:, kk:kk + 1]), dep=True,
                inc=[(sma, 1)])
            gi_ok = sma.tick()
            gb = g0b if kk == 0 else g1b
            ctx.emit("v", lambda v, gb=gb, slot=slot: v.memset(
                gb[:, slot, :], 0.0), dep=True, inc=[(sma, 1)])
            gz = sma.tick()
            ctx.emit("g", lambda g, gb=gb, slot=slot, kk=kk: g.indirect_dma_start(
                gb[:, slot, :], None, out_all[:, :],
                IndirectOffsetOnAxis(gcoli[:, kk:kk + 1], 0),
                bounds_check=ctx.breg(g, NEXP * EPAD - 1), oob_is_err=False),
                wait=[(sma, gz)], inc=[(smg[slot], 16)])
            smg[slot].tick(16)
        g_ok = smg[slot].n
        ctx.emit("v", lambda v, slot=slot: v.tensor_tensor(
            xt[:, slot, :], xt[:, slot, :], ht[:, slot, :], ALU.add),
            wait=[(smg[slot], g_ok)], dep=True, mark=True)
        ctx.emit("v", lambda v, slot=slot: v.scalar_tensor_tensor(
            xt[:, slot, :], g0b[:, slot, :], et[:, slot, 2:3],
            xt[:, slot, :], ALU.mult, ALU.add), dep=True, mark=True)
        ctx.emit("v", lambda v, slot=slot: v.scalar_tensor_tensor(
            xt[:, slot, :], g1b[:, slot, :], et[:, slot, 3:4],
            xt[:, slot, :], ALU.mult, ALU.add), dep=True, inc=[(smxe, 1)])
        smxe.tick()
        ctx.dma("g", xov[:, i, :], xt[:, slot, :],
                wait=[(smxe, smxe.n)], inc=[(smst[slot], 16)])
        smst[slot].tick(16)

    ctx.emit("y", None, wait=[(smst[0], smst[0].n), (smst[1], smst[1].n)])
    ctx.barrier(all_sems=True)
    ctx.free_to(_m_all)


_NC_CACHE = {}


def _get(name, builder, *a):
    if name not in _NC_CACHE:
        _NC_CACHE[name] = builder(*a)
    return _NC_CACHE[name]


def _run(nc, ins):
    res = run_bass_kernel_spmd(nc, ins, core_ids=list(range(NCORE)))
    return res.results


def run_l0_attn_a(x_full, wq, wk, wv, wg, wo):
    nc = _get("l0aa", build_l0_attn_a)
    ins = [{"x": np.ascontiguousarray(x_full[c]), "wq": wq, "wk": wk,
            "wv": wv, "wg": wg, "wo": wo} for c in range(NCORE)]
    r = _run(nc, ins)
    return np.stack([r[c]["xo"] for c in range(NCORE)], 0)


def run_l0_ffn(x_full, Win, bin_, W1, b1, W2, b2, Wout, bout):
    nc = _get("l0ffn", build_l0_ffn)
    ins = [{"x": np.ascontiguousarray(x_full[c]), "Win": Win, "bin_": bin_,
            "W1": W1, "b1": b1, "W2": W2, "b2": b2, "Wout": Wout,
            "bout": bout} for c in range(NCORE)]
    r = _run(nc, ins)
    return np.stack([r[c]["xo"] for c in range(NCORE)], 0)


def run_l0_attn_b(x_full, W_qkvb, W_o, ln_g, ln_b):
    nc = _get("l0ab", build_l0_attn_b)
    ins = []
    for c in range(NCORE):
        xr = np.ascontiguousarray(
            x_full[:, c * 256:(c + 1) * 256, :].reshape(S, D))
        ins.append({"x": xr, "W_qkvb": W_qkvb, "W_o": W_o, "ln_g": ln_g,
                    "ln_b": ln_b})
    r = _run(nc, ins)
    out = np.empty_like(x_full)
    for c in range(NCORE):
        out[:, c * 256:(c + 1) * 256, :] = r[c]["xo"].reshape(B, 256, D)
    return out


def run_l1_attn(variant, x_full, **w):
    nc = _get("l1" + variant, build_l1_attn, variant)
    ins = [dict({"x": np.ascontiguousarray(x_full[c])}, **w)
           for c in range(NCORE)]
    r = _run(nc, ins)
    return np.stack([r[c]["xo"] for c in range(NCORE)], 0)


def run_moe(x_full, router_w, w1, w2):
    import ml_dtypes
    ncA = _get("moeA", build_moe_route)
    ins = [{"x": np.ascontiguousarray(x_full[c]), "rw": router_w,
            "cid": np.array([c], np.float32)} for c in range(NCORE)]
    rA = _run(ncA, ins)
    h_all = np.concatenate([rA[c]["h_out"] for c in range(NCORE)], 0)
    einfo = [rA[c]["einfo"] for c in range(NCORE)]
    counts = np.stack([np.asarray(rA[c]["counts"]).reshape(NEXP)
                       for c in range(NCORE)], 0).astype(np.float32)
    idxp = np.stack([rA[c]["idxp"] for c in range(NCORE)], 0)

    ncB = _get("moeB", build_moe_expert)
    ins = [{"h_all": h_all, "idxp_all": np.ascontiguousarray(idxp[:, c, :]),
            "counts_all": counts, "cid": np.array([c], np.float32),
            "w1": np.ascontiguousarray(w1[c]),
            "w2": np.ascontiguousarray(w2[c])} for c in range(NCORE)]
    rB = _run(ncB, ins)
    out_all = np.concatenate([rB[c]["out_d"] for c in range(NCORE)], 0)

    ncC = _get("moeC", build_moe_combine)
    ins = [{"x": np.ascontiguousarray(x_full[c]),
            "h_in": np.ascontiguousarray(h_all[c * S:(c + 1) * S]),
            "einfo": einfo[c], "counts_all": counts,
            "cid": np.array([c], np.float32), "out_all": out_all}
           for c in range(NCORE)]
    rC = _run(ncC, ins)
    return np.stack([rC[c]["xo"] for c in range(NCORE)], 0)


_FUSED = True


def kernel_fused(**inputs):
    f32 = lambda k: np.ascontiguousarray(np.asarray(inputs[k], np.float32))
    nc = _get("fused", build_fused)
    m = {
        "wq": f32("L0_attn_a_wq"), "wk": f32("L0_attn_a_wk"),
        "wv": f32("L0_attn_a_wv"), "wg": f32("L0_attn_a_wg"),
        "wo": f32("L0_attn_a_wo"),
        "Win": f32("L0_ffn_Win"), "bin_": f32("L0_ffn_bin_"),
        "W1": f32("L0_ffn_W1"), "b1": f32("L0_ffn_b1"),
        "W2": f32("L0_ffn_W2"), "b2": f32("L0_ffn_b2"),
        "Wout": f32("L0_ffn_Wout"), "bout": f32("L0_ffn_bout"),
        "W_qkvb": f32("L0_attn_b_W_qkvb"), "W_o": f32("L0_attn_b_W_o"),
        "ln_g": f32("L0_attn_b_ln_g"), "ln_b": f32("L0_attn_b_ln_b"),
        "Wq": f32("L1_attn_a_Wq"), "Wk": f32("L1_attn_a_Wk"),
        "Wv": f32("L1_attn_a_Wv"), "Wproj": f32("L1_attn_a_Wproj"),
        "bproj": f32("L1_attn_a_bproj"), "Wpre": f32("L1_attn_a_Wpre"),
        "bpre": f32("L1_attn_a_bpre"), "pi": f32("L1_attn_a_pi"),
        "W_attn": f32("L1_attn_b_W_attn"), "b_attn": f32("L1_attn_b_b_attn"),
        "W_proj": f32("L1_attn_b_W_proj"), "b_proj": f32("L1_attn_b_b_proj"),
        "rw": f32("L1_ffn_router_w"),
    }
    w1a = f32("L1_ffn_w1")
    w2a = f32("L1_ffn_w2")
    x = f32("L0_attn_a_x")
    ins = []
    for c in range(NCORE):
        d = dict(m)
        d["x0"] = np.ascontiguousarray(x[c])
        d["cid"] = np.array([c], np.float32)
        d["w1"] = np.ascontiguousarray(w1a[c])
        d["w2"] = np.ascontiguousarray(w2a[c])
        ins.append(d)
    r = _run(nc, ins)
    return np.stack([r[c]["xo"] for c in range(NCORE)], 0)


def kernel(**inputs):
    if _FUSED:
        return kernel_fused(**inputs)
    return kernel_multi(**inputs)


def kernel_multi(**inputs):
    f32 = lambda k: np.ascontiguousarray(np.asarray(inputs[k], np.float32))
    x = f32("L0_attn_a_x")
    x = run_l0_attn_a(x, f32("L0_attn_a_wq"), f32("L0_attn_a_wk"),
                      f32("L0_attn_a_wv"), f32("L0_attn_a_wg"),
                      f32("L0_attn_a_wo"))
    ffn_args = (f32("L0_ffn_Win"), f32("L0_ffn_bin_"), f32("L0_ffn_W1"),
                f32("L0_ffn_b1"), f32("L0_ffn_W2"), f32("L0_ffn_b2"),
                f32("L0_ffn_Wout"), f32("L0_ffn_bout"))
    x = run_l0_ffn(x, *ffn_args)
    x = run_l0_attn_b(x, f32("L0_attn_b_W_qkvb"), f32("L0_attn_b_W_o"),
                      f32("L0_attn_b_ln_g"), f32("L0_attn_b_ln_b"))
    x = run_l0_ffn(x, *ffn_args)
    x = run_l1_attn("a", x, Wq=f32("L1_attn_a_Wq"), Wk=f32("L1_attn_a_Wk"),
                    Wv=f32("L1_attn_a_Wv"), Wproj=f32("L1_attn_a_Wproj"),
                    bproj=f32("L1_attn_a_bproj"), Wpre=f32("L1_attn_a_Wpre"),
                    bpre=f32("L1_attn_a_bpre"), pi=f32("L1_attn_a_pi"))
    moe_args = (f32("L1_ffn_router_w"), f32("L1_ffn_w1"), f32("L1_ffn_w2"))
    x = run_moe(x, *moe_args)
    x = run_l1_attn("b", x, W_attn=f32("L1_attn_b_W_attn"),
                    b_attn=f32("L1_attn_b_b_attn"),
                    W_proj=f32("L1_attn_b_W_proj"),
                    b_proj=f32("L1_attn_b_b_proj"))
    x = run_moe(x, *moe_args)
    return x


def _ext_new(nc, specs):
    d = {}
    for n, s, dt_, kind in specs:
        d[n] = nc.dram_tensor(n, s, dt_, kind=kind).ap()
    return d


def build_l0_attn_a():
    nc = bass.Bass("TRN2", target_bir_lowering=False)
    ext = _ext_new(nc, [("x", [S, D], F32, "ExternalInput")] +
                   [(k, [D, D], F32, "ExternalInput")
                    for k in ("wq", "wk", "wv", "wg", "wo")] +
                   [("xo", [S, D], F32, "ExternalOutput")])
    ctx = Ctx(nc)
    emit_l0_attn_a(ctx, ext)
    ctx.finish(nc)
    return nc


def build_l0_ffn():
    nc = bass.Bass("TRN2", target_bir_lowering=False)
    ext = _ext_new(nc, [("x", [S, D], F32, "ExternalInput"),
                        ("Win", [D, 64], F32, "ExternalInput"),
                        ("bin_", [64], F32, "ExternalInput"),
                        ("W1", [64, 16], F32, "ExternalInput"),
                        ("b1", [64, 16], F32, "ExternalInput"),
                        ("W2", [64, 16], F32, "ExternalInput"),
                        ("b2", [64], F32, "ExternalInput"),
                        ("Wout", [64, D], F32, "ExternalInput"),
                        ("bout", [D], F32, "ExternalInput"),
                        ("xo", [S, D], F32, "ExternalOutput")])
    ctx = Ctx(nc)
    emit_l0_ffn(ctx, ext)
    ctx.finish(nc)
    return nc


def build_l1_attn(variant):
    nc = bass.Bass("TRN2", target_bir_lowering=False)
    if variant == "b":
        specs = [("x", [S, D], F32, "ExternalInput"),
                 ("W_attn", [D, 3 * D], F32, "ExternalInput"),
                 ("b_attn", [3 * D], F32, "ExternalInput"),
                 ("W_proj", [D, D], F32, "ExternalInput"),
                 ("b_proj", [D], F32, "ExternalInput")]
    else:
        specs = [("x", [S, D], F32, "ExternalInput")] + \
                [(k, [D, D], F32, "ExternalInput")
                 for k in ("Wq", "Wk", "Wv", "Wproj", "Wpre")] + \
                [("bproj", [D], F32, "ExternalInput"),
                 ("bpre", [D], F32, "ExternalInput"),
                 ("pi", [1], F32, "ExternalInput")]
    specs.append(("xo", [S, D], F32, "ExternalOutput"))
    ext = _ext_new(nc, specs)
    ctx = Ctx(nc)
    emit_l1_attn(ctx, ext, variant)
    ctx.finish(nc)
    return nc


def build_l0_attn_b():
    nc = bass.Bass("TRN2", target_bir_lowering=False)
    ext = _ext_new(nc, [("x", [S, D], F32, "ExternalInput"),
                        ("W_qkvb", [3088, D], F32, "ExternalInput"),
                        ("W_o", [D, D], F32, "ExternalInput"),
                        ("ln_g", [D], F32, "ExternalInput"),
                        ("ln_b", [D], F32, "ExternalInput"),
                        ("xo", [S, D], F32, "ExternalOutput")])
    ctx = Ctx(nc)
    emit_l0_attn_b(ctx, ext)
    ctx.finish(nc)
    return nc


def build_moe_route():
    nc = bass.Bass("TRN2", target_bir_lowering=False)
    ext = _ext_new(nc, [("x", [S, D], F32, "ExternalInput"),
                        ("rw", [NEXP, D], F32, "ExternalInput"),
                        ("cid", [1], F32, "ExternalInput"),
                        ("h_out", [S, D], BF16, "ExternalOutput"),
                        ("einfo", [S, 8], F32, "ExternalOutput"),
                        ("counts", [NEXP], F32, "ExternalOutput"),
                        ("idxp", [NEXP, EPAD + 4096], F32, "ExternalOutput")])
    ctx = Ctx(nc)
    emit_moe_route(ctx, ext)
    ctx.finish(nc)
    return nc


def build_moe_expert():
    nc = bass.Bass("TRN2", target_bir_lowering=False)
    ext = _ext_new(nc, [("h_all", [B * S, D], BF16, "ExternalInput"),
                        ("idxp_all", [NCORE, EPAD + 4096], F32, "ExternalInput"),
                        ("counts_all", [NCORE, NEXP], F32, "ExternalInput"),
                        ("cid", [1], F32, "ExternalInput"),
                        ("w1", [D, 4096], F32, "ExternalInput"),
                        ("w2", [4096, D], F32, "ExternalInput"),
                        ("out_d", [EPAD, D], BF16, "ExternalOutput")])
    ctx = Ctx(nc)
    emit_moe_expert(ctx, ext)
    ctx.finish(nc)
    return nc


def build_moe_combine():
    nc = bass.Bass("TRN2", target_bir_lowering=False)
    ext = _ext_new(nc, [("x", [S, D], F32, "ExternalInput"),
                        ("h_in", [S, D], BF16, "ExternalInput"),
                        ("einfo", [S, 8], F32, "ExternalInput"),
                        ("counts_all", [NCORE, NEXP], F32, "ExternalInput"),
                        ("cid", [1], F32, "ExternalInput"),
                        ("out_all", [NEXP * EPAD, D], BF16, "ExternalInput"),
                        ("xo", [S, D], F32, "ExternalOutput")])
    ctx = Ctx(nc)
    emit_moe_combine(ctx, ext)
    ctx.finish(nc)
    return nc


RG8 = [list(range(8))]


def emit_collective(ctx, kind, op, in_ap, out_ap, wait=()):
    sm = ctx.sem()
    ctx.emit("g", lambda g: g.collective_compute(
        kind, op, RG8, [in_ap], [out_ap]), wait=wait, inc=[(sm, 1)])
    sm.tick()
    return sm, sm.n


def emit_moe_fused(ctx, tag, x_in, x_out, W):
    nc = ctx.nc
    ctx.prefix = tag + "r_"
    h_loc = nc.dram_tensor(tag + "h_loc", [S, D], BF16, kind="Internal").ap()
    einfo_loc = nc.dram_tensor(tag + "einfo", [S, 8], F32, kind="Internal").ap()
    counts_loc = nc.dram_tensor(tag + "cnt", [NEXP], F32, kind="Internal").ap()
    idxp_loc = nc.dram_tensor(tag + "idxp", [NEXP, EPAD + 4096], F32,
                              kind="Internal").ap()
    h_all = nc.dram_tensor(tag + "h_all", [B * S, D], BF16, kind="Internal").ap()
    counts_all = nc.dram_tensor(tag + "cnt_all", [NCORE, NEXP], F32,
                                kind="Internal").ap()
    idxp_all = nc.dram_tensor(tag + "idxp_all", [NCORE * NEXP, EPAD + 4096],
                              F32, kind="Internal").ap()
    out_loc = nc.dram_tensor(tag + "out_loc", [EPAD, D], BF16,
                             kind="Internal").ap()
    out_all = nc.dram_tensor(tag + "out_all", [NEXP * EPAD, D], BF16,
                             kind="Internal").ap()
    def h_hook(waits):
        emit_collective(ctx, "AllGather", ALU.bypass, h_loc[:, :],
                        h_all[:, :], wait=waits)

    emit_moe_route(ctx, {"x": x_in, "rw": W["rw"], "cid": W["cid"],
                         "h_out": h_loc, "einfo": einfo_loc,
                         "counts": counts_loc, "idxp": idxp_loc,
                         "h_hook": h_hook})
    emit_collective(ctx, "AllGather", ALU.bypass, counts_loc[None, :],
                    counts_all[:, :])
    sm, smn = emit_collective(ctx, "AllGather", ALU.bypass, idxp_loc[:, :],
                              idxp_all[:, :])
    ctx.emit("g", None, wait=[(sm, smn)])
    ctx.barrier()
    ctx.prefix = tag + "e_"
    emit_moe_expert(ctx, {"h_all": h_all, "idxp_all": idxp_all,
                          "counts_all": counts_all, "cid": W["cid"],
                          "w1": W["w1"], "w2": W["w2"], "out_d": out_loc},
                    fused=True)
    emit_collective(ctx, "AllGather", ALU.bypass, out_loc[:, :], out_all[:, :])
    ctx.barrier()
    ctx.prefix = tag + "c_"
    emit_moe_combine(ctx, {"x": x_in, "h_in": h_loc, "einfo": einfo_loc,
                           "counts_all": counts_all, "cid": W["cid"],
                           "out_all": out_all, "xo": x_out})


def build_fused():
    nc = bass.Bass("TRN2", target_bir_lowering=False)
    E = {}
    E["x0"] = nc.dram_tensor("x0", [S, D], F32, kind="ExternalInput").ap()
    E["cid"] = nc.dram_tensor("cid", [1], F32, kind="ExternalInput").ap()
    names = [("wq", [D, D]), ("wk", [D, D]), ("wv", [D, D]), ("wg", [D, D]),
             ("wo", [D, D]),
             ("Win", [D, 64]), ("bin_", [64]), ("W1", [64, 16]),
             ("b1", [64, 16]), ("W2", [64, 16]), ("b2", [64]),
             ("Wout", [64, D]), ("bout", [D]),
             ("W_qkvb", [3088, D]), ("W_o", [D, D]), ("ln_g", [D]),
             ("ln_b", [D]),
             ("Wq", [D, D]), ("Wk", [D, D]), ("Wv", [D, D]),
             ("Wproj", [D, D]), ("bproj", [D]), ("Wpre", [D, D]),
             ("bpre", [D]), ("pi", [1]),
             ("W_attn", [D, 3 * D]), ("b_attn", [3 * D]),
             ("W_proj", [D, D]), ("b_proj", [D]),
             ("rw", [NEXP, D]), ("w1", [D, 4096]), ("w2", [4096, D])]
    for n, s in names:
        E[n] = nc.dram_tensor(n, s, F32, kind="ExternalInput").ap()
    xout = nc.dram_tensor("xo", [S, D], F32, kind="ExternalOutput").ap()
    xa = nc.dram_tensor("xa", [S, D], F32, kind="Internal").ap()
    xb = nc.dram_tensor("xb", [S, D], F32, kind="Internal").ap()
    xab = nc.dram_tensor("xab", [S, D], F32, kind="Internal").ap()
    xab2 = nc.dram_tensor("xab2", [S, D], F32, kind="Internal").ap()
    cidrow = nc.dram_tensor("cidrow", [1, 1], F32, kind="Internal").ap()

    ctx = Ctx(nc)

    ctx.prefix = "s1_"
    emit_l0_attn_a(ctx, {"x": E["x0"], "wq": E["wq"], "wk": E["wk"],
                         "wv": E["wv"], "wg": E["wg"], "wo": E["wo"],
                         "xo": xa})
    ctx.prefix = "s2_"
    emit_l0_ffn(ctx, {"x": xa, "Win": E["Win"], "bin_": E["bin_"],
                      "W1": E["W1"], "b1": E["b1"], "W2": E["W2"],
                      "b2": E["b2"], "Wout": E["Wout"], "bout": E["bout"],
                      "xo": xb})
    ctx.prefix = "s3_"
    sm, smn = emit_collective(ctx, "AllToAll", ALU.bypass, xb[:, :],
                              xab[:, :])
    ctx.emit("g", None, wait=[(sm, smn)])
    ctx.barrier()
    emit_l0_attn_b(ctx, {"x": xab, "W_qkvb": E["W_qkvb"], "W_o": E["W_o"],
                         "ln_g": E["ln_g"], "ln_b": E["ln_b"], "xo": xab2})
    sm2, smn2 = emit_collective(ctx, "AllToAll", ALU.bypass, xab2[:, :],
                                xa[:, :])
    ctx.emit("g", None, wait=[(sm2, smn2)])
    ctx.barrier()
    ctx.prefix = "s4_"
    emit_l0_ffn(ctx, {"x": xa, "Win": E["Win"], "bin_": E["bin_"],
                      "W1": E["W1"], "b1": E["b1"], "W2": E["W2"],
                      "b2": E["b2"], "Wout": E["Wout"], "bout": E["bout"],
                      "xo": xb})
    ctx.prefix = "s5_"
    emit_l1_attn(ctx, {"x": xb, "Wq": E["Wq"], "Wk": E["Wk"], "Wv": E["Wv"],
                       "Wproj": E["Wproj"], "bproj": E["bproj"],
                       "Wpre": E["Wpre"], "bpre": E["bpre"], "pi": E["pi"],
                       "xo": xa}, "a")
    emit_moe_fused(ctx, "m6_", xa, xb,
                   {"rw": E["rw"], "w1": E["w1"], "w2": E["w2"],
                    "cid": E["cid"]})
    ctx.prefix = "s7_"
    emit_l1_attn(ctx, {"x": xb, "W_attn": E["W_attn"], "b_attn": E["b_attn"],
                       "W_proj": E["W_proj"], "b_proj": E["b_proj"],
                       "xo": xa}, "b")
    emit_moe_fused(ctx, "m8_", xa, xout,
                   {"rw": E["rw"], "w1": E["w1"], "w2": E["w2"],
                    "cid": E["cid"]})
    ctx.finish(nc)
    return nc


# version 14 (final)
# speedup vs baseline: 1.0043x; 1.0043x over previous
import os
os.environ.setdefault("TRN_ALLOW_ICI_ALL_TO_ALL", "1")
import numpy as np

import concourse.bass as bass
import concourse.mybir as mybir
from concourse.bass_utils import run_bass_kernel_spmd
from concourse.bass_core_rust import IndirectOffsetOnAxis

F32 = mybir.dt.float32
BF16 = mybir.dt.bfloat16
I32 = mybir.dt.int32
AF = mybir.ActivationFunctionType
ALU = mybir.AluOpType
AX = mybir.AxisListType

NCORE = 8
B, S, D = 8, 2048, 1024
NH, DH = 16, 64
NTILE = S // 128
ENGS = ("t", "v", "s", "g", "y")


class Sem:
    __slots__ = ("h", "n", "name")

    def __init__(self, h, name):
        self.h, self.n, self.name = h, 0, name

    def tick(self, k=1):
        self.n += k
        return self.n


class Ctx:
    def __init__(self, nc):
        self.nc = nc
        self.tapes = {e: [] for e in ENGS}
        self.sems = {}
        self._sn = 0
        self._alloc = []

    prefix = ""
    scope = None

    def sb(self, name, shape, dtype):
        g = self.nc.sbuf_tensor(self.prefix + name, shape, dtype)
        h = g.__enter__()
        self._alloc.append(g)
        return h

    def ps(self, name, shape, dtype=F32):
        g = self.nc.psum_tensor(self.prefix + name, shape, dtype)
        h = g.__enter__()
        self._alloc.append(g)
        return h

    def il(self, name, arr):
        return self.nc.inline_tensor(np.ascontiguousarray(arr),
                                     self.prefix + name)

    def mark(self):
        return len(self._alloc)

    def free_to(self, m):
        while len(self._alloc) > m:
            self._alloc.pop().__exit__(None, None, None)

    def sem(self, name=None):
        if name is None:
            self._sn += 1
            name = f"_anon{self._sn % 8}"
        if name not in self.sems:
            self.sems[name] = Sem(self.nc.alloc_semaphore(name), name)
        return self.sems[name]

    def breg(self, g, val):
        if not hasattr(self, "_bregs"):
            self._bregs = {}
        if val not in self._bregs:
            r = g.alloc_register(f"bnd{len(self._bregs)}")
            g.reg_mov(r, val)
            self._bregs[val] = r
        return self._bregs[val]

    def emit(self, eng, fn, wait=(), inc=(), dep=False, mark=False):
        wait = list(wait)
        inc = list(inc)
        if not hasattr(self, "_last_inc"):
            self._last_inc = {}
        if dep and eng in self._last_inc:
            sm, val = self._last_inc[eng]
            wait.append((sm, val))
        if mark and not inc:
            s = self.sem("self_" + eng)
            inc.append((s, 1))
            s.tick()
            self._last_inc[eng] = (s, s.n)
        elif inc:
            sm, k = inc[-1]
            self._last_inc[eng] = (sm, sm.n + k)
        assert len(inc) <= 1, "one sem update per instruction"
        self.tapes[eng].append((fn, tuple(wait), tuple(inc),
                                self.scope or self.prefix))

    def dma(self, eng, out, in_, wait=(), inc=(), **kw):
        self.emit(eng, (lambda e, o=out, i=in_, kw=kw: e.dma_start(o, i, **kw)),
                  wait=wait, inc=inc)

    def barrier(self, engs=ENGS, all_sems=False):
        bar = self.sem("bar")
        li = getattr(self, "_last_inc", {})
        if all_sems:
            for sm in list(self.sems.values()):
                if sm.name != "bar" and sm.n > 0:
                    self.emit("y", None, wait=[(sm, sm.n)])
        for e in engs:
            w = [li[e]] if e in li else []
            self.emit(e, lambda en: en.nop(), wait=w, inc=[(bar, 1)])
        bar.tick(len(engs))
        tgt = bar.n
        for e in engs:
            self.emit(e, None, wait=[(bar, tgt)])
        for e in engs:
            self._last_inc[e] = (bar, tgt)

    def check(self):
        bad = []
        for e in ENGS:
            for idx, (fn, waits, incs, _tag) in enumerate(self.tapes[e]):
                for sm, val in waits:
                    if val > sm.n:
                        bad.append((e, idx, sm.name, val, sm.n))
        if bad:
            raise RuntimeError(f"unreachable waits: {bad[:10]}")

    def finish(self, nc):
        self.check()
        with nc.Block() as block:
            def mk(key):
                tape = self.tapes[key]

                def run(engine):
                    cur, sid = None, None
                    for fn, waits, incs, tag in tape:
                        if tag != cur:
                            if cur:
                                nc.leave_named_scope(cur, sid, False)
                            sid = (nc.enter_named_scope(tag, False)[0]
                                   if tag else None)
                            cur = tag
                        for sm, val in waits:
                            engine.wait_ge(sm.h, val)
                        if fn is None:
                            continue
                        ins = fn(engine)
                        for sm, k in incs:
                            ins = ins.then_inc(sm.h, k)
                    if cur:
                        nc.leave_named_scope(cur, sid, False)
                return run
            block.tensor(mk("t"))
            block.vector(mk("v"))
            block.scalar(mk("s"))
            block.gpsimd(mk("g"))
            block.sync(mk("y"))
        self.free_to(0)



def _ctx_il0(self, arr, name):
    return self.nc.inline_tensor(np.ascontiguousarray(arr), self.prefix + name)

Ctx.il0 = _ctx_il0


def np_ident():
    return np.eye(128, dtype=np.float32)


def np_rope():
    inv = 1.0 / (10000.0 ** (np.arange(0, DH, 2, dtype=np.float64) / DH))
    tpos = np.arange(S, dtype=np.float64)
    fr = tpos[None, :] * inv[:, None]
    emb = np.concatenate([fr, fr], 0)
    cos2 = np.concatenate([np.cos(emb), np.cos(emb)], 0).astype(np.float32)
    sin2 = np.concatenate([np.sin(emb), np.sin(emb)], 0).astype(np.float32)
    return cos2, sin2


def np_rotP():
    P = np.zeros((128, 128), np.float32)
    for hh in range(2):
        b0 = hh * 64
        for j in range(32):
            P[b0 + 32 + j, b0 + j] = -1.0
            P[b0 + j, b0 + 32 + j] = 1.0
    return P


def np_decay():
    gam = 1.0 - np.exp2(-5.0 - np.arange(NH, dtype=np.float64))
    ii = np.arange(512)[None, :]
    jj = np.arange(128)[:, None]
    Efull = np.stack([g ** (ii - jj + 128) for g in gam], 0)
    i2 = np.arange(128)[None, :]
    Ediag = np.stack([np.where(i2 >= jj, g ** (i2 - jj), 0.0) for g in gam], 0)
    return Efull.astype(np.float32), Ediag.astype(np.float32), gam


def emit_ln_tile(ctx, xt, st4, hslot, sm, pre_wait=()):
    scr = ctx._lnscr
    eps_ = ctx._eps
    ctx.emit("v", lambda v: v.scalar_tensor_tensor(
        scr[:, :], xt, 1.0, xt, ALU.mult, ALU.mult, accum_out=st4[:, 1:2]),
        wait=pre_wait)
    ctx.emit("v", lambda v: v.reduce_sum(st4[:, 0:1], xt, axis=AX.X),
             inc=[(sm, 1)])
    t0 = sm.tick()
    ctx.emit("s", lambda s: s.mul(st4[:, 2:3], st4[:, 0:1], 1.0 / 1024.0),
             wait=[(sm, t0)], mark=True)
    ctx.emit("s", lambda s: s.mul(st4[:, 0:1], st4[:, 1:2], 1.0 / 1024.0),
             dep=True, mark=True)
    ctx.emit("s", lambda s: s.activation(st4[:, 1:2], st4[:, 2:3], AF.Square),
             dep=True, inc=[(sm, 1)])
    t1 = sm.tick()
    ctx.emit("v", lambda v: v.tensor_tensor(st4[:, 0:1], st4[:, 0:1],
                                            st4[:, 1:2], ALU.subtract),
             wait=[(sm, t1)], inc=[(sm, 1)])
    t2 = sm.tick()
    ctx.emit("s", lambda s: s.activation(st4[:, 0:1], st4[:, 0:1], AF.Sqrt,
                                         bias=eps_[:, :]),
             wait=[(sm, t2)], inc=[(sm, 1)])
    t3 = sm.tick()
    ctx.emit("v", lambda v: v.reciprocal(st4[:, 3:4], st4[:, 0:1]),
             wait=[(sm, t3)], mark=True)
    ctx.emit("v", lambda v: v.tensor_scalar(
        hslot, xt, st4[:, 2:3], st4[:, 3:4], ALU.subtract, ALU.mult),
        dep=True, inc=[(sm, 1)])
    return sm.tick()


def emit_ln_transpose_stream(ctx, x_src_tile, hTs, name, identbf, ident_wait,
                             keep_h=None, keep_mrs=None):
    nc = ctx.nc
    ctx._lnscr = ctx.sb(name + "scr", [128, 1024], BF16)
    if not hasattr(ctx, "_eps") or ctx._eps is None:
        pass
    ctx._eps = getattr(ctx, "_eps_t", None)
    if ctx._eps is None:
        ctx._eps_t = ctx.sb(name + "eps", [128, 1], F32)
        ctx._eps = ctx._eps_t
        _e = ctx._eps_t
        ctx.emit("v", lambda v, _e=_e: v.memset(_e[:, :], 1e-5), mark=True)
    stats = ctx.sb(name + "stats", [128, NTILE, 4], F32)
    if keep_h is None:
        hring = ctx.sb(name + "hring", [128, 3, 1024], BF16)
    sm = ctx.sem(name + "ln")
    smtr = ctx.sem(name + "tr")
    smev = ctx.sem(name + "ev")
    pst = ctx.ps(name + "pst", [128, 2, 1024], BF16)
    for i in range(NTILE):
        xt, xw = x_src_tile(i)
        hslot = keep_h[:, i, :] if keep_h is not None else hring[:, i % 3, :]
        w = list(xw)
        if keep_h is None and i >= 3:
            w.append((smev, smev.n - 16))
        hn = emit_ln_tile(ctx, xt, stats[:, i, :], hslot, sm, pre_wait=w)
        if keep_mrs is not None:
            ctx.emit("v", lambda v, i=i: v.tensor_copy(
                keep_mrs[:, i, :], stats[:, i, 2:4]), dep=True, mark=True)
        for c in range(8):
            k = i * 8 + c
            par = k % 2
            w2 = [(sm, hn)] if c == 0 else []
            if k == 0:
                w2 = w2 + list(ident_wait)
            if k >= 2:
                w2.append((smev, smev.n - 1))
            ctx.emit("t", lambda t, hslot=hslot, c=c, par=par: t.transpose(
                pst[:, par, 0:128], hslot[:, c * 128:(c + 1) * 128], identbf[:, :]),
                wait=w2, inc=[(smtr, 1)])
            smtr.tick()
            ctx.emit("s", lambda s, i=i, c=c, par=par: s.copy(
                hTs[:, c, i * 128:(i + 1) * 128], pst[:, par, 0:128]),
                wait=[(smtr, smtr.n)], inc=[(smev, 1)])
            smev.tick()
    return smev


def emit_wT_load(ctx, w_bf_dram, wT, name, wait=()):
    sm = ctx.sem("wl_" + name[-1])
    for mc in range(D // 128):
        ctx.emit("y", lambda y, mc=mc: y.dma_start(
            wT[:, mc, :], w_bf_dram[:, mc * 128:(mc + 1) * 128], transpose=True),
            wait=wait if mc == 0 else (), inc=[(sm, 16)])
        sm.tick(16)
    return sm


def emit_l0_attn_a(ctx, ext):
    nc = ctx.nc
    ctx._eps_t = None
    _m_all = ctx.mark()
    x = ext["x"]
    win = {k: ext[k] for k in ("wq", "wk", "wv", "wg", "wo")}
    xo = ext["xo"]
    wbf = {k: nc.dram_tensor(ctx.prefix + k + "_bf", [D, D], BF16, kind="Internal").ap()
           for k in win}

    cos2, sin2 = np_rope()
    Efull, Ediag, gam = np_decay()
    c_ident = ctx.il0(np_ident(), "identsrc")
    c_cos = ctx.il0(cos2, "ropecos")
    c_sin = ctx.il0(sin2, "ropesin")
    c_rot = ctx.il0(np_rotP(), "rotP")
    c_ef = ctx.il0(Efull, "Efull")
    c_ed = ctx.il0(Ediag, "Ediag")

    xv = x.rearrange("(n p) d -> p n d", p=128)
    xov = xo.rearrange("(n p) d -> p n d", p=128)

    qT = ctx.sb("qT", [128, 8, S], BF16)
    kT = ctx.sb("kT", [128, 8, S], BF16)
    vtm = ctx.sb("vtm", [128, NTILE, 1024], BF16)
    outT = ctx.sb("outT", [128, 8, S], BF16)
    identbf = ctx.sb("identbf", [128, 128], BF16)
    hTs = ctx.sb("hTs", [128, 8, S], BF16)

    smc = ctx.sem("consts")
    ctx.dma("g", identbf[:, :], c_ident.ap()[:, :], inc=[(smc, 16)])
    ident_ok = smc.tick(16)
    smw = ctx.sem("gc_")
    wcast = {}
    for k in win:
        ctx.dma("g", wbf[k][:, :], win[k][:, :], inc=[(smw, 16)])
        smw.tick(16)
    for k in win:
        wcast[k] = smw.n

    m0 = ctx.mark()
    xring = ctx.sb("xring", [128, 3, 1024], F32)
    smx3 = [ctx.sem(f"lx{j}") for j in range(3)]

    def xsrc(i):
        slot = i % 3
        smx = smx3[slot]
        w = []
        if i >= 3:
            w.append((ctx.sems["lnln"], ctx.sems["lnln"].n - 10))
        ctx.dma("y", xring[:, slot, :], xv[:, i, :], wait=w, inc=[(smx, 16)])
        smx.tick(16)
        return xring[:, slot, :], [(smx, smx.n)]

    trsem = emit_ln_transpose_stream(ctx, xsrc, hTs, "ln", identbf,
                                     [(smc, ident_ok)])
    hT_done = trsem.n
    ctx.barrier()
    ctx.free_to(m0)

    m1 = ctx.mark()
    wT = ctx.sb("wT", [128, 8, D], BF16)
    cosS = ctx.sb("cosS", [128, S], BF16)
    sinS = ctx.sb("sinS", [128, S], BF16)
    rotP = ctx.sb("rotPs", [128, 128], BF16)
    qsb = ctx.sb("ropeqsb", [128, 512], BF16)
    t1f = ctx.sb("ropet1", [128, 512], F32)
    psq = ctx.ps("psq", [128, 2, 512], F32)
    psrot = ctx.ps("psrot", [128, 512], F32)
    ctx.dma("g", cosS[:, :], c_cos.ap()[:, :], inc=[(smc, 16)])
    smc.tick(16)
    ctx.dma("g", sinS[:, :], c_sin.ap()[:, :], inc=[(smc, 16)])
    smc.tick(16)
    ctx.dma("g", rotP[:, :], c_rot.ap()[:, :], inc=[(smc, 16)])
    consts2 = smc.tick(16)

    smmm = ctx.sem("mms")
    smev = ctx.sem("evs")

    def fm_proj(wname, outbuf, mode):
        wl = emit_wT_load(ctx, wbf[wname], wT, wname,
                          wait=[(smw, wcast[wname]), (smev, smev.n)])
        wld = wl.n
        for oc in range(8):
            for tg in range(4):
                it = oc * 4 + tg
                par = it % 2
                w2 = [(wl, wld)]
                if it == 0:
                    w2.append((trsem, hT_done))
                if it >= 2:
                    w2.append((smev, smev.n - 1))
                for mc in range(8):
                    ctx.emit("t", lambda t, oc=oc, tg=tg, mc=mc, par=par: t.matmul(
                        psq[:, par, :], wT[:, mc, oc * 128:(oc + 1) * 128],
                        hTs[:, mc, tg * 512:(tg + 1) * 512],
                        start=(mc == 0), stop=(mc == 7)),
                        wait=w2 if mc == 0 else (),
                        inc=[(smmm, 1)] if mc == 7 else [])
                mmw = smmm.tick()
                if mode == "rope":
                    ctx.emit("v", lambda v, tg=tg, par=par: v.tensor_tensor(
                        t1f[:, :], psq[:, par, :], cosS[:, tg * 512:(tg + 1) * 512],
                        ALU.mult), wait=[(smmm, mmw), (smc, consts2)],
                        dep=True, mark=True)
                    ctx.emit("v", lambda v, par=par: v.tensor_copy(
                        qsb[:, :], psq[:, par, :]), dep=True, mark=True,
                        inc=[(smev, 1)])
                    qsb_ready = smev.tick()
                    ctx.emit("t", lambda t: t.matmul(
                        psrot[:, :], rotP[:, :], qsb[:, :], start=True, stop=True),
                        wait=[(smev, qsb_ready), (smc, consts2)], inc=[(smmm, 1)])
                    rot_done = smmm.tick()
                    ctx.emit("v", lambda v, tg=tg: v.tensor_tensor(
                        psrot[:, :], psrot[:, :], sinS[:, tg * 512:(tg + 1) * 512],
                        ALU.mult), wait=[(smmm, rot_done)], dep=True, mark=True)
                    ctx.emit("v", lambda v, oc=oc, tg=tg: v.tensor_tensor(
                        outbuf[:, oc, tg * 512:(tg + 1) * 512], t1f[:, :],
                        psrot[:, :], ALU.add), dep=True, mark=True,
                        inc=[(smev, 1)])
                    smev.tick()
                else:
                    ctx.emit("s", lambda s, oc=oc, tg=tg, par=par: s.copy(
                        outbuf[:, oc, tg * 512:(tg + 1) * 512], psq[:, par, :]),
                        wait=[(smmm, mmw)], inc=[(smev, 1)])
                    smev.tick()

    fm_proj("wq", qT, "rope")
    fm_proj("wk", kT, "rope")

    wlv = emit_wT_load(ctx, wbf["wv"], wT, "wv",
                       wait=[(smw, wcast["wv"]), (smev, smev.n)])
    wlvd = wlv.n
    for i in range(NTILE):
        for og in range(2):
            it = i * 2 + og
            par = it % 2
            w2 = [(wlv, wlvd)]
            if it >= 2:
                w2.append((smev, smev.n - 1))
            for mc in range(8):
                ctx.emit("t", lambda t, i=i, og=og, mc=mc, par=par: t.matmul(
                    psq[:, par, :], hTs[:, mc, i * 128:(i + 1) * 128],
                    wT[:, mc, og * 512:(og + 1) * 512],
                    start=(mc == 0), stop=(mc == 7)),
                    wait=w2 if mc == 0 else (),
                    inc=[(smmm, 1)] if mc == 7 else [])
            mmw = smmm.tick()
            ctx.emit("s", lambda s, i=i, og=og, par=par: s.copy(
                vtm[:, i, og * 512:(og + 1) * 512], psq[:, par, :]),
                wait=[(smmm, mmw)], inc=[(smev, 1)])
            smev.tick()

    ctx.barrier()
    ctx.free_to(m1)

    m2 = ctx.mark()
    Efs = ctx.sb("Efs", [128, NH, 512], BF16)
    Eds = ctx.sb("Eds", [128, NH, 128], BF16)
    pT = ctx.sb("pT", [128, 4, 512], BF16)
    psS = ctx.ps("psS", [128, 4, 512], F32)
    psO = ctx.ps("psO", [128, 2, 512], F32)
    ctx.dma("g", Efs[:, :, :], c_ef.ap().rearrange("h p i -> p h i"),
            inc=[(smc, 16)])
    smc.tick(16)
    ctx.dma("g", Eds[:, :, :], c_ed.ap().rearrange("h p i -> p h i"),
            inc=[(smc, 16)])
    dec_done = smc.tick(16)

    smrS = ctx.sem("mms")
    smrE = ctx.sem("evs")
    smrO = ctx.sem("mm2")
    smrG = ctx.sem("ev2")
    inv8 = 1.0 / 8.0
    e0r = smrE.n

    for sg in range(4):
        for h in range(NH):
            c = h // 2
            rbase = (h % 2) * 64
            po = psO[rbase:rbase + 64, h % 2, :]
            nch = 4 * sg + 4
            head_seq = sg * NH + h
            po_guard = [(smrG, smrG.n - 1)] if head_seq >= 2 else []
            pend = []
            for tc in range(nch):
                par = smrE.n % 4
                w2 = [(smc, dec_done)]
                if smrE.n - e0r >= 4:
                    w2.append((smrE, smrE.n - 3))
                if len(pend) == 2:
                    pend.pop(0)()
                ctx.emit("t", lambda t, c=c, rbase=rbase, tc=tc, sg=sg, par=par: t.matmul(
                    psS[:, par, :], kT[rbase:rbase + 64, c, tc * 128:(tc + 1) * 128],
                    qT[rbase:rbase + 64, c, sg * 512:(sg + 1) * 512],
                    start=True, stop=True),
                    wait=w2, inc=[(smrS, 1)])
                sS = smrS.tick()
                mdel = 4 * sg - tc
                if mdel >= 1:
                    ctx.emit("v", lambda v, h=h, par=par, mdel=mdel: v.scalar_tensor_tensor(
                        pT[:, par, :], psS[:, par, :],
                        float(inv8 * (gam[h] ** (128 * (mdel - 1)))),
                        Efs[:, h, :], ALU.mult, ALU.mult),
                        wait=[(smrS, sS)], inc=[(smrE, 1)])
                    smrE.tick()
                else:
                    cl = tc - 4 * sg
                    first = True
                    if cl > 0:
                        ctx.emit("v", lambda v, par=par, cl=cl: v.memset(
                            pT[:, par, 0:cl * 128], 0.0), wait=[(smrS, sS)],
                            mark=True)
                        first = False
                    if cl < 3:
                        wdt = 512 - 128 * (cl + 1)
                        ctx.emit("v", lambda v, h=h, par=par, cl=cl, wdt=wdt: v.scalar_tensor_tensor(
                            pT[:, par, (cl + 1) * 128:512],
                            psS[:, par, (cl + 1) * 128:512],
                            inv8, Efs[:, h, 0:wdt], ALU.mult, ALU.mult),
                            wait=[(smrS, sS)] if first else (),
                            dep=not first, mark=True)
                        first = False
                    ctx.emit("v", lambda v, h=h, par=par, cl=cl: v.scalar_tensor_tensor(
                        pT[:, par, cl * 128:(cl + 1) * 128],
                        psS[:, par, cl * 128:(cl + 1) * 128],
                        inv8, Eds[:, h, :], ALU.mult, ALU.mult),
                        wait=[(smrS, sS)] if first else (),
                        dep=not first, inc=[(smrE, 1)])
                    smrE.tick()
                eE = smrE.n
                pend.append(lambda tc=tc, h=h, par=par, po=po, eE=eE,
                            fp=(tc == 0), last=(tc == nch - 1),
                            pg=(po_guard if tc == 0 else []): ctx.emit(
                    "t", lambda t: t.matmul(
                        po, vtm[:, tc, h * 64:(h + 1) * 64], pT[:, par, :],
                        start=fp, stop=last),
                    wait=[(smrE, eE)] + pg,
                    inc=[(smrO, 1)] if last else []))
            for f in pend:
                f()
            oO = smrO.tick()
            ctx.emit("v", lambda v, c=c, rbase=rbase, sg=sg, po=po: v.tensor_copy(
                outT[rbase:rbase + 64, c, sg * 512:(sg + 1) * 512], po),
                wait=[(smrO, oO)], inc=[(smrG, 1)])
            smrG.tick()

    ctx.barrier()
    ctx.free_to(m2)

    m3 = ctx.mark()
    wT2 = ctx.sb("wT2", [128, 8, D], BF16)
    gsl = ctx.sb("gsl", [128, 2, 512], BF16)
    psq2 = ctx.ps("psq2", [128, 2, 512], F32)
    wlg = emit_wT_load(ctx, wbf["wg"], wT2, "wg", wait=[(smw, wcast["wg"])])
    wlgd = wlg.n
    smgm = ctx.sem("mm2")
    smga = ctx.sem("acts")
    smge = ctx.sem("ev2")
    for oc in range(8):
        for tg in range(4):
            it = oc * 4 + tg
            par = it % 2
            w2 = [(wlg, wlgd)]
            if it >= 2:
                w2.append((smga, smga.n - 1))
            for mc in range(8):
                ctx.emit("t", lambda t, oc=oc, tg=tg, mc=mc, par=par: t.matmul(
                    psq2[:, par, :], wT2[:, mc, oc * 128:(oc + 1) * 128],
                    hTs[:, mc, tg * 512:(tg + 1) * 512],
                    start=(mc == 0), stop=(mc == 7)),
                    wait=w2 if mc == 0 else (),
                    inc=[(smgm, 1)] if mc == 7 else [])
            mmw = smgm.tick()
            wga = [(smgm, mmw)]
            if it >= 2:
                wga.append((smge, smge.n - 1))
            ctx.emit("s", lambda s, par=par: s.activation(
                gsl[:, par, :], psq2[:, par, :], AF.Silu),
                wait=wga, inc=[(smga, 1)])
            act_done = smga.tick()
            ctx.emit("v", lambda v, oc=oc, tg=tg, par=par: v.tensor_tensor(
                outT[:, oc, tg * 512:(tg + 1) * 512],
                outT[:, oc, tg * 512:(tg + 1) * 512],
                gsl[:, par, :], ALU.mult),
                wait=[(smga, act_done)], inc=[(smge, 1)])
            smge.tick()
    ctx.barrier()
    ctx.free_to(m3)

    m4 = ctx.mark()
    woT = ctx.sb("woT", [128, 8, D], BF16)
    xin = ctx.sb("xin", [128, 2, 1024], F32)
    psX = ctx.ps("psX", [128, 2, 512], F32)
    wlo = emit_wT_load(ctx, wbf["wo"], woT, "wo", wait=[(smw, wcast["wo"])])
    wlod = wlo.n
    smXm = ctx.sem("mm2")
    smXe = ctx.sem("ev2")
    smxl2 = [ctx.sem(f"lx{j}") for j in range(2)]
    smxs2 = [ctx.sem(f"sx{j}") for j in range(2)]
    for gt in range(NTILE):
        slot = gt % 2
        smxl = smxl2[slot]
        smxs = smxs2[slot]
        wx = [(smxs, smxs.n)] if gt >= 2 else []
        ctx.dma("y", xin[:, slot, :], xv[:, gt, :], wait=wx, inc=[(smxl, 16)])
        xl = smxl.tick(16)
        for oh in range(2):
            it2 = gt * 2 + oh
            par = it2 % 2
            w2 = [(wlo, wlod)]
            if it2 >= 2:
                w2.append((smXe, smXe.n - 1))
            for c2 in range(8):
                ctx.emit("t", lambda t, gt=gt, oh=oh, c2=c2, par=par: t.matmul(
                    psX[:, par, :], outT[:, c2, gt * 128:(gt + 1) * 128],
                    woT[:, c2, oh * 512:(oh + 1) * 512],
                    start=(c2 == 0), stop=(c2 == 7)),
                    wait=w2 if c2 == 0 else (),
                    inc=[(smXm, 1)] if c2 == 7 else [])
            mmw = smXm.tick()
            ctx.emit("v", lambda v, slot=slot, oh=oh, par=par: v.tensor_tensor(
                xin[:, slot, oh * 512:(oh + 1) * 512],
                xin[:, slot, oh * 512:(oh + 1) * 512],
                psX[:, par, :], ALU.add),
                wait=[(smXm, mmw), (smxl, xl)], inc=[(smXe, 1)])
            smXe.tick()
        ctx.dma("g", xov[:, gt, :], xin[:, slot, :],
                wait=[(smXe, smXe.n)], inc=[(smxs, 16)])
        smxs.tick(16)

    ctx.emit("y", None, wait=[(smxs2[0], smxs2[0].n), (smxs2[1], smxs2[1].n)])
    ctx.barrier(all_sems=True)
    ctx.free_to(_m_all)


def emit_l0_ffn(ctx, ext):
    nc = ctx.nc
    ctx._eps_t = None
    _m_all = ctx.mark()
    x = ext["x"]
    Win = ext["Win"]
    bin_ = ext["bin_"]
    W1 = ext["W1"]
    b1 = ext["b1"]
    W2 = ext["W2"]
    b2 = ext["b2"]
    Wout = ext["Wout"]
    bout = ext["bout"]
    xo = ext["xo"]

    c_ident = ctx.il0(np_ident(), "identsrc")

    xv = x.rearrange("(n p) d -> p n d", p=128)
    xov = xo.rearrange("(n p) d -> p n d", p=128)

    xs = ctx.sb("xs", [128, NTILE, 1024], F32)
    hTs = ctx.sb("hTs", [128, 8, S], BF16)
    identbf = ctx.sb("identbf", [128, 128], BF16)
    WinS = ctx.sb("WinS", [128, 8, 64], BF16)
    WoutS = ctx.sb("WoutS", [64, D], BF16)
    binbf = ctx.sb("binbf", [1, 2, 64], BF16)
    boutbf = ctx.sb("boutbf", [1, 2, D], BF16)
    onesbf = ctx.sb("onesbf", [1, 128], BF16)
    onesf = ctx.sb("onesf", [1, 128], F32)
    W1b = ctx.sb("W1b", [128, 1024], F32)
    b1b = ctx.sb("b1b", [128, 1024], F32)
    W2b = ctx.sb("W2b", [128, 1024], F32)
    b2b = ctx.sb("b2b", [128, 64], F32)
    m_pre = ctx.mark()
    psb = ctx.ps("psb", [128, 1024], F32)

    smc = ctx.sem("consts")
    smst = ctx.sem("t2v")
    smb = ctx.sem("bch")
    smbr = ctx.sem("bc2")

    ctx.dma("g", identbf[:, :], c_ident.ap()[:, :], inc=[(smc, 16)])
    smc.tick(16)
    ctx.dma("g", WinS[:, :, :], Win.rearrange("(c p) m -> p c m", p=128),
            inc=[(smc, 16)])
    smc.tick(16)
    ctx.dma("g", WoutS[:, :], Wout[:, :], inc=[(smc, 16)])
    consts_done = smc.tick(16)

    ctx.emit("v", lambda v: v.memset(onesbf[:, :], 1.0))
    ctx.emit("v", lambda v: v.memset(onesf[:, :], 1.0), inc=[(smst, 1)])
    ones_done = smst.tick()

    tmpb = ctx.sb("tmpb", [1, D], F32)

    def hi_lo(row_ap, dst_bf, width):
        ctx.dma("y", tmpb[:, 0:width], row_ap, wait=[(smb, smb.n)],
                inc=[(smb, 16)])
        w0 = smb.tick(16)
        ctx.emit("v", lambda v: v.tensor_copy(dst_bf[:, 0, 0:width],
                                              tmpb[:, 0:width]),
                 wait=[(smb, w0)], mark=True)
        ctx.emit("v", lambda v: v.tensor_tensor(
            tmpb[:, 0:width], tmpb[:, 0:width], dst_bf[:, 0, 0:width],
            ALU.subtract), dep=True, mark=True)
        ctx.emit("v", lambda v: v.tensor_copy(dst_bf[:, 1, 0:width],
                                              tmpb[:, 0:width]),
                 dep=True, inc=[(smb, 1)])
        smb.tick()

    hi_lo(bin_[None, :], binbf, 64)
    hi_lo(bout[None, :], boutbf, D)
    bias_done = smb.n

    rbf = ctx.sb("rbf", [1, 1024], F32)

    def bcast(src_flat, dst, width):
        ctx.dma("y", rbf[:, 0:width], src_flat, wait=[(smbr, smbr.n)],
                inc=[(smbr, 16)])
        w0 = smbr.tick(16)
        for h0 in range(0, width, 512):
            h1 = min(h0 + 512, width)
            ctx.emit("t", lambda t, h0=h0, h1=h1: t.matmul(
                psb[:, h0:h1], onesf[:, :], rbf[:, h0:h1], start=True, stop=True),
                wait=[(smbr, w0), (smst, ones_done)] if h0 == 0 else (),
                inc=[(smbr, 1)] if h1 == width else [])
        w1 = smbr.tick()
        ctx.emit("v", lambda v, width=width: v.tensor_copy(
            dst[:, 0:width], psb[:, 0:width]),
            wait=[(smbr, w1)], inc=[(smbr, 1)])
        smbr.tick()

    bcast(W1.rearrange("m i -> (m i)")[None, :], W1b, 1024)
    bcast(b1.rearrange("m i -> (m i)")[None, :], b1b, 1024)
    bcast(W2.rearrange("m i -> (m i)")[None, :], W2b, 1024)
    bcast(b2[None, :], b2b, 64)
    bc_done = smbr.n

    smx = ctx.sem("lx0")
    ctx.dma("y", xs[:, :, :], xv, inc=[(smx, 16)])
    xs_ok = smx.tick(16)

    trsem = emit_ln_transpose_stream(
        ctx, lambda i: (xs[:, i, :], [(smx, xs_ok)]), hTs, "ln", identbf,
        [(smc, consts_done)], keep_h=None)
    hT_done = trsem.n
    ctx.barrier()
    ctx.free_to(m_pre)

    pss = ctx.ps("pss", [128, 2, 512], F32)
    psx = ctx.ps("psx", [128, 2, 512], F32)
    pstr = ctx.ps("pstr", [128, 2, 1024], BF16)
    sV = ctx.sb("sV", [128, 2, 64], F32)
    t1 = ctx.sb("t1", [128, 1024], F32)
    t2 = ctx.sb("t2", [128, 1024], F32)
    gVb = ctx.sb("gVb", [128, 2, 64], BF16)
    gTt = ctx.sb("gTt", [64, 2, 128], BF16)

    smm = ctx.sem("mms")
    sme = ctx.sem("evs")
    smg = ctx.sem("ev2")
    smt = ctx.sem("t2s")
    smtv = ctx.sem("t2v")
    smo = ctx.sem("mm2")
    smxe = ctx.sem("ev3")
    smxs = ctx.sem("xst")

    for i in range(NTILE):
        par = i % 2
        w2 = [(smc, consts_done), (smb, bias_done), (trsem, hT_done)]
        if i >= 2:
            w2.append((smg, smg.n - 1))
        for mc in range(8):
            ctx.emit("t", lambda t, i=i, mc=mc, par=par: t.matmul(
                pss[:, par, 0:64], hTs[:, mc, i * 128:(i + 1) * 128],
                WinS[:, mc, :], start=(mc == 0), stop=False),
                wait=w2 if mc == 0 else ())
        ctx.emit("t", lambda t, par=par: t.matmul(
            pss[:, par, 0:64], onesbf[:, :], binbf[:, 0, :],
            start=False, stop=False))
        ctx.emit("t", lambda t, par=par: t.matmul(
            pss[:, par, 0:64], onesbf[:, :], binbf[:, 1, :],
            start=False, stop=True), inc=[(smm, 1)])
        mmw = smm.tick()
        ctx.emit("v", lambda v, par=par: v.tensor_copy(
            sV[:, par, :], pss[:, par, 0:64]), wait=[(smm, mmw)],
            inc=[(sme, 1)])
        se = sme.tick()
        ctx.emit("v", lambda v, par=par: v.tensor_tensor(
            t1.rearrange("p (m i) -> p m i", i=16),
            sV[:, par, :].unsqueeze(2).broadcast_to([128, 64, 16]),
            W1b.rearrange("p (m i) -> p m i", i=16), ALU.mult),
            wait=[(sme, se), (smbr, bc_done)], mark=True)
        ctx.emit("v", lambda v: v.tensor_tensor(t2[:, :], t1[:, :], b1b[:, :],
                                                ALU.add), dep=True,
                 inc=[(smtv, 1)])
        tw = smtv.tick()
        ctx.emit("s", lambda s: s.activation(t2[:, :], t2[:, :], AF.Relu),
                 wait=[(smtv, tw)], inc=[(smt, 1)])
        st = smt.tick()
        ctx.emit("v", lambda v: v.tensor_tensor(t1[:, :], t2[:, :], W2b[:, :],
                                                ALU.mult), wait=[(smt, st)],
                 mark=True)
        ctx.emit("v", lambda v, par=par: v.reduce_sum(
            sV[:, par, :], t1.rearrange("p (m i) -> p m i", i=16), axis=AX.X),
            dep=True, mark=True)
        ctx.emit("v", lambda v, par=par: v.tensor_tensor(
            sV[:, par, :], sV[:, par, :], b2b[:, :], ALU.add),
            dep=True, mark=True)
        ctx.emit("v", lambda v, par=par: v.tensor_copy(
            gVb[:, par, :], sV[:, par, :]), dep=True,
            wait=[(smm, smm.n - 3)] if i >= 2 else (), inc=[(smg, 1)])
        sg_ = smg.tick()
        ctx.emit("t", lambda t, par=par: t.transpose(
            pstr[0:64, par, 0:128], gVb[:, par, :], identbf[:, :]),
            wait=[(smg, sg_)] + ([(smt, smt.n - 2)] if i >= 2 else []),
            inc=[(smm, 1)])
        mm2 = smm.tick()
        ctx.emit("s", lambda s, par=par: s.copy(gTt[:, par, :],
                                                pstr[0:64, par, 0:128]),
                 wait=[(smm, mm2)], inc=[(smt, 1)])
        gt_done = smt.tick()
        for oh in range(2):
            pxp = (2 * i + oh) % 2
            wpx = [(smt, gt_done)] if oh == 0 else []
            if 2 * i + oh - 1 >= 1:
                wpx.append((smxe, smxe.n - 1))
            ctx.emit("t", lambda t, pxp=pxp, par=par, oh=oh: t.matmul(
                psx[:, pxp, :], gTt[:, par, :],
                WoutS[:, oh * 512:(oh + 1) * 512], start=True, stop=False),
                wait=wpx)
            ctx.emit("t", lambda t, pxp=pxp, oh=oh: t.matmul(
                psx[:, pxp, :], onesbf[:, :], boutbf[:, 0, oh * 512:(oh + 1) * 512],
                start=False, stop=False))
            ctx.emit("t", lambda t, pxp=pxp, oh=oh: t.matmul(
                psx[:, pxp, :], onesbf[:, :], boutbf[:, 1, oh * 512:(oh + 1) * 512],
                start=False, stop=True), inc=[(smo, 1)])
            so = smo.tick()
            ctx.emit("v", lambda v, i=i, oh=oh, pxp=pxp: v.tensor_tensor(
                xs[:, i, oh * 512:(oh + 1) * 512],
                xs[:, i, oh * 512:(oh + 1) * 512],
                psx[:, pxp, :], ALU.add), wait=[(smo, so)],
                inc=[(smxe, 1)])
            smxe.tick()
        xe = smxe.n
        ctx.dma("g", xov[:, i, :], xs[:, i, :], wait=[(smxe, xe)],
                inc=[(smxs, 16)])
        smxs.tick(16)

    ctx.emit("y", None, wait=[(smxs, smxs.n)])
    ctx.barrier(all_sems=True)
    ctx.free_to(_m_all)




def np_causal_masks():
    n_ = np.arange(128)[:, None]
    m_ = np.arange(512)[None, :]
    M1 = np.stack([np.where(m_ > n_ + 128 * j, -1e9, 0.0) for j in range(4)], 0)
    N2 = np.stack([np.where(m_ < n_ + 128 * j * 0 + 0, 0.0, 0.0) for j in range(1)], 0)
    M2 = np.stack([np.where(m_ - 128 * j < n_, -1e9, 0.0).T * 0.0 for j in range(1)], 0)
    return M1.astype(np.float32)


def emit_l1_attn(ctx, ext, variant):
    nc = ctx.nc
    ctx._eps_t = None
    _m_all = ctx.mark()
    x = ext["x"]
    if variant == "b":
        W_attn = ext["W_attn"]
        b_attn = ext["b_attn"]
        W_proj = ext["W_proj"]
        b_proj = ext["b_proj"]
        wbf = nc.dram_tensor(ctx.prefix + "wbf", [D, 3 * D], BF16, kind="Internal").ap()
        wpbf = nc.dram_tensor(ctx.prefix + "wpbf", [D, D], BF16, kind="Internal").ap()
    else:
        Wq = ext["Wq"]
        Wk = ext["Wk"]
        Wv = ext["Wv"]
        Wproj = ext["Wproj"]
        bproj = ext["bproj"]
        Wpre = ext["Wpre"]
        bpre = ext["bpre"]
        pi_in = ext["pi"]
        wqbf = nc.dram_tensor(ctx.prefix + "wqbf", [D, D], BF16, kind="Internal").ap()
        wkbf = nc.dram_tensor(ctx.prefix + "wkbf", [D, D], BF16, kind="Internal").ap()
        wvbf = nc.dram_tensor(ctx.prefix + "wvbf", [D, D], BF16, kind="Internal").ap()
        wpbf = nc.dram_tensor(ctx.prefix + "wpbf", [D, D], BF16, kind="Internal").ap()
        wrbf = nc.dram_tensor(ctx.prefix + "wrbf", [D, D], BF16, kind="Internal").ap()
        L_d = nc.dram_tensor(ctx.prefix + "L_d", [S, S], BF16, kind="Internal").ap()
    xo = ext["xo"]
    nlrow_d = nc.dram_tensor(ctx.prefix + "nlrow_d", [2, 2, 128, 128], BF16, kind="Internal").ap()

    c_ident = ctx.il0(np_ident(), "identsrc")
    causal = variant == "b"
    if causal:
        n_ = np.arange(128)[:, None]
        m_ = np.arange(512)[None, :]
        M1 = np.stack([np.where(m_ > n_ + 128 * j, 0.0, 1.0)
                       for j in range(4)], 0).astype(np.float32)
        ml = np.arange(128)[:, None]
        nl = np.arange(512)[None, :]
        M2 = np.stack([np.where(nl < 128 * j + ml, 0.0, 1.0)
                       for j in range(4)], 0).astype(np.float32)
        c_m1 = ctx.il0(M1, "M1")
        c_m2 = ctx.il0(M2, "M2")

    xv = x.rearrange("(n p) d -> p n d", p=128)
    xov = xo.rearrange("(n p) d -> p n d", p=128)

    qT = ctx.sb("qT", [128, 8, S], BF16)
    kT = ctx.sb("kT", [128, 8, S], BF16)
    vtm = ctx.sb("vtm", [128, NTILE, 1024], BF16)
    TT = ctx.sb("TT", [128, 8, S], BF16)
    outT = TT
    preT = TT
    identbf = ctx.sb("identbf", [128, 128], BF16)
    mrs = ctx.sb("mrs", [128, NTILE, 2], F32)
    o1b_ = ctx.sb("ones1bf", [1, 128], BF16)
    ctx._ones1bf = o1b_
    ctx.emit("v", lambda v, o1b_=o1b_: v.memset(o1b_[:, :], 1.0), mark=True)

    smc = ctx.sem("consts")
    ctx.dma("g", identbf[:, :], c_ident.ap()[:, :], inc=[(smc, 16)])
    ident_ok = smc.tick(16)
    smw = ctx.sem("gc_")
    if variant == "b":
        ctx.dma("g", wbf[:, :], W_attn[:, :], inc=[(smw, 16)])
        smw.tick(16)
        ctx.dma("g", wpbf[:, :], W_proj[:, :], inc=[(smw, 16)])
        smw.tick(16)
    else:
        for dst, srcw in ((wqbf, Wq), (wkbf, Wk), (wvbf, Wv), (wpbf, Wproj),
                          (wrbf, Wpre)):
            ctx.dma("g", dst[:, :], srcw[:, :], inc=[(smw, 16)])
            smw.tick(16)
    wcast_all = smw.n

    m_ln = ctx.mark()
    hTs = ctx.sb("hTs", [128, 8, S], BF16)
    m0 = ctx.mark()
    xring = ctx.sb("xring", [128, 3, 1024], F32)
    smx3 = [ctx.sem(f"lx{j}") for j in range(3)]

    def xsrc(i):
        slot = i % 3
        smx = smx3[slot]
        w = []
        if i >= 3:
            w.append((ctx.sems["lnln"], ctx.sems["lnln"].n - 10))
        ctx.dma("y", xring[:, slot, :], xv[:, i, :], wait=w, inc=[(smx, 16)])
        smx.tick(16)
        return xring[:, slot, :], [(smx, smx.n)]

    trsem = emit_ln_transpose_stream(ctx, xsrc, hTs, "ln", identbf,
                                     [(smc, ident_ok)], keep_h=None, keep_mrs=mrs)
    hT_done = trsem.n
    ctx.barrier()
    ctx.free_to(m0)

    m1 = ctx.mark()
    wcol = ctx.sb("wcol", [128, 2, 8, 128], BF16)
    bias_c = ctx.sb("bias_c", [128, 32], F32)
    psq = ctx.ps("psq", [128, 2, 512], F32)
    smmm = ctx.sem("mms")
    smev = ctx.sem("evs")
    smbc = ctx.sem("bch")

    piB = None
    btmp = ctx.sb("btmp", [24, 128], F32)
    identf = ctx.sb("identf", [128, 128], F32)
    ctx.dma("g", identf[:, :], c_ident.ap()[:, :], inc=[(smc, 16)])
    smc.tick(16)
    idf_ok = smc.n
    psbt = ctx.ps("psbt", [128, 512], F32)
    nbc = 24 if variant == "b" else 8
    bap = b_attn if variant == "b" else bpre
    ctx.dma("y", btmp[0:nbc, :],
            bap.rearrange("(c p) -> c p", p=128)[0:nbc, :], inc=[(smbc, 16)])
    smbc.tick(16)
    ctx.emit("t", lambda t, nbc=nbc: t.transpose(
        psbt[:, 0:nbc], btmp[0:nbc, :], identf[0:nbc, 0:nbc]),
        wait=[(smbc, smbc.n), (smc, idf_ok)], inc=[(smbc, 1)])
    smbc.tick()
    ctx.emit("v", lambda v, nbc=nbc: v.tensor_copy(
        bias_c[:, 0:nbc], psbt[:, 0:nbc]),
        wait=[(smbc, smbc.n)], inc=[(smbc, 1)])
    smbc.tick()
    if variant == "a":
        piB = ctx.sb("piB", [128, 1], F32)
        pi1 = ctx.sb("pi1", [1, 1], F32)
        onesf1 = ctx.sb("onesf1", [1, 128], F32)
        smpi = ctx.sem("lx2")
        ctx.emit("v", lambda v: v.memset(onesf1[:, :], 1.0), mark=True)
        vv = ctx._last_inc["v"]
        ctx.dma("y", pi1[:, :], pi_in[None, :], inc=[(smpi, 16)])
        smpi.tick(16)
        ctx.emit("t", lambda t: t.matmul(psq[:, 0, 0:1], onesf1[:, :], pi1[:, :],
                                         start=True, stop=True),
                 wait=[(smpi, smpi.n), vv], inc=[(smpi, 1)])
        smpi.tick()
        ctx.emit("v", lambda v: v.tensor_scalar(piB[:, :], psq[:, 0, 0:1],
                                                1.0 / 32.0, None, ALU.mult),
                 wait=[(smpi, smpi.n)], inc=[(smbc, 1)])
        smbc.tick()
    bias_ready = smbc.n
    ctx.barrier()

    def fm_proj2(w_dram_bf, col_off, outbuf, mode, scale=1.0, bcol0=0):
        sm_l2 = [ctx.sem(), ctx.sem()]
        ev_entry = smev.n
        for oc in range(8):
            par = oc % 2
            sm_l = sm_l2[par]
            w0 = [(smw, wcast_all)]
            if oc >= 2:
                w0.append((smev, smev.n - 1))
            else:
                w0.append((smev, ev_entry))
            for mc in range(8):
                ctx.dma("y", wcol[:, par, mc, :],
                        w_dram_bf[mc * 128:(mc + 1) * 128,
                                  col_off + oc * 128: col_off + (oc + 1) * 128],
                        wait=w0 if mc == 0 else (), inc=[(sm_l, 16)])
                sm_l.tick(16)
            for tg in range(4):
                it = oc * 4 + tg
                pp = it % 2
                w2 = [(sm_l, sm_l.n)]
                if it == 0:
                    w2.append((trsem, hT_done))
                if it >= 2:
                    w2.append((smev, smev.n - 1))
                for mc in range(8):
                    ctx.emit("t", lambda t, oc=oc, tg=tg, mc=mc, par=par, pp=pp: t.matmul(
                        psq[:, pp, :], wcol[:, par, mc, :],
                        hTs[:, mc, tg * 512:(tg + 1) * 512],
                        start=(mc == 0), stop=(mc == 7)),
                        wait=w2 if mc == 0 else (),
                        inc=[(smmm, 1)] if mc == 7 else [])
                mmw = smmm.tick()
                if mode == "silu":
                    ctx.emit("s", lambda s, oc=oc, tg=tg, pp=pp, scale=scale, bcol0=bcol0: s.activation(
                        outbuf[:, oc, tg * 512:(tg + 1) * 512], psq[:, pp, :],
                        AF.Silu, bias=bias_c[:, bcol0 + oc:bcol0 + oc + 1],
                        scale=scale),
                        wait=[(smmm, mmw), (smbc, bias_ready)], inc=[(smev, 1)])
                elif mode == "bias":
                    ctx.emit("v", lambda v, oc=oc, tg=tg, pp=pp, scale=scale, bcol0=bcol0: v.tensor_scalar(
                        outbuf[:, oc, tg * 512:(tg + 1) * 512], psq[:, pp, :],
                        bias_c[:, bcol0 + oc:bcol0 + oc + 1], scale,
                        ALU.add, ALU.mult),
                        wait=[(smmm, mmw), (smbc, bias_ready)], inc=[(smev, 1)])
                else:
                    ctx.emit("s", lambda s, oc=oc, tg=tg, pp=pp, scale=scale: s.activation(
                        outbuf[:, oc, tg * 512:(tg + 1) * 512], psq[:, pp, :],
                        AF.Copy, scale=scale),
                        wait=[(smmm, mmw)], inc=[(smev, 1)])
                smev.tick()

    def tm_proj2(w_dram_bf, col_off, brow_bf, outbuf):
        wv_sb = ctx.sb("wv_sb", [128, 8, 1024], BF16)
        sm_l = ctx.sem()
        for mc in range(8):
            ctx.dma("y", wv_sb[:, mc, :],
                    w_dram_bf[mc * 128:(mc + 1) * 128, col_off:col_off + 1024],
                    wait=[(smw, wcast_all), (smev, smev.n)] if mc == 0 else (),
                    inc=[(sm_l, 16)])
            sm_l.tick(16)
        for i in range(NTILE):
            for og in range(2):
                it = i * 2 + og
                pp = it % 2
                w2 = [(sm_l, sm_l.n)]
                if it >= 2:
                    w2.append((smev, smev.n - 1))
                use_b = brow_bf is not None
                for mc in range(8):
                    ctx.emit("t", lambda t, i=i, og=og, mc=mc, pp=pp, use_b=use_b: t.matmul(
                        psq[:, pp, :], hTs[:, mc, i * 128:(i + 1) * 128],
                        wv_sb[:, mc, og * 512:(og + 1) * 512],
                        start=(mc == 0), stop=(mc == 7 and not use_b)),
                        wait=w2 if mc == 0 else (),
                        inc=[(smmm, 1)] if (mc == 7 and not use_b) else [])
                if use_b:
                    ctx.emit("t", lambda t, og=og, pp=pp: t.matmul(
                        psq[:, pp, :], o1b_[:, :],
                        brow_bf[:, 0, og * 512:(og + 1) * 512],
                        start=False, stop=False))
                    ctx.emit("t", lambda t, og=og, pp=pp: t.matmul(
                        psq[:, pp, :], o1b_[:, :],
                        brow_bf[:, 1, og * 512:(og + 1) * 512],
                        start=False, stop=True), inc=[(smmm, 1)])
                mmw = smmm.tick()
                ctx.emit("s", lambda s, i=i, og=og, pp=pp: s.copy(
                    outbuf[:, i, og * 512:(og + 1) * 512], psq[:, pp, :]),
                    wait=[(smmm, mmw)], inc=[(smev, 1)])
                smev.tick()

    smbr = ctx.sem("lx2")
    if variant == "b":
        vbrow = ctx.sb("vbrow", [1, 2, 1024], BF16)
        vbtmp = ctx.sb("vbtmp", [1, 1024], F32)
        ctx.dma("y", vbtmp[:, :], b_attn[None, 2048:3072], inc=[(smbr, 16)])
        smbr.tick(16)
        ctx.emit("v", lambda v: v.tensor_copy(vbrow[:, 0, :], vbtmp[:, :]),
                 wait=[(smbr, smbr.n)], mark=True)
        ctx.emit("v", lambda v: v.tensor_tensor(vbtmp[:, :], vbtmp[:, :],
                                                vbrow[:, 0, :], ALU.subtract),
                 dep=True, mark=True)
        ctx.emit("v", lambda v: v.tensor_copy(vbrow[:, 1, :], vbtmp[:, :]),
                 dep=True, inc=[(smbr, 1)])
        smbr.tick()
        fm_proj2(wbf, 0, qT, "bias", scale=0.125, bcol0=0)
        fm_proj2(wbf, 1024, kT, "bias", scale=1.0, bcol0=8)
        tm_proj2(wbf, 2048, vbrow, vtm)
    else:
        fm_proj2(wqbf, 0, qT, "plain", scale=0.125)
        fm_proj2(wkbf, 0, kT, "plain", scale=1.0)
        fm_proj2(wrbf, 0, preT, "silu", scale=1.0)
        tm_proj2(wvbf, 0, None, vtm)

    ctx.barrier()
    ctx.free_to(m1)
    if variant == "b":
        ctx.free_to(m_ln)
    else:
        ctx.free_to(m_ln)

    m2 = ctx.mark()
    stat = ctx.sb("stat", [128, NH, 8], F32)
    stat2 = ctx.sb("stat2", [128, NH, 8], F32)
    nlC = ctx.sb("nlC", [128, 2, 256], BF16)
    exp_scr = ctx.sb("exp_scr", [128, 512], BF16)
    exp_scrs = ctx.sb("exp_scrs", [128, 4, 512], BF16)
    exp_scr2 = ctx.sb("exp_scr2", [128, 512], BF16)
    if causal:
        M1s = ctx.sb("M1s", [128, 4, 512], BF16)
        ctx.dma("g", M1s[:, :, :], c_m1.ap().rearrange("j p m -> p j m"),
                inc=[(smc, 16)])
        smc.tick(16)
    if variant == "a":
        L1buf = ctx.sb("L1buf", [128, 2, 512], BF16)
        psL = ctx.ps("psL", [128, 2, 512], F32)
    psS = ctx.ps("psS", [128, 4, 512], F32)
    consts_p1 = smc.n

    smS = ctx.sem("mms")
    smE = ctx.sem("ev2")
    smA = ctx.sem("acts")
    smL = ctx.sem("evs")
    smLds = [ctx.sem("sd0"), ctx.sem("sd1")]
    smFz = ctx.sem("lu0")
    e0_p1 = smE.n

    for nb in range(NTILE):
        ngroups = (nb // 4) + 1 if causal else 4
        for mg in range(ngroups):
            if variant == "a":
                lit = smL.n % 2
                smLd = smLds[lit]
                w2 = [(smc, consts_p1), (smbc, bias_ready)]
                if smL.n >= 2:
                    w2.append((smLd, smLd.n))
                    w2.append((smL, smL.n - 1))
                if smE.n - e0_p1 >= 16:
                    w2.append((smE, smE.n - 16))
                for mc in range(8):
                    ctx.emit("t", lambda t, nb=nb, mg=mg, mc=mc, lit=lit: t.matmul(
                        psL[:, lit, :], preT[:, mc, nb * 128:(nb + 1) * 128],
                        preT[:, mc, mg * 512:(mg + 1) * 512],
                        start=(mc == 0), stop=(mc == 7)),
                        wait=w2 if mc == 0 else (),
                        inc=[(smS, 1)] if mc == 7 else [])
                sw = smS.tick()
                ctx.emit("s", lambda s, mg=mg, lit=lit: s.activation(
                    L1buf[:, lit, :], psL[:, lit, :], AF.Exp,
                    scale=piB[:, 0:1]),
                    wait=[(smS, sw)], inc=[(smL, 1)])
                smL.tick()
                ctx.dma("g", L_d[nb * 128:(nb + 1) * 128,
                                 mg * 512:(mg + 1) * 512],
                        L1buf[:, lit, :], wait=[(smL, smL.n)], inc=[(smLd, 16)])
                smLd.tick(16)
                _ = smLd
            for h in range(NH):
                c = h // 2
                rb = (h % 2) * 64
                lit = (smL.n - 1) % 2 if variant == "a" else 0
                it = smE.n % 4
                diag = causal and mg == ngroups - 1
                w2 = [(smc, consts_p1)]
                if variant == "a":
                    w2.append((smL, smL.n))
                if smE.n >= 4:
                    w2.append((smE, smE.n - 3))
                ctx.emit("t", lambda t, c=c, rb=rb, nb=nb, mg=mg, it=it: t.matmul(
                    psS[:, it, :], qT[rb:rb + 64, c, nb * 128:(nb + 1) * 128],
                    kT[rb:rb + 64, c, mg * 512:(mg + 1) * 512],
                    start=True, stop=True),
                    wait=w2, inc=[(smS, 1)])
                sw = smS.tick()
                if variant == "a":
                    mo = L1buf[:, lit, :]
                elif diag:
                    mo = M1s[:, nb % 4, :]
                else:
                    mo = None
                psum = stat[:, h, 4 + mg:5 + mg]
                if mo is None:
                    ctx.emit("s", lambda s, psum=psum, it=it: s.activation(
                        exp_scrs[:, it, :], psS[:, it, :], AF.Exp,
                        accum_out=psum),
                        wait=[(smS, sw), (smFz, smFz.n)], inc=[(smA, 1)])
                    smA.tick()
                    ctx.emit("v", lambda v: v.engine_nop(),
                             wait=[(smA, smA.n)], dep=True, inc=[(smE, 1)])
                    smE.tick()
                else:
                    ctx.emit("s", lambda s, it=it: s.activation(
                        exp_scrs[:, it, :], psS[:, it, :], AF.Exp),
                        wait=[(smS, sw)], inc=[(smA, 1)])
                    smA.tick()
                    ctx.emit("v", lambda v, psum=psum, it=it, mo=mo:
                             v.scalar_tensor_tensor(
                        exp_scr2[:, :], exp_scrs[:, it, :], 1.0, mo,
                        ALU.mult, ALU.mult, accum_out=psum),
                        wait=[(smA, smA.n)], dep=True, inc=[(smE, 1)])
                    smE.tick()
        ngc = ngroups
        ew = [(smE, smE.n)]
        if ngc == 1:
            ctx.emit("v", lambda v: v.tensor_copy(
                stat2[:, :, 5:6], stat[:, :, 4:5]), wait=ew, dep=True,
                mark=True)
        else:
            ctx.emit("v", lambda v, ngc=ngc: v.reduce_sum(
                stat2[:, :, 5:6], stat[:, :, 4:4 + ngc], axis=AX.X),
                wait=ew, dep=True, mark=True)
        vv = ctx._last_inc["v"]
        ctx.emit("s", lambda s: s.activation(
            stat2[:, :, 6:7], stat2[:, :, 5:6], AF.Ln),
            wait=[vv], inc=[(smA, 1)])
        smA.tick()
        nlv0 = nlC[:, 0, :].rearrange("p (h q) -> p h q", q=16)[:, :, nb:nb + 1]
        nlv1 = nlC[:, 1, :].rearrange("p (h q) -> p h q", q=16)[:, :, nb:nb + 1]
        ctx.emit("v", lambda v: v.tensor_scalar(
            stat2[:, :, 7:8], stat2[:, :, 6:7], -1.0, None, ALU.mult),
            wait=[(smA, smA.n)], dep=True, mark=True)
        ctx.emit("v", lambda v, nlv0=nlv0: v.tensor_copy(
            nlv0, stat2[:, :, 7:8]), dep=True, mark=True)
        ctx.emit("v", lambda v, nlv0=nlv0: v.tensor_tensor(
            stat2[:, :, 6:7], stat2[:, :, 7:8], nlv0, ALU.subtract),
            dep=True, mark=True)
        ctx.emit("v", lambda v, nlv1=nlv1: v.tensor_copy(
            nlv1, stat2[:, :, 6:7]), dep=True, inc=[(smFz, 1)])
        smFz.tick()

    ctx.barrier()
    pstn = ctx.ps("pstn", [128, 2, 1024], BF16)
    smtb = ctx.sem("tps")
    smtd = ctx.sem("sd0")
    for hl in range(2):
        for half in range(2):
            k = hl * 2 + half
            ctx.emit("t", lambda t, hl=hl, half=half, k=k: t.transpose(
                pstn[:, k % 2, 0:128], nlC[:, hl, half * 128:(half + 1) * 128],
                identbf[:, :]), wait=[(smtb, smtb.n)], inc=[(smtb, 1)])
            smtb.tick()
            ctx.emit("s", lambda s, k=k: s.copy(
                exp_scr[:, 0:128], pstn[:, k % 2, 0:128]),
                wait=[(smtb, smtb.n), (smtd, smtd.n)], inc=[(smtb, 1)])
            smtb.tick()
            ctx.dma("g", nlrow_d[hl, half, :, :], exp_scr[:, 0:128],
                    wait=[(smtb, smtb.n)], inc=[(smtd, 16)])
            smtd.tick(16)
    nl_bounced = smtd.n
    ctx.barrier()
    ctx.free_to(m2)

    m3 = ctx.mark()
    nlR = ctx.sb("nlR", [2, 2, 16, 512], BF16)
    ones2b = ctx.sb("ones2b", [2, 128], BF16)
    ctx.emit("v", lambda v: v.memset(ones2b[:, :], 1.0), mark=True)
    ctx.emit("t", None, wait=[ctx._last_inc["v"]])
    if causal:
        M2s = ctx.sb("M2s", [128, 4, 512], BF16)
        ctx.dma("g", M2s[:, :, :], c_m2.ap().rearrange("j p m -> p j m"),
                inc=[(smc, 16)])
        smc.tick(16)
    if variant == "a":
        L2col = ctx.sb("L2col", [128, NTILE, 512], BF16)
    pTt = ctx.sb("pTt", [128, 4, 512], BF16)
    pTt2 = ctx.sb("pTt2", [128, 4, 512], BF16)
    psS2 = ctx.ps("psS2", [128, 4, 512], F32)
    psO = ctx.ps("psO", [128, 2, 512], F32)
    consts_p2 = smc.n

    smS2 = ctx.sem("mms")
    smA2 = ctx.sem("acts")
    smA3 = ctx.sem("ev3")
    smO = ctx.sem("mm2")
    smG2 = ctx.sem("ev2")
    smL2 = ctx.sem("lx0")
    smnls = [ctx.sem("lu0"), ctx.sem("lu1")]
    a30_p2 = smA3.n

    for ng in range(4):
        ngs = ng % 2
        smnl = smnls[ngs]
        wnl = [(smtd, nl_bounced)]
        if ng >= 2:
            wnl.append((smS2, smS2.n))
        ctx.dma("y", nlR[:, ngs, :, :].rearrange(
            "hl h (nb p) -> hl h nb p", p=128),
            nlrow_d[:, :, :, :].rearrange(
                "hl half r p -> hl (half r) p").rearrange(
                "hl (h nb) p -> hl h nb p", nb=16)[:, :, 4 * ng:4 * ng + 4, :],
            wait=wnl, inc=[(smnl, 16)])
        smnl.tick(16)
        nl_ok = smnl.n
        if variant == "a":
            for mb2 in range(NTILE):
                wl = []
                if mb2 == 0:
                    wl = [(smA3, smA3.n), (smLds[0], smLds[0].n),
                          (smLds[1], smLds[1].n)]
                ctx.dma("y", L2col[:, mb2, :],
                        L_d[mb2 * 128:(mb2 + 1) * 128,
                            ng * 512:(ng + 1) * 512],
                        wait=wl, inc=[(smL2, 16)])
                smL2.tick(16)
        l2_ok = smL2.n
        for h in range(NH):
            c = h // 2
            rb = (h % 2) * 64
            head_seq = ng * NH + h
            po = psO[rb:rb + 64, h % 2, :]
            po_guard = [(smG2, smG2.n - 1)] if head_seq >= 2 else []
            mb_hi = 4 * (ng + 1) if causal else NTILE
            first = True
            pend = []
            for mb in range(mb_hi):
                it = smA2.n % 4
                diag = causal and mb // 4 == ng
                w2 = [(smc, consts_p2), (smnl, nl_ok)]
                if smA2.n >= 4:
                    w2.append((smA2, smA2.n - 3))
                if len(pend) == 3:
                    pend.pop(0)()
                ctx.emit("t", lambda t, c=c, rb=rb, ng=ng, mb=mb, it=it: t.matmul(
                    psS2[:, it, :], kT[rb:rb + 64, c, mb * 128:(mb + 1) * 128],
                    qT[rb:rb + 64, c, ng * 512:(ng + 1) * 512],
                    start=True, stop=False),
                    wait=w2)
                ctx.emit("t", lambda t, h=h, ngs=ngs, it=it: t.matmul(
                    psS2[:, it, :], ones2b[:, :],
                    nlR[:, ngs, h, :], start=False, stop=True),
                    inc=[(smS2, 1)])
                sw = smS2.tick()
                if variant == "a":
                    mo2 = L2col[:, mb, :]
                    mow = [(smL2, l2_ok)]
                elif diag:
                    mo2 = M2s[:, mb % 4, :]
                    mow = [(smc, consts_p2)]
                else:
                    mo2 = None
                if mo2 is None:
                    ctx.emit("s", lambda s, it=it: s.activation(
                        pTt[:, it, :], psS2[:, it, :], AF.Exp),
                        wait=[(smS2, sw)], inc=[(smA2, 1)])
                    aw = smA2.tick()
                    pvw = [(smA2, aw)]
                else:
                    if variant == "a":
                        w_p2 = ([(smA3, smA3.n - 3)]
                                if smA3.n - a30_p2 >= 4 else [])
                    else:
                        w_p2 = [(smA3, smA3.n)]
                    ctx.emit("s", lambda s, it=it: s.activation(
                        pTt2[:, it, :], psS2[:, it, :], AF.Exp),
                        wait=[(smS2, sw)] + w_p2, inc=[(smA2, 1)])
                    aw = smA2.tick()
                    ctx.emit("v", lambda v, it=it, mo2=mo2: v.tensor_tensor(
                        pTt[:, it, :], pTt2[:, it, :], mo2, ALU.mult),
                        wait=[(smA2, aw)] + mow, dep=True, inc=[(smA3, 1)])
                    a3w = smA3.tick()
                    pvw = [(smA3, a3w)]
                pend.append(lambda mb=mb, h=h, it=it, po=po, fp=first,
                            last=(mb == mb_hi - 1), pvw=pvw,
                            pg=(po_guard if first else []): ctx.emit(
                    "t", lambda t: t.matmul(
                        po, vtm[:, mb, h * 64:(h + 1) * 64], pTt[:, it, :],
                        start=fp, stop=last),
                    wait=pvw + pg,
                    inc=[(smO, 1)] if last else []))
                first = False
            for f in pend:
                f()
            ow = smO.tick()
            ctx.emit("v", lambda v, c=c, rb=rb, ng=ng, po=po: v.tensor_copy(
                outT[rb:rb + 64, c, ng * 512:(ng + 1) * 512], po),
                wait=[(smO, ow)], inc=[(smG2, 1)])
            smG2.tick()

    ctx.barrier()
    ctx.free_to(m3)

    m4 = ctx.mark()
    wpS = ctx.sb("wpS", [128, 8, 1024], BF16)
    pbrow = ctx.sb("pbrow", [1, 2, 1024], BF16)
    pbtmp = ctx.sb("pbtmp", [1, 1024], F32)
    xin = ctx.sb("xin", [128, 2, 1024], F32)
    psX = ctx.ps("psX", [128, 2, 512], F32)
    if variant == "a":
        aff = ctx.sb("aff", [128, NTILE, 2], F32)
        ctx.emit("v", lambda v: v.tensor_scalar(
            aff[:, :, 0], mrs[:, :, 1], 1.0, None, ALU.add), mark=True)
        ctx.emit("v", lambda v: v.tensor_tensor(
            aff[:, :, 1], mrs[:, :, 0], mrs[:, :, 1], ALU.mult),
            dep=True, mark=True)
        ctx.emit("v", lambda v: v.tensor_scalar(
            aff[:, :, 1], aff[:, :, 1], -1.0, None, ALU.mult),
            dep=True, mark=True)
    smwp = ctx.sem("wl_")
    for mc in range(8):
        ctx.dma("y", wpS[:, mc, :], wpbf[mc * 128:(mc + 1) * 128, :],
                inc=[(smwp, 16)])
        smwp.tick(16)
    bsrc = b_proj if variant == "b" else bproj
    ctx.dma("y", pbtmp[:, :], bsrc[None, :], inc=[(smwp, 16)])
    smwp.tick(16)
    wp_done = smwp.n
    ctx.emit("v", lambda v: v.tensor_copy(pbrow[:, 0, :], pbtmp[:, :]),
             wait=[(smwp, wp_done)], mark=True)
    ctx.emit("v", lambda v: v.tensor_tensor(pbtmp[:, :], pbtmp[:, :],
                                            pbrow[:, 0, :], ALU.subtract),
             dep=True, mark=True)
    ctx.emit("v", lambda v: v.tensor_copy(pbrow[:, 1, :], pbtmp[:, :]),
             dep=True, inc=[(smwp, 1)])
    smwp.tick()
    pb_done = smwp.n
    aff_done = ctx._last_inc["v"]

    smXm = ctx.sem("mm2")
    smXe = ctx.sem("ev2")
    smxl2 = [ctx.sem(f"lx{j}") for j in range(2)]
    smxs2 = [ctx.sem(f"sx{j}") for j in range(2)]
    for gt in range(NTILE):
        slot = gt % 2
        smxl = smxl2[slot]
        smxs = smxs2[slot]
        wx = [(smxs, smxs.n)] if gt >= 2 else []
        ctx.dma("y", xin[:, slot, :], xv[:, gt, :], wait=wx, inc=[(smxl, 16)])
        xl = smxl.tick(16)
        for oh in range(2):
            it2 = gt * 2 + oh
            par = it2 % 2
            w2 = [(smwp, pb_done)]
            if it2 >= 2:
                w2.append((smXe, smXe.n - 1))
            for c2 in range(8):
                ctx.emit("t", lambda t, gt=gt, oh=oh, c2=c2, par=par: t.matmul(
                    psX[:, par, :], outT[:, c2, gt * 128:(gt + 1) * 128],
                    wpS[:, c2, oh * 512:(oh + 1) * 512],
                    start=(c2 == 0), stop=False),
                    wait=w2 if c2 == 0 else ())
            ctx.emit("t", lambda t, oh=oh, par=par: t.matmul(
                psX[:, par, :], o1b_[:, :],
                pbrow[:, 0, oh * 512:(oh + 1) * 512], start=False, stop=False))
            ctx.emit("t", lambda t, oh=oh, par=par: t.matmul(
                psX[:, par, :], o1b_[:, :],
                pbrow[:, 1, oh * 512:(oh + 1) * 512], start=False, stop=True),
                inc=[(smXm, 1)])
            mmw = smXm.tick()
            if variant == "a":
                ctx.emit("v", lambda v, slot=slot, gt=gt, oh=oh: v.tensor_scalar(
                    xin[:, slot, oh * 512:(oh + 1) * 512],
                    xin[:, slot, oh * 512:(oh + 1) * 512],
                    aff[:, gt, 0:1], aff[:, gt, 1:2], ALU.mult, ALU.add),
                    wait=[(smXm, mmw), (smxl, xl), aff_done], dep=True, mark=True)
                ctx.emit("v", lambda v, slot=slot, oh=oh, par=par: v.tensor_tensor(
                    xin[:, slot, oh * 512:(oh + 1) * 512],
                    xin[:, slot, oh * 512:(oh + 1) * 512],
                    psX[:, par, :], ALU.add), dep=True, inc=[(smXe, 1)])
            else:
                ctx.emit("v", lambda v, slot=slot, oh=oh, par=par: v.tensor_tensor(
                    xin[:, slot, oh * 512:(oh + 1) * 512],
                    xin[:, slot, oh * 512:(oh + 1) * 512],
                    psX[:, par, :], ALU.add),
                    wait=[(smXm, mmw), (smxl, xl)], inc=[(smXe, 1)])
            smXe.tick()
        ctx.dma("g", xov[:, gt, :], xin[:, slot, :],
                wait=[(smXe, smXe.n)], inc=[(smxs, 16)])
        smxs.tick(16)

    ctx.emit("y", None, wait=[(smxs2[0], smxs2[0].n), (smxs2[1], smxs2[1].n)])
    ctx.barrier(all_sems=True)
    ctx.free_to(_m_all)




def emit_l0_attn_b(ctx, ext):
    nc = ctx.nc
    ctx._eps_t = None
    _m_all = ctx.mark()
    x = ext["x"]
    W_qkvb = ext["W_qkvb"]
    W_o = ext["W_o"]
    ln_g = ext["ln_g"]
    ln_b = ext["ln_b"]
    xo = ext["xo"]
    wbf = nc.dram_tensor(ctx.prefix + "wbf", [3088, D], BF16, kind="Internal").ap()
    wobf = nc.dram_tensor(ctx.prefix + "wobf", [D, D], BF16, kind="Internal").ap()
    qkvb_d = nc.dram_tensor(ctx.prefix + "qkvb_d", [2, 8, 128, 3088], BF16, kind="Internal").ap()
    c_ident = ctx.il0(np_ident(), "identsrc")

    xv = x.rearrange("(n p) d -> p n d", p=128)
    xov = xo.rearrange("(n p) d -> p n d", p=128)
    NQ = 3088

    identbf = ctx.sb("identbf", [128, 128], BF16)
    mrs = ctx.sb("mrs", [128, NTILE, 2], F32)
    OUT = ctx.sb("OUT", [128, NTILE, 1024], BF16)
    ctx._eps_t = ctx.sb("eps0", [128, 1], F32)
    _e0 = ctx._eps_t
    ctx.emit("v", lambda v, _e0=_e0: v.memset(_e0[:, :], 1e-5), mark=True)
    smc = ctx.sem("consts")
    ctx.dma("g", identbf[:, :], c_ident.ap()[:, :], inc=[(smc, 16)])
    ident_ok = smc.tick(16)
    smw = ctx.sem("gc_")
    ctx.dma("g", wbf[:, :], W_qkvb[:, :], inc=[(smw, 16)])
    smw.tick(16)
    ctx.dma("g", wobf[:, :], W_o[:, :], inc=[(smw, 16)])
    smw.tick(16)
    wcast_all = smw.n

    m_ln = ctx.mark()
    hTs = ctx.sb("hTs", [128, 8, S], BF16)
    m0 = ctx.mark()
    xring = ctx.sb("xring", [128, 3, 1024], F32)
    smx3 = [ctx.sem(f"lx{j}") for j in range(3)]

    def xsrc(i):
        slot = i % 3
        smx = smx3[slot]
        w = []
        if i >= 3:
            w.append((ctx.sems["lnln"], ctx.sems["lnln"].n - 10))
        ctx.dma("y", xring[:, slot, :], xv[:, i, :], wait=w, inc=[(smx, 16)])
        smx.tick(16)
        return xring[:, slot, :], [(smx, smx.n)]

    trsem = emit_ln_transpose_stream(ctx, xsrc, hTs, "ln", identbf,
                                     [(smc, ident_ok)], keep_h=None, keep_mrs=mrs)
    hT_done = trsem.n
    ctx.barrier()
    ctx.free_to(m0)

    m1 = ctx.mark()
    wT = ctx.sb("wT", [128, 8, NQ], BF16)
    qbuf = ctx.sb("qbuf", [128, 2, 3088], BF16)
    psq = ctx.ps("psq", [128, 2, 512], F32)
    smwl = ctx.sem("wl_")
    for mc in range(8):
        ctx.emit("y", lambda y, mc=mc: y.dma_start(
            wT[:, mc, :], wbf[:, mc * 128:(mc + 1) * 128], transpose=True),
            wait=[(smw, wcast_all)] if mc == 0 else (), inc=[(smwl, 16)])
        smwl.tick(16)
    wl_done = smwl.n
    smmm = ctx.sem("mms")
    smev = ctx.sem("evs")
    smqd = [ctx.sem("sd0"), ctx.sem("sd1")]
    GRP = [(0, 512), (512, 512), (1024, 512), (1536, 512), (2048, 512),
           (2560, 512), (3072, 16)]
    for i in range(NTILE):
        slot = i % 2
        smq = smqd[slot]
        for gi, (off, wd) in enumerate(GRP):
            it = smev.n % 2
            w2 = [(smwl, wl_done)]
            if i >= 2 and gi == 0:
                w2.append((smq, smq.n))
            if smev.n >= 2:
                w2.append((smev, smev.n - 1))
            for mc in range(8):
                ctx.emit("t", lambda t, i=i, off=off, wd=wd, mc=mc, it=it: t.matmul(
                    psq[:, it, 0:wd], hTs[:, mc, i * 128:(i + 1) * 128],
                    wT[:, mc, off:off + wd], start=(mc == 0), stop=(mc == 7)),
                    wait=w2 if mc == 0 else (),
                    inc=[(smmm, 1)] if mc == 7 else [])
            mmw = smmm.tick()
            ctx.emit("s", lambda s, slot=slot, off=off, wd=wd, it=it: s.copy(
                qbuf[:, slot, off:off + wd], psq[:, it, 0:wd]),
                wait=[(smmm, mmw)], inc=[(smev, 1)])
            smev.tick()
        ctx.dma("g", qkvb_d[i % 2 if False else (i // 8 if False else 0), 0, 0, 0]
                if False else
                qkvb_d.rearrange("ph s p f -> (ph s) p f")[i, :, :],
                qbuf[:, slot, :], wait=[(smev, smev.n)], inc=[(smq, 16)])
        smq.tick(16)
    qkvb_stored = (smqd[0].n, smqd[1].n)
    ctx.barrier()
    ctx.free_to(m_ln)

    qk_flat = qkvb_d.rearrange("ph s p f -> (ph s) p f")

    kh = ctx.sb("kh", [128, 8, 16, 256], BF16)
    qh = ctx.sb("qh", [128, 16, 256], BF16)
    ytmp = ctx.sb("ytmp", [128, 16, 128], BF16)
    qkv_s = ctx.sb("qkv_s", [128, 8, NQ], BF16)
    scr = ctx.sb("scrD", [128, 8, 256], BF16)
    Abuf = ctx.sb("Abuf", [128, 28, 16], F32)
    Sbuf = ctx.sb("Sbuf", [128, 36, 16], F32)
    kn = ctx.sb("kn", [128, 8, 16], F32)
    qn = ctx.sb("qn", [128, 8, 16], F32)
    kd = ctx.sb("kd", [128, 8, 16], F32)
    fT = ctx.sb("fT", [128, 8, 16], F32)
    bP = ctx.sb("bP", [128, 8, 16], F32)
    dn = ctx.sb("dn", [128, 8, 16], F32)
    U = ctx.sb("U", [128, 8, 16, 64], F32)
    acc = ctx.sb("accB", [128, 8, 64], F32)
    tmp64 = ctx.sb("tmp64", [128, 8, 64], F32)

    smld = [ctx.sem("lq0"), ctx.sem("lq1")]
    smD = ctx.sem("acts")
    APAIR = [(t, r) for t in range(1, 8) for r in range(t)]
    SPAIR = [(s, t) for s in range(8) for t in range(s + 1)]

    for ph in range(2):
        smq = smld[ph % 2]
        wq0 = [(smqd[0], qkvb_stored[0]), (smqd[1], qkvb_stored[1])]
        if ph == 1:
            wq0.append((ctx._last_inc["v"][0], ctx._last_inc["v"][1]))
            wq0.append((smD, smD.n))
        ctx.dma("y", qkv_s[:, :, :],
                qk_flat[:, :, :].rearrange("i p f -> p i f")[:, ph::2, :],
                wait=wq0, inc=[(smq, 16)])
        smq.tick(16)
        qk_ok = smq.n

        for s in range(8):
            koff = 64
            wv_ = [(smq, qk_ok)]
            if s > 0:
                wv_.append(ctx._last_inc["v"])
            ctx.emit("s", lambda s_, s2=s: s_.activation(
                ytmp[:, :, 0:64],
                qkv_s[:, s2, :].rearrange("p (h f) -> p h f", f=193)[:, :, 64:128],
                AF.Relu), wait=wv_, inc=[(smD, 1)])
            smD.tick()
            ctx.emit("s", lambda s_, s2=s: s_.activation(
                ytmp[:, :, 64:128],
                qkv_s[:, s2, :].rearrange("p (h f) -> p h f", f=193)[:, :, 64:128],
                AF.Relu, scale=-1.0), inc=[(smD, 1)])
            smD.tick()
            yready = smD.n
            for rv in (1, 2):
                half0 = (rv - 1) * 128
                ctx.emit("v", lambda v, s2=s, rv=rv, half0=half0: v.tensor_tensor(
                    kh[:, s2, :, half0 + rv:half0 + 128],
                    ytmp[:, :, rv:128], ytmp[:, :, 0:128 - rv], ALU.mult),
                    wait=[(smD, yready)], mark=True)
                ctx.emit("v", lambda v, s2=s, rv=rv, half0=half0: v.tensor_tensor(
                    kh[:, s2, :, half0:half0 + rv],
                    ytmp[:, :, 0:rv], ytmp[:, :, 128 - rv:128], ALU.mult),
                    dep=True, mark=True)
            ctx.emit("v", lambda v, s2=s: v.reduce_sum(
                kn[:, s2, :], kh[:, s2, :, :], axis=AX.X), dep=True, mark=True)

        Ar = Abuf.rearrange("p a (h one) -> p a h one", one=1)
        Sr = Sbuf.rearrange("p a (h one) -> p a h one", one=1)
        for pi_, (t, r) in enumerate(APAIR):
            for hb in range(2):
                h0 = hb * 8
                ctx.emit("v", lambda v, t=t, r=r, h0=h0: v.tensor_tensor(
                    scr[:, :, :], kh[:, t, h0:h0 + 8, :],
                    kh[:, r, h0:h0 + 8, :], ALU.mult),
                    dep=True, mark=True)
                ctx.emit("v", lambda v, pi_=pi_, h0=h0: v.reduce_sum(
                    Ar[:, pi_, h0:h0 + 8, :], scr[:, :, :], axis=AX.X),
                    dep=True, mark=True)
        for s in range(8):
            ctx.emit("s", lambda s_, s2=s: s_.activation(
                ytmp[:, :, 0:64],
                qkv_s[:, s2, :].rearrange("p (h f) -> p h f", f=193)[:, :, 0:64],
                AF.Relu), wait=[(ctx._last_inc["v"][0], ctx._last_inc["v"][1])],
                inc=[(smD, 1)])
            smD.tick()
            ctx.emit("s", lambda s_, s2=s: s_.activation(
                ytmp[:, :, 64:128],
                qkv_s[:, s2, :].rearrange("p (h f) -> p h f", f=193)[:, :, 0:64],
                AF.Relu, scale=-1.0), inc=[(smD, 1)])
            smD.tick()
            yready = smD.n
            for rv in (1, 2):
                half0 = (rv - 1) * 128
                ctx.emit("v", lambda v, rv=rv, half0=half0: v.tensor_tensor(
                    qh[:, :, half0 + rv:half0 + 128],
                    ytmp[:, :, rv:128], ytmp[:, :, 0:128 - rv], ALU.mult),
                    wait=[(smD, yready)], mark=True)
                ctx.emit("v", lambda v, rv=rv, half0=half0: v.tensor_tensor(
                    qh[:, :, half0:half0 + rv],
                    ytmp[:, :, 0:rv], ytmp[:, :, 128 - rv:128], ALU.mult),
                    dep=True, mark=True)
            ctx.emit("v", lambda v, s2=s: v.reduce_sum(
                qn[:, s2, :], qh[:, :, :], axis=AX.X), dep=True, mark=True)
            for pi_, (s_p, t) in enumerate(SPAIR):
                if s_p != s:
                    continue
                for hb in range(2):
                    h0 = hb * 8
                    ctx.emit("v", lambda v, t=t, h0=h0: v.tensor_tensor(
                        scr[:, :, :], qh[:, h0:h0 + 8, :],
                        kh[:, t, h0:h0 + 8, :], ALU.mult),
                        dep=True, mark=True)
                    ctx.emit("v", lambda v, pi_=pi_, h0=h0: v.reduce_sum(
                        Sr[:, pi_, h0:h0 + 8, :], scr[:, :, :], axis=AX.X),
                        dep=True, mark=True)

        ctx.emit("v", lambda v: v.engine_nop(), dep=True, mark=True)
        vsync = ctx._last_inc["v"]
        ctx.emit("v", lambda v: v.reciprocal(kn.rearrange("p s h -> p (s h)"),
                                             kn.rearrange("p s h -> p (s h)")),
                 wait=[vsync], dep=True, mark=True)
        ctx.emit("v", lambda v: v.reciprocal(qn.rearrange("p s h -> p (s h)"),
                                             qn.rearrange("p s h -> p (s h)")),
                 dep=True, mark=True)
        for pi_, (t, r) in enumerate(APAIR):
            ctx.emit("v", lambda v, pi_=pi_, t=t: v.tensor_tensor(
                Abuf[:, pi_, :], Abuf[:, pi_, :], kn[:, t, :], ALU.mult),
                dep=True, mark=True)
            ctx.emit("v", lambda v, pi_=pi_, r=r: v.tensor_tensor(
                Abuf[:, pi_, :], Abuf[:, pi_, :], kn[:, r, :], ALU.mult),
                dep=True, mark=True)
        for pi_, (s, t) in enumerate(SPAIR):
            ctx.emit("v", lambda v, pi_=pi_, s=s: v.tensor_tensor(
                Sbuf[:, pi_, :], Sbuf[:, pi_, :], qn[:, s, :], ALU.mult),
                dep=True, mark=True)
            ctx.emit("v", lambda v, pi_=pi_, t=t: v.tensor_tensor(
                Sbuf[:, pi_, :], Sbuf[:, pi_, :], kn[:, t, :], ALU.mult),
                dep=True, mark=True)
            if t == 0:
                ctx.emit("v", lambda v, pi_=pi_, s=s: v.tensor_copy(
                    dn[:, s, :], Sbuf[:, pi_, :]), dep=True, mark=True)
            else:
                ctx.emit("v", lambda v, pi_=pi_, s=s: v.tensor_tensor(
                    dn[:, s, :], dn[:, s, :], Sbuf[:, pi_, :], ALU.add),
                    dep=True, mark=True)
        ctx.emit("v", lambda v: v.memset(kd[:, 0, :], 1.0), dep=True, mark=True)
        for t in range(1, 8):
            for r in range(t):
                pi_ = APAIR.index((t, r))
                if r == 0:
                    ctx.emit("v", lambda v, t=t, pi_=pi_: v.tensor_copy(
                        kd[:, t, :], Abuf[:, pi_, :]), dep=True, mark=True)
                else:
                    ctx.emit("v", lambda v, t=t, pi_=pi_: v.tensor_tensor(
                        kd[:, t, :], kd[:, t, :], Abuf[:, pi_, :], ALU.add),
                        dep=True, mark=True)
        ctx.emit("v", lambda v: v.tensor_scalar(
            fT.rearrange("p s h -> p (s h)"), kd.rearrange("p s h -> p (s h)"),
            1e-5, None, ALU.add), dep=True, mark=True)
        ctx.emit("v", lambda v: v.reciprocal(
            fT.rearrange("p s h -> p (s h)"), fT.rearrange("p s h -> p (s h)")),
            dep=True, mark=True)
        vsync = ctx._last_inc["v"]
        for s in range(8):
            ctx.emit("s", lambda s_, s2=s, ph=ph: s_.activation(
                bP[:, s2, :],
                qkv_s[:, s2, :].rearrange("p (h f) -> p h f", f=193)[:, :, 192],
                AF.Sigmoid), wait=[vsync] if s == 0 else (), inc=[(smD, 1)])
            smD.tick()
        sig_done = smD.n
        ctx.emit("v", lambda v: v.tensor_tensor(
            bP.rearrange("p s h -> p (s h)"), bP.rearrange("p s h -> p (s h)"),
            kd.rearrange("p s h -> p (s h)"), ALU.mult),
            wait=[(smD, sig_done)], dep=True, mark=True)
        for pi_, (t, r) in enumerate(APAIR):
            ctx.emit("v", lambda v, pi_=pi_, t=t: v.tensor_tensor(
                Abuf[:, pi_, :], Abuf[:, pi_, :], fT[:, t, :], ALU.mult),
                dep=True, mark=True)
            ctx.emit("v", lambda v, pi_=pi_, r=r: v.tensor_tensor(
                Abuf[:, pi_, :], Abuf[:, pi_, :], fT[:, r, :], ALU.mult),
                dep=True, mark=True)
        for pi_, (s, t) in enumerate(SPAIR):
            ctx.emit("v", lambda v, pi_=pi_, t=t: v.tensor_tensor(
                Sbuf[:, pi_, :], Sbuf[:, pi_, :], fT[:, t, :], ALU.mult),
                dep=True, mark=True)
        ctx.emit("v", lambda v: v.tensor_scalar(
            dn.rearrange("p s h -> p (s h)"), dn.rearrange("p s h -> p (s h)"),
            1e-5, None, ALU.add), dep=True, mark=True)
        ctx.emit("v", lambda v: v.reciprocal(
            dn.rearrange("p s h -> p (s h)"), dn.rearrange("p s h -> p (s h)")),
            dep=True, mark=True)
        ctx.emit("v", lambda v: v.tensor_scalar(
            dn.rearrange("p s h -> p (s h)"), dn.rearrange("p s h -> p (s h)"),
            0.125, None, ALU.mult), dep=True, mark=True)

        def vall(t, h0):
            return qkv_s[:, t, :].rearrange("p (h f) -> p h f",
                                            f=193)[:, h0:h0 + 8, 128:192]

        def bcast(buf2d, h0):
            return buf2d[:, h0:h0 + 8].rearrange(
                "p (h one) -> p h one", one=1).broadcast_to((128, 8, 64))

        for h0 in (0, 8):
            for t in range(8):
                if t == 0:
                    ctx.emit("v", lambda v, h0=h0: v.tensor_tensor(
                        U[:, 0, h0:h0 + 8, :], vall(0, h0),
                        bcast(bP[:, 0, :], h0), ALU.mult),
                        dep=True, mark=True)
                else:
                    for r in range(t):
                        pi_ = APAIR.index((t, r))
                        if r == 0:
                            ctx.emit("v", lambda v, pi_=pi_, h0=h0: v.tensor_tensor(
                                acc[:, :, :], U[:, 0, h0:h0 + 8, :],
                                bcast(Abuf[:, pi_, :], h0), ALU.mult),
                                dep=True, mark=True)
                        else:
                            ctx.emit("v", lambda v, r=r, pi_=pi_, h0=h0: v.tensor_tensor(
                                tmp64[:, :, :], U[:, r, h0:h0 + 8, :],
                                bcast(Abuf[:, pi_, :], h0), ALU.mult),
                                dep=True, mark=True)
                            ctx.emit("v", lambda v: v.tensor_tensor(
                                acc[:, :, :], acc[:, :, :], tmp64[:, :, :],
                                ALU.add), dep=True, mark=True)
                    ctx.emit("v", lambda v, t=t, h0=h0: v.tensor_tensor(
                        tmp64[:, :, :], vall(t, h0), acc[:, :, :],
                        ALU.subtract), dep=True, mark=True)
                    ctx.emit("v", lambda v, t=t, h0=h0: v.tensor_tensor(
                        U[:, t, h0:h0 + 8, :], tmp64[:, :, :],
                        bcast(bP[:, t, :], h0), ALU.mult),
                        dep=True, mark=True)
            for s in range(8):
                for t in range(s + 1):
                    pi_ = SPAIR.index((s, t))
                    if t == 0:
                        ctx.emit("v", lambda v, pi_=pi_, h0=h0: v.tensor_tensor(
                            acc[:, :, :], U[:, 0, h0:h0 + 8, :],
                            bcast(Sbuf[:, pi_, :], h0), ALU.mult),
                            dep=True, mark=True)
                    else:
                        ctx.emit("v", lambda v, t=t, pi_=pi_, h0=h0: v.tensor_tensor(
                            tmp64[:, :, :], U[:, t, h0:h0 + 8, :],
                            bcast(Sbuf[:, pi_, :], h0), ALU.mult),
                            dep=True, mark=True)
                        ctx.emit("v", lambda v: v.tensor_tensor(
                            acc[:, :, :], acc[:, :, :], tmp64[:, :, :],
                            ALU.add), dep=True, mark=True)
                ctx.emit("v", lambda v, s=s, ph=ph, h0=h0: v.tensor_tensor(
                    OUT[:, 2 * s + ph, :].rearrange(
                        "p (h f) -> p h f", f=64)[:, h0:h0 + 8, :],
                    acc[:, :, :], bcast(dn[:, s, :], h0), ALU.mult),
                    dep=True, mark=True)

    ctx.barrier()
    ctx.free_to(m_ln)

    m4 = ctx.mark()
    loT = ctx.sb("loT", [128, 8, S], BF16)
    woT = ctx.sb("woT", [128, 8, D], BF16)
    gB = ctx.sb("gB", [128, 1024], F32)
    bB = ctx.sb("bB", [128, 1024], F32)
    grow = ctx.sb("grow", [1, 1024], F32)
    onesf1 = ctx.sb("onesf1", [1, 128], F32)
    xin = ctx.sb("xin", [128, 2, 1024], F32)
    xx = ctx.sb("xx", [128, 2, 1024], F32)
    st8 = ctx.sb("st8", [128, 8], F32)
    scrq = ctx.sb("scrq", [128, 1024], BF16)
    psT = ctx.ps("psT", [128, 2, 1024], BF16)
    psX = ctx.ps("psX", [128, 2, 512], F32)
    psB = ctx.ps("psB", [128, 1024], F32)

    smtr2 = ctx.sem("tps")
    smwo = ctx.sem("wl_")
    smbb = ctx.sem("bc2")
    for mc in range(8):
        ctx.emit("y", lambda y, mc=mc: y.dma_start(
            woT[:, mc, :], wobf[:, mc * 128:(mc + 1) * 128], transpose=True),
            wait=[(smw, wcast_all)] if mc == 0 else (), inc=[(smwo, 16)])
        smwo.tick(16)
    wo_done = smwo.n
    ctx.emit("v", lambda v: v.memset(onesf1[:, :], 1.0), mark=True)
    ones_ok = ctx._last_inc["v"]
    for name, srcap, dstb in (("g", ln_g, gB), ("b", ln_b, bB)):
        ctx.dma("y", grow[:, :], srcap[None, :], wait=[(smbb, smbb.n)],
                inc=[(smbb, 16)])
        smbb.tick(16)
        for hf in range(2):
            ctx.emit("t", lambda t, hf=hf: t.matmul(
                psB[:, hf * 512:(hf + 1) * 512], onesf1[:, :],
                grow[:, hf * 512:(hf + 1) * 512], start=True, stop=True),
                wait=[(smbb, smbb.n), ones_ok] if hf == 0 else (),
                inc=[(smbb, 1)] if hf == 1 else [])
        smbb.tick()
        ctx.emit("v", lambda v, dstb=dstb: v.tensor_copy(dstb[:, :], psB[:, :]),
                 wait=[(smbb, smbb.n)], inc=[(smbb, 1)])
        smbb.tick()
    bcast_done = smbb.n

    for i in range(NTILE):
        for cch in range(8):
            k = i * 8 + cch
            par = k % 2
            w = []
            if k >= 2:
                w.append((smtr2, smtr2.n - 1))
            ctx.emit("t", lambda t, i=i, cch=cch, par=par: t.transpose(
                psT[:, par, 0:128], OUT[:, i, cch * 128:(cch + 1) * 128],
                identbf[:, :]), wait=w, inc=[(smtr2, 1)])
            smtr2.tick()
            ctx.emit("s", lambda s, i=i, cch=cch, par=par: s.copy(
                loT[:, cch, i * 128:(i + 1) * 128], psT[:, par, 0:128]),
                wait=[(smtr2, smtr2.n)], inc=[(smtr2, 1)])
            smtr2.tick()
    tr2_done = smtr2.n

    smXm = ctx.sem("mm2")
    smXe = ctx.sem("ev2")
    smst = ctx.sem("t2s")
    smxl2 = [ctx.sem(f"lx{j}") for j in range(2)]
    smxs2 = [ctx.sem(f"sx{j}") for j in range(2)]
    for gt in range(NTILE):
        slot = gt % 2
        smxl = smxl2[slot]
        smxs = smxs2[slot]
        wx = [(smxs, smxs.n)] if gt >= 2 else []
        ctx.dma("y", xin[:, slot, :], xv[:, gt, :], wait=wx, inc=[(smxl, 16)])
        xl = smxl.tick(16)
        for oh in range(2):
            it2 = gt * 2 + oh
            par = it2 % 2
            w2 = [(smwo, wo_done), (smtr2, tr2_done)]
            if it2 >= 2:
                w2.append((smXe, smXe.n - 1))
            for c2 in range(8):
                ctx.emit("t", lambda t, gt=gt, oh=oh, c2=c2, par=par: t.matmul(
                    psX[:, par, :], loT[:, c2, gt * 128:(gt + 1) * 128],
                    woT[:, c2, oh * 512:(oh + 1) * 512],
                    start=(c2 == 0), stop=(c2 == 7)),
                    wait=w2 if c2 == 0 else (),
                    inc=[(smXm, 1)] if c2 == 7 else [])
            mmw = smXm.tick()
            ctx.emit("v", lambda v, gt=gt: v.tensor_tensor(
                st8[:, 0:1], mrs[:, gt, 0:1], mrs[:, gt, 1:2], ALU.mult),
                wait=[(smXm, mmw), (smxl, xl)], dep=(it2 > 0), mark=True)
            ctx.emit("v", lambda v: v.tensor_scalar(
                st8[:, 0:1], st8[:, 0:1], -1.0, None, ALU.mult),
                dep=True, mark=True)
            ctx.emit("v", lambda v, slot=slot, gt=gt, oh=oh: v.tensor_scalar(
                xx[:, slot, oh * 512:(oh + 1) * 512],
                xin[:, slot, oh * 512:(oh + 1) * 512],
                mrs[:, gt, 1:2], st8[:, 0:1], ALU.mult, ALU.add),
                dep=True, mark=True)
            ctx.emit("v", lambda v, slot=slot, oh=oh, par=par: v.tensor_tensor(
                xx[:, slot, oh * 512:(oh + 1) * 512],
                xx[:, slot, oh * 512:(oh + 1) * 512],
                psX[:, par, :], ALU.add), dep=True, inc=[(smXe, 1)])
            smXe.tick()
        ctx.emit("v", lambda v, slot=slot: v.scalar_tensor_tensor(
            scrq[:, :], xx[:, slot, :], 1.0,
            xx[:, slot, :], ALU.mult, ALU.mult, accum_out=st8[:, 1:2]),
            dep=True, mark=True)
        ctx.emit("v", lambda v, slot=slot: v.reduce_sum(
            st8[:, 2:3], xx[:, slot, :], axis=AX.X), dep=True, inc=[(smst, 1)])
        t0 = smst.tick()
        ctx.emit("s", lambda s: s.mul(st8[:, 3:4], st8[:, 2:3], 1.0 / 1024.0),
                 wait=[(smst, t0)], mark=True)
        ctx.emit("s", lambda s: s.mul(st8[:, 4:5], st8[:, 1:2], 1.0 / 1024.0),
                 dep=True, mark=True)
        ctx.emit("s", lambda s: s.activation(st8[:, 5:6], st8[:, 3:4], AF.Square),
                 dep=True, inc=[(smst, 1)])
        t1 = smst.tick()
        ctx.emit("v", lambda v: v.tensor_tensor(st8[:, 4:5], st8[:, 4:5],
                                                st8[:, 5:6], ALU.subtract),
                 wait=[(smst, t1)], inc=[(smst, 1)])
        t2 = smst.tick()
        ctx.emit("s", lambda s, _e=ctx._eps: s.activation(
            st8[:, 4:5], st8[:, 4:5], AF.Sqrt, bias=_e[:, :]),
                 wait=[(smst, t2)], inc=[(smst, 1)])
        t3 = smst.tick()
        ctx.emit("v", lambda v: v.reciprocal(st8[:, 5:6], st8[:, 4:5]),
                 wait=[(smst, t3)], mark=True)
        ctx.emit("v", lambda v, slot=slot: v.tensor_scalar(
            xx[:, slot, :], xx[:, slot, :], st8[:, 3:4], st8[:, 5:6],
            ALU.subtract, ALU.mult), dep=True, mark=True)
        ctx.emit("v", lambda v, slot=slot: v.tensor_tensor(
            xx[:, slot, :], xx[:, slot, :], gB[:, :], ALU.mult),
            dep=True, mark=True)
        ctx.emit("v", lambda v, slot=slot: v.tensor_tensor(
            xx[:, slot, :], xx[:, slot, :], bB[:, :], ALU.add),
            dep=True, mark=True)
        ctx.emit("v", lambda v, slot=slot: v.tensor_tensor(
            xin[:, slot, :], xin[:, slot, :], xx[:, slot, :], ALU.add),
            dep=True, inc=[(smXe, 1)])
        smXe.tick()
        ctx.dma("g", xov[:, gt, :], xin[:, slot, :],
                wait=[(smXe, smXe.n)], inc=[(smxs, 16)])
        smxs.tick(16)

    ctx.emit("y", None, wait=[(smxs2[0], smxs2[0].n), (smxs2[1], smxs2[1].n)])
    ctx.barrier(all_sems=True)
    ctx.free_to(_m_all)




ECAP = 5121
EPAD = 5248
NEXP = 8


def emit_moe_route(ctx, ext):
    nc = ctx.nc
    ctx._eps_t = None
    _m_all = ctx.mark()
    x = ext["x"]
    rw = ext["rw"]
    cid = ext["cid"]
    h_out = ext["h_out"]
    einfo = ext["einfo"]
    counts = ext["counts"]
    idxp = ext["idxp"]

    c_ident = ctx.il0(np_ident(), "identsrc")
    c_iota8 = ctx.il0(
        np.tile(np.arange(8, dtype=np.float32)[None, :], (128, 1)), "iota8c")
    c_ustrict = ctx.il0(
        np.triu(np.ones((128, 128), np.float32), 1).T * 0 +
        np.tril(np.ones((128, 128), np.float32), -1).T, "ustrictc")
    c_iotap = ctx.il0(np.arange(128, dtype=np.float32)[:, None], "iotapc")

    xv = x.rearrange("(n p) d -> p n d", p=128)
    hv = h_out.rearrange("(n p) d -> p n d", p=128)
    ev = einfo.rearrange("(n p) e -> p n e", p=128)

    identf = ctx.sb("identf", [128, 128], F32)
    iota8 = ctx.sb("iota8", [128, 8], F32)
    iotap = ctx.sb("iotap", [128, 1], F32)
    ustr = ctx.sb("ustr", [128, 128], BF16)
    rws = ctx.sb("rws", [8, D], F32)
    rwT = ctx.sb("rwT", [128, 8, 8], F32)
    cid1 = ctx.sb("cid1", [1, 1], F32)
    cidB = ctx.sb("cidB", [128, 1], F32)
    onesf1 = ctx.sb("onesf1", [1, 128], F32)
    onesb = ctx.sb("onesb", [1, 128], BF16)
    onescol = ctx.sb("onescol", [128, 1], BF16)
    xs = ctx.sb("xs", [128, NTILE, 1024], F32)
    xT = ctx.sb("xT", [128, 8, 128], F32)
    st = ctx.sb("stx", [128, 8], F32)
    probs = ctx.sb("probs", [128, 8], F32)
    mx8 = ctx.sb("mx8", [128, 8], F32)
    oh0 = ctx.sb("oh0", [128, 8], F32)
    oh1 = ctx.sb("oh1", [128, 8], F32)
    ohc = ctx.sb("ohc", [128, 8], BF16)
    einf = ctx.sb("einf", [128, NTILE, 8], F32)
    offrow = ctx.sb("offrow", [1, 8], F32)
    offhl = ctx.sb("offhl", [1, 2, 8], BF16)
    cu = ctx.sb("cu", [128, 8], F32)
    dst0 = ctx.sb("dst0", [128, 2], I32)
    gidc = ctx.sb("gidc", [128, 1], F32)
    zz = ctx.sb("zz", [128, 1024], F32)
    hbuf = ctx.sb("hbuf", [128, 1024], BF16)
    psT = ctx.ps("psT", [128, 2, 512], F32)
    psL = ctx.ps("psL", [128, 512], F32)
    psC = ctx.ps("psC", [128, 512], F32)

    ctx._eps0 = ctx.sb("eps0", [128, 1], F32)
    _e0 = ctx._eps0
    ctx.emit("v", lambda v, _e0=_e0: v.memset(_e0[:, :], 1e-5), mark=True)
    smc = ctx.sem("consts")
    smx3 = [ctx.sem(f"lx{j}") for j in range(3)]
    smt = ctx.sem("tps")
    smm = ctx.sem("mms")
    sma = ctx.sem("acts")
    smsc = ctx.sem("scat")
    smst = ctx.sem("sx0")

    ctx.dma("g", identf[:, :], c_ident.ap()[:, :], inc=[(smc, 16)])
    smc.tick(16)
    ctx.dma("g", iota8[:, :], c_iota8.ap()[:, :], inc=[(smc, 16)])
    smc.tick(16)
    ctx.dma("g", iotap[:, :], c_iotap.ap()[:, :], inc=[(smc, 16)])
    smc.tick(16)
    ctx.dma("g", ustr[:, :], c_ustrict.ap()[:, :], inc=[(smc, 16)])
    smc.tick(16)
    ctx.dma("g", rws[:, :], rw[:, :], inc=[(smc, 16)])
    smc.tick(16)
    ctx.dma("g", cid1[:, :], cid[None, :], inc=[(smc, 16)])
    smc.tick(16)
    ctx.dma("y", xs[:, :, :], xv, inc=[(smx3[0], 16)])
    xs_ok = smx3[0].tick(16)
    consts_ok = smc.n

    ctx.emit("v", lambda v: v.memset(onesf1[:, :], 1.0), mark=True)
    ctx.emit("v", lambda v: v.memset(onesb[:, :], 1.0), dep=True, mark=True)
    ctx.emit("v", lambda v: v.memset(onescol[:, :], 1.0), dep=True, mark=True)
    ctx.emit("v", lambda v: v.memset(einf.rearrange("p n e -> p (n e)"), 0.0),
             dep=True, mark=True)
    ctx.emit("v", lambda v: v.memset(offrow[:, :], 0.0), dep=True, mark=True)
    ctx.emit("v", lambda v: v.memset(zz[:, :], 0.0), dep=True, mark=True)
    ones_ok = ctx._last_inc["v"]

    for mc in range(8):
        ctx.emit("t", lambda t, mc=mc: t.transpose(
            psT[:, mc % 2, 0:8], rws[:, mc * 128:(mc + 1) * 128], identf[0:8, 0:8]),
            wait=[(smc, consts_ok), (smt, smt.n - 1)] if mc >= 2 else
            [(smc, consts_ok)], inc=[(smt, 1)])
        smt.tick()
        ctx.emit("v", lambda v, mc=mc: v.tensor_copy(rwT[:, mc, :],
                                                     psT[:, mc % 2, 0:8]),
                 wait=[(smt, smt.n)], inc=[(smt, 1)])
        smt.tick()
    ctx.emit("t", lambda t: t.matmul(psT[:, 0, 0:1], onesf1[:, :], cid1[:, :],
                                     start=True, stop=True),
             wait=[(smc, consts_ok), (smt, smt.n)], inc=[(smt, 1)])
    smt.tick()
    ctx.emit("v", lambda v: v.tensor_copy(cidB[:, :], psT[:, 0, 0:1]),
             wait=[(smt, smt.n)], inc=[(smt, 1)])
    smt.tick()
    prep_ok = smt.n

    idxf = idxp.rearrange("e n -> (e n)")
    total = NEXP * (EPAD + 4096)
    off0 = 0
    nz = 0
    while off0 < total:
        wdt = min(128 * 1024, total - off0)
        if wdt % 128 == 0:
            ctx.dma("g", idxf[off0:off0 + wdt].rearrange("(p a) -> p a", p=128),
                    zz[:, 0:wdt // 128], wait=[ones_ok], inc=[(smsc, 16)])
        else:
            ctx.dma("g", idxf[off0:off0 + wdt][None, :], zz[0:1, 0:wdt],
                    wait=[ones_ok], inc=[(smsc, 16)])
        smsc.tick(16)
        nz += 1
        off0 += wdt
    zeroed = smsc.n

    h_hook = ext.get("h_hook")
    sst0 = smst.n
    for i in range(NTILE):
        st4 = st
        xt = xs[:, i, :]
        ctx.emit("v", lambda v, xt=xt: v.scalar_tensor_tensor(
            hbuf[:, :], xt, 1.0, xt, ALU.mult, ALU.mult, accum_out=st[:, 1:2]),
            wait=[(smx3[0], xs_ok), (smst, smst.n)], dep=True, mark=True)
        ctx.emit("v", lambda v, xt=xt: v.reduce_sum(st[:, 0:1], xt, axis=AX.X),
                 dep=True, inc=[(sma, 1)])
        t0 = sma.tick()
        ctx.emit("s", lambda s: s.mul(st[:, 2:3], st[:, 0:1], 1.0 / 1024.0),
                 wait=[(sma, t0)], mark=True)
        ctx.emit("s", lambda s: s.mul(st[:, 0:1], st[:, 1:2], 1.0 / 1024.0),
                 dep=True, mark=True)
        ctx.emit("s", lambda s: s.activation(st[:, 1:2], st[:, 2:3], AF.Square),
                 dep=True, inc=[(sma, 1)])
        t1 = sma.tick()
        ctx.emit("v", lambda v: v.tensor_tensor(st[:, 0:1], st[:, 0:1],
                                                st[:, 1:2], ALU.subtract),
                 wait=[(sma, t1)], inc=[(sma, 1)])
        t2 = sma.tick()
        ctx.emit("s", lambda s: s.activation(st[:, 0:1], st[:, 0:1], AF.Sqrt,
                                             bias=_e0[:, :]),
                 wait=[(sma, t2)], inc=[(sma, 1)])
        t3 = sma.tick()
        ctx.emit("v", lambda v: v.reciprocal(st[:, 3:4], st[:, 0:1]),
                 wait=[(sma, t3)], mark=True)
        ctx.emit("v", lambda v, xt=xt: v.tensor_scalar(
            xt, xt, st[:, 2:3], None, ALU.subtract), dep=True, mark=True)
        ctx.emit("v", lambda v, xt=xt: v.tensor_scalar(
            hbuf[:, :], xt, st[:, 3:4], None, ALU.mult), dep=True,
            inc=[(sma, 1)])
        h_ok = sma.tick()
        ctx.dma("g", hv[:, i, :], hbuf[:, :], wait=[(sma, h_ok)],
                inc=[(smst, 16)])
        smst.tick(16)
        for cch in range(8):
            k = i * 8 + cch
            par = k % 2
            w = [(sma, h_ok)]
            if k >= 2:
                w.append((smt, smt.n - 1))
            ctx.emit("t", lambda t, xt=xt, cch=cch, par=par: t.transpose(
                psT[:, par, 0:128], xt[:, cch * 128:(cch + 1) * 128],
                identf[:, :]), wait=w, inc=[(smt, 1)])
            smt.tick()
            ctx.emit("v", lambda v, cch=cch, par=par: v.tensor_copy(
                xT[:, cch, :], psT[:, par, 0:128]),
                wait=[(smt, smt.n)], inc=[(smt, 1)])
            smt.tick()
        tr_ok = smt.n
        for mc in range(8):
            ctx.emit("t", lambda t, mc=mc: t.matmul(
                psL[:, 0:8], xT[:, mc, :], rwT[:, mc, :],
                start=(mc == 0), stop=(mc == 7)),
                wait=[(smt, tr_ok), (smt if False else sma, h_ok)] if mc == 0 else (),
                inc=[(smm, 1)] if mc == 7 else [])
        mmw = smm.tick()
        ctx.emit("v", lambda v: v.tensor_scalar(
            probs[:, :], psL[:, 0:8], st[:, 3:4], None, ALU.mult),
            wait=[(smm, mmw)], dep=True, mark=True)
        ctx.emit("v", lambda v: v.reduce_max(st[:, 4:5], probs[:, :],
                                             axis=AX.X, negate=True),
                 dep=True, inc=[(sma, 1)])
        t4 = sma.tick()
        ctx.emit("s", lambda s: s.activation(probs[:, :], probs[:, :], AF.Exp,
                                             bias=st[:, 4:5],
                                             accum_out=st[:, 5:6]),
                 wait=[(sma, t4)], inc=[(sma, 1)])
        t5 = sma.tick()
        ctx.emit("v", lambda v: v.reciprocal(st[:, 6:7], st[:, 5:6]),
                 wait=[(sma, t5)], mark=True)
        ctx.emit("v", lambda v: v.tensor_scalar(
            probs[:, :], probs[:, :], st[:, 6:7], None, ALU.mult),
            dep=True, mark=True)
        ctx.emit("v", lambda v: v.max(mx8[:, :], probs[:, :]), dep=True,
                 mark=True)
        ctx.emit("v", lambda v: v.tensor_scalar(
            oh0[:, :], probs[:, :], mx8[:, 0:1], None, ALU.is_equal),
            dep=True, mark=True)
        ctx.emit("v", lambda v: v.tensor_scalar(
            oh1[:, :], probs[:, :], mx8[:, 1:2], None, ALU.is_equal),
            dep=True, mark=True)
        ctx.emit("v", lambda v: v.tensor_tensor(
            st[:, 7:8], mx8[:, 0:1], mx8[:, 1:2], ALU.add), dep=True, mark=True)
        ctx.emit("v", lambda v: v.reciprocal(st[:, 4:5], st[:, 7:8]),
                 dep=True, mark=True)
        ctx.emit("v", lambda v, i=i: v.tensor_tensor(
            einf[:, i, 2:3], mx8[:, 0:1], st[:, 4:5], ALU.mult),
            dep=True, mark=True)
        ctx.emit("v", lambda v, i=i: v.tensor_tensor(
            einf[:, i, 3:4], mx8[:, 1:2], st[:, 4:5], ALU.mult),
            dep=True, mark=True)
        ctx.emit("v", lambda v: v.tensor_tensor(cu[:, :], oh0[:, :],
                                                iota8[:, :], ALU.mult),
                 dep=True, mark=True)
        ctx.emit("v", lambda v, i=i: v.reduce_sum(einf[:, i, 0:1], cu[:, :],
                                                  axis=AX.X), dep=True, mark=True)
        ctx.emit("v", lambda v: v.tensor_tensor(cu[:, :], oh1[:, :],
                                                iota8[:, :], ALU.mult),
                 dep=True, mark=True)
        ctx.emit("v", lambda v, i=i: v.reduce_sum(einf[:, i, 1:2], cu[:, :],
                                                  axis=AX.X), dep=True, mark=True)
        ctx.emit("v", lambda v: v.tensor_tensor(ohc[:, :], oh0[:, :],
                                                oh1[:, :], ALU.add),
                 dep=True, inc=[(sma, 1)])
        oh_ok = sma.tick()
        ctx.emit("v", lambda v: v.tensor_copy(offhl[:, 0, :], offrow[:, :]),
                 dep=True, mark=True)
        ctx.emit("v", lambda v: v.tensor_tensor(
            st[0:1, 0:8], offrow[:, :], offhl[:, 0, :], ALU.subtract),
            dep=True, mark=True)
        ctx.emit("v", lambda v: v.tensor_copy(offhl[:, 1, :], st[0:1, 0:8]),
                 dep=True, inc=[(sma, 1)])
        ohl_ok = sma.tick()
        ctx.emit("t", lambda t: t.matmul(psC[:, 0:8], ustr[:, :], ohc[:, :],
                                         start=True, stop=False),
                 wait=[(sma, oh_ok), (sma, ohl_ok)])
        ctx.emit("t", lambda t: t.matmul(psC[:, 0:8], onesb[:, :],
                                         offhl[:, 0, :], start=False, stop=False))
        ctx.emit("t", lambda t: t.matmul(psC[:, 0:8], onesb[:, :],
                                         offhl[:, 1, :], start=False, stop=True),
                 inc=[(smm, 1)])
        mm2 = smm.tick()
        ctx.emit("v", lambda v: v.tensor_tensor(cu[:, :], psC[:, 0:8],
                                                oh0[:, :], ALU.mult),
                 wait=[(smm, mm2)], mark=True)
        ctx.emit("v", lambda v, i=i: v.reduce_sum(einf[:, i, 4:5], cu[:, :],
                                                  axis=AX.X), dep=True, mark=True)
        ctx.emit("v", lambda v: v.tensor_tensor(cu[:, :], psC[:, 0:8],
                                                oh1[:, :], ALU.mult),
                 dep=True, mark=True)
        ctx.emit("v", lambda v, i=i: v.reduce_sum(einf[:, i, 5:6], cu[:, :],
                                                  axis=AX.X), dep=True, mark=True)
        ctx.emit("t", lambda t: t.matmul(psC[0:1, 8:16], onescol[:, :], ohc[:, :],
                                         start=True, stop=True),
                 inc=[(smm, 1)])
        mm3 = smm.tick()
        ctx.emit("v", lambda v: v.tensor_tensor(
            offrow[:, :], offrow[:, :], psC[0:1, 8:16], ALU.add),
            wait=[(smm, mm3)], dep=True, mark=True)
        ctx.emit("v", lambda v, i=i: v.tensor_scalar(
            gidc[:, :], cidB[:, :], 2048.0, float(i * 128 + 1), ALU.mult,
            ALU.add),
            wait=([(smt, prep_ok)] if i == 0 else []) + [(smsc, smsc.n)],
            dep=True, mark=True)
        ctx.emit("v", lambda v: v.tensor_tensor(gidc[:, :], gidc[:, :],
                                                iotap[:, :], ALU.add),
                 dep=True, mark=True)
        for kk in range(2):
            ctx.emit("v", lambda v, i=i, kk=kk: v.tensor_scalar(
                cu[:, 0:1], einf[:, i, kk:kk + 1], float(EPAD + 4096),
                None, ALU.mult), dep=True, mark=True)
            ctx.emit("v", lambda v, i=i, kk=kk: v.tensor_tensor(
                cu[:, 0:1], cu[:, 0:1], einf[:, i, 4 + kk:5 + kk], ALU.add),
                dep=True, mark=True)
            ctx.emit("v", lambda v, kk=kk: v.tensor_copy(
                dst0[:, kk:kk + 1], cu[:, 0:1]), dep=True, inc=[(sma, 1)])
            dsts = sma.tick()
            ctx.emit("g", lambda g, kk=kk: g.indirect_dma_start(
                idxf.rearrange("(n one) -> n one", one=1),
                IndirectOffsetOnAxis(dst0[:, kk:kk + 1], 0),
                gidc[:, :], None,
                bounds_check=ctx.breg(g, total - 1), oob_is_err=False),
                wait=[(sma, dsts), (smsc, zeroed)], inc=[(smsc, 16)])
            smsc.tick(16)
        ctx.dma("g", ev[:, i, :], einf[:, i, :],
                wait=[(sma, sma.n), (ctx._last_inc["v"][0], ctx._last_inc["v"][1])],
                inc=[(smst, 16)])
        smst.tick(16)

    if h_hook is not None:
        h_hook([(smst, smst.n)])
    ctx.dma("g", counts[None, :], offrow[:, :],
            wait=[(ctx._last_inc["v"][0], ctx._last_inc["v"][1])],
            inc=[(smst, 16)])
    smst.tick(16)
    ctx.emit("y", None, wait=[(smst, smst.n), (smsc, smsc.n)])
    ctx.barrier(all_sems=True)
    ctx.free_to(_m_all)




def emit_moe_expert(ctx, ext, fused=False):
    nc = ctx.nc
    ctx._eps_t = None
    _m_all = ctx.mark()
    h_all = ext["h_all"]
    idxp_all = ext["idxp_all"]
    counts_all = ext["counts_all"]
    cid = ext["cid"]
    w1 = ext["w1"]
    w2 = ext["w2"]
    out_d = ext["out_d"]
    w1bf = nc.dram_tensor(ctx.prefix + "w1bf", [D, 4096], BF16, kind="Internal").ap()
    w2bf = nc.dram_tensor(ctx.prefix + "w2bf", [4096, D], BF16, kind="Internal").ap()
    c_ident = ctx.il0(np_ident(), "identsrc")
    c_iotac = ctx.il0(np.arange(8, dtype=np.float32)[:, None], "iota8cc")
    c_iotap = ctx.il0(np.arange(128, dtype=np.float32)[:, None], "iotapc")

    PART = EPAD + 4096
    PSTRIDE = (NEXP * PART) if fused else PART

    identf = ctx.sb("identf", [128, 128], F32)
    identb = ctx.sb("identb", [128, 128], BF16)
    cnt = ctx.sb("cnt", [8, 8], F32)
    cntT = ctx.sb("cntT", [8, 8], F32)
    ohE = ctx.sb("ohE", [8, 1], F32)
    iota8c = ctx.sb("iota8c", [8, 1], F32)
    iotap = ctx.sb("iotap", [128, 1], F32)
    ones8 = ctx.sb("ones8", [1, 8], F32)
    onesf1 = ctx.sb("onesf1", [1, 128], F32)
    cid1 = ctx.sb("cid1", [1, 1], F32)
    prefB = ctx.sb("prefB", [128, 9], F32)
    pref = ctx.sb("pref", [1, 9], F32)
    srci = ctx.sb("srci", [128, 41], F32)
    srcii = ctx.sb("srcii", [128, 41], I32)
    gidx = ctx.sb("gidx", [128, 41], F32)
    gidxi = ctx.sb("gidxi", [128, 41], I32)
    scr41 = ctx.sb("scr41", [128, 41], F32)
    scr41b = ctx.sb("scr41b", [128, 41], F32)
    idxm = ctx.sb("idxm", [128, 41], F32)
    w1S = ctx.sb("w1S", [128, 8, 4096], BF16)
    w2S = ctx.sb("w2S", [128, 32, 1024], BF16)
    gath = ctx.sb("gath", [128, 4, 1024], BF16)
    bufT = ctx.sb("bufT", [128, 8, 512], BF16)
    h1sb = ctx.sb("h1sb", [128, 32, 512], BF16)
    orow = ctx.sb("orow", [128, 2, 512], BF16)
    psq = ctx.ps("psqE", [128, 2, 512], F32)
    psT = ctx.ps("psTE", [128, 2, 1024], BF16)
    psO = ctx.ps("psOE", [128, 2, 512], F32)

    smc = ctx.sem("consts")
    smw = ctx.sem("gc_")
    smwl = ctx.sem("wl_")
    smix = ctx.sem("scat")
    smt = ctx.sem("tps")
    smm = ctx.sem("mms")
    sma = ctx.sem("acts")
    smo = ctx.sem("ev3")
    smg4 = [ctx.sem(f"gt{j}") for j in range(4)]
    smst2 = [ctx.sem(f"sx{j}") for j in range(2)]

    for dst, srcw in ((identf, c_ident.ap()), (iota8c, c_iotac.ap()),
                      (iotap, c_iotap.ap())):
        ctx.dma("g", dst[:, :], srcw[:, :], inc=[(smc, 16)])
        smc.tick(16)
    ctx.dma("g", identb[:, :], c_ident.ap()[:, :], inc=[(smc, 16)])
    smc.tick(16)
    smcy = ctx.sem("lx2")
    ctx.dma("y", cnt[:, :], counts_all[:, :], inc=[(smcy, 16)])
    smcy.tick(16)
    ctx.dma("y", cid1[:, :], cid[None, :], inc=[(smcy, 16)])
    smcy.tick(16)
    consts_ok = smc.n
    ctx.dma("g", w1bf[:, :], w1[:, :], inc=[(smw, 16)])
    smw.tick(16)
    ctx.dma("g", w2bf[:, :], w2[:, :], inc=[(smw, 16)])
    smw.tick(16)
    for mc in range(8):
        ctx.dma("y", w1S[:, mc, :], w1bf[mc * 128:(mc + 1) * 128, :],
                wait=[(smw, smw.n)] if mc == 0 else (), inc=[(smwl, 16)])
        smwl.tick(16)
    for fc in range(32):
        ctx.dma("y", w2S[:, fc, :], w2bf[fc * 128:(fc + 1) * 128, :],
                inc=[(smwl, 16)])
        smwl.tick(16)
    w_ok = smwl.n

    ctx.emit("v", lambda v: v.memset(ones8[:, :], 1.0), mark=True)
    ctx.emit("v", lambda v: v.memset(onesf1[:, :], 1.0), dep=True, mark=True)
    ones_ok = ctx._last_inc["v"]

    ctx.emit("t", lambda t: t.transpose(psq[0:8, 0, 0:8], cnt[:, :],
                                        identf[0:8, 0:8]),
             wait=[(smc, consts_ok), (smcy, smcy.n)], inc=[(smt, 1)])
    smt.tick()
    ctx.emit("v", lambda v: v.tensor_copy(cntT[:, :], psq[0:8, 0, 0:8]),
             wait=[(smt, smt.n)], inc=[(smt, 1)])
    smt.tick()
    ctx.emit("t", lambda t: t.matmul(psq[0:8, 1, 0:1], ones8[:, :], cid1[:, :],
                                     start=True, stop=True),
             wait=[ones_ok, (smt, smt.n)], inc=[(smt, 1)])
    smt.tick()
    ctx.emit("v", lambda v: v.tensor_copy(pref[:, 0:1], psq[0:1, 1, 0:1]),
             wait=[(smt, smt.n)], mark=True)
    ctx.emit("v", lambda v: v.tensor_scalar(
        ohE[:, :], iota8c[:, :], psq[0:8, 1, 0:1], None, ALU.is_equal),
        dep=True, inc=[(smt, 1)])
    smt.tick()
    ctx.emit("t", lambda t: t.matmul(psq[0:1, 0, 0:8], ohE[:, :], cntT[:, :],
                                     start=True, stop=True),
             wait=[(smt, smt.n)], inc=[(smt, 1)])
    smt.tick()
    ctx.emit("v", lambda v: v.tensor_copy(pref[:, 1:9], psq[0:1, 0, 0:8]),
             wait=[(smt, smt.n)], mark=True)
    ctx.emit("v", lambda v: v.memset(pref[:, 0:1], 0.0), dep=True, mark=True)
    for c in range(1, 8):
        ctx.emit("v", lambda v, c=c: v.tensor_tensor(
            pref[:, c:c + 1], pref[:, c - 1:c], pref[:, c:c + 1], ALU.add),
            dep=True, mark=True)
    ctx.emit("v", lambda v: v.engine_nop(), dep=True, inc=[(sma, 1)])
    pref_ok = sma.tick()
    ctx.emit("t", lambda t: t.matmul(psq[:, 0, 0:9], onesf1[:, :], pref[:, 0:9],
                                     start=True, stop=True),
             wait=[(sma, pref_ok)], inc=[(smt, 1)])
    smt.tick()
    ctx.emit("v", lambda v: v.tensor_copy(prefB[:, :], psq[:, 0, 0:9]),
             wait=[(smt, smt.n)], inc=[(sma, 1)])
    prefb_ok = sma.tick()

    c_col = ctx.il0(
        (128.0 * np.arange(41, dtype=np.float32))[None, :].repeat(128, 0), "colcc")
    colc = ctx.sb("colc", [128, 41], F32)
    smc5 = ctx.sem("colcs")
    ctx.dma("g", colc[:, :], c_col.ap()[:, :], inc=[(smc5, 16)])
    smc5.tick(16)
    cst3 = smc5.n
    ctx.emit("v", lambda v: v.tensor_scalar(
        scr41[:, :], colc[:, :], iotap[:, 0:1], None, ALU.add),
        wait=[(smc5, cst3), (sma, prefb_ok)], dep=True, mark=True)
    ctx.emit("v", lambda v: v.memset(srci[:, :], 0.0), dep=True, mark=True)
    ctx.emit("v", lambda v: v.memset(gidx[:, :], 0.0), dep=True, mark=True)
    for c in range(1, 8):
        ctx.emit("v", lambda v, c=c: v.tensor_scalar(
            scr41b[:, :], scr41[:, :], prefB[:, c:c + 1], None, ALU.is_ge),
            dep=True, mark=True)
        ctx.emit("v", lambda v: v.scalar_tensor_tensor(
            srci[:, :], scr41b[:, :], float(PSTRIDE), srci[:, :],
            ALU.mult, ALU.add),
            dep=True, mark=True)
        ctx.emit("v", lambda v, c=c: v.tensor_tensor(
            scr41b[:, 0:1] if False else prefB[:, 8:9], prefB[:, c:c + 1],
            prefB[:, c - 1:c], ALU.subtract), dep=True, mark=True)
        ctx.emit("v", lambda v, c=c: v.scalar_tensor_tensor(
            gidx[:, :], scr41b[:, :], prefB[:, 8:9], gidx[:, :],
            ALU.mult, ALU.add), dep=True, mark=True)
    ctx.emit("v", lambda v: v.tensor_tensor(srci[:, :], srci[:, :],
                                            scr41[:, :], ALU.add),
             dep=True, mark=True)
    if fused:
        cidB128 = ctx.sb("cidB128", [128, 1], F32)
        ctx.emit("t", lambda t: t.matmul(psq[:, 1, 0:1], onesf1[:, :],
                                         cid1[:, :], start=True, stop=True),
                 wait=[(ctx._last_inc["v"][0], ctx._last_inc["v"][1])],
                 inc=[(smt, 1)])
        smt.tick()
        ctx.emit("v", lambda v: v.tensor_copy(cidB128[:, :], psq[:, 1, 0:1]),
                 wait=[(smt, smt.n)], dep=True, mark=True)
        ctx.emit("v", lambda v: v.tensor_scalar(
            cidB128[:, :], cidB128[:, :], float(PART), None, ALU.mult),
            dep=True, mark=True)
        ctx.emit("v", lambda v: v.tensor_scalar(
            srci[:, :], srci[:, :], cidB128[:, 0:1], None, ALU.add),
            dep=True, mark=True)
    ctx.emit("v", lambda v: v.tensor_tensor(srci[:, :], srci[:, :],
                                            gidx[:, :], ALU.subtract),
             dep=True, mark=True)
    ctx.emit("v", lambda v: v.tensor_copy(srcii[:, :], srci[:, :]),
             dep=True, inc=[(sma, 1)])
    srci_ok = sma.tick()

    idxp_flat = idxp_all.rearrange("c n -> (c n)")
    ctx._bndreg = [None, None]

    def bndreg(g, which, val):
        if ctx._bndreg[which] is None:
            r = g.alloc_register(f"bnd{which}")
            g.reg_mov(r, val)
            ctx._bndreg[which] = r
        return ctx._bndreg[which]

    for j in range(41):
        ctx.emit("g", lambda g, j=j: g.indirect_dma_start(
            idxm[:, j:j + 1], None,
            idxp_flat.rearrange("(n one) -> n one", one=1),
            IndirectOffsetOnAxis(srcii[:, j:j + 1], 0),
            bounds_check=ctx.breg(g, NCORE * PSTRIDE - 1), oob_is_err=False),
            wait=[(sma, srci_ok)] if j == 0 else (), inc=[(smix, 16)])
        smix.tick(16)
    idxm_ok = smix.n
    ctx.emit("v", lambda v: v.tensor_scalar(
        gidx[:, :], idxm[:, :], 0.5, None, ALU.is_lt),
        wait=[(smix, idxm_ok)], dep=True, mark=True)
    ctx.emit("v", lambda v: v.tensor_scalar(
        gidx[:, :], gidx[:, :], 1.0e7, None, ALU.mult), dep=True, mark=True)
    ctx.emit("v", lambda v: v.tensor_tensor(gidx[:, :], gidx[:, :],
                                            idxm[:, :], ALU.add),
             dep=True, mark=True)
    ctx.emit("v", lambda v: v.tensor_scalar(
        gidx[:, :], gidx[:, :], 1.0, None, ALU.subtract), dep=True, mark=True)
    ctx.emit("v", lambda v: v.tensor_copy(gidxi[:, :], gidx[:, :]),
             dep=True, inc=[(sma, 1)])
    gidx_ok = sma.tick()

    groups = [(g * 4, 4) for g in range(10)] + [(40, 1)]
    o_hook = ext.get("o_hook")
    _ck = {}
    for gi, (j0, tg) in enumerate(groups):
        prev_mm = smm.n
        for jj in range(tg):
            j = j0 + jj
            slot = jj
            wz = [(sma, gidx_ok)]
            if gi >= 1:
                wz.append((smt, smt.n))
            ctx.emit("v", lambda v, slot=slot: v.memset(gath[:, slot, :], 0.0),
                     wait=wz, dep=True, inc=[(sma, 1)])
            gz = sma.tick()
            ctx.emit("g", lambda g, j=j, slot=slot: g.indirect_dma_start(
                gath[:, slot, :], None, h_all[:, :],
                IndirectOffsetOnAxis(gidxi[:, j:j + 1], 0),
                bounds_check=ctx.breg(g, B * S - 1), oob_is_err=False),
                wait=[(sma, gz)], inc=[(smg4[slot], 16)])
            smg4[slot].tick(16)
        for jj in range(tg):
            for cch in range(8):
                k = jj * 8 + cch
                par = k % 2
                w = [(smg4[jj], smg4[jj].n)]
                if smt.n >= 2:
                    w.append((smt, smt.n - 1))
                ctx.emit("t", lambda t, jj=jj, cch=cch, par=par: t.transpose(
                    psT[:, par, 0:128], gath[:, jj, cch * 128:(cch + 1) * 128],
                    identb[:, :]), wait=w, inc=[(smt, 1)])
                smt.tick()
                wcp = [(smt, smt.n)]
                if gi >= 1 and k == 0:
                    wcp.append((smm, prev_mm))
                ctx.emit("s", lambda s, jj=jj, cch=cch, par=par: s.copy(
                    bufT[:, cch, jj * 128:(jj + 1) * 128], psT[:, par, 0:128]),
                    wait=wcp, inc=[(smt, 1)])
                smt.tick()
        tr_ok = smt.n
        wd = tg * 128
        for fc in range(32):
            it = fc % 2
            w2_ = [(smt, tr_ok), (smwl, w_ok)]
            if fc >= 2:
                w2_.append((sma, sma.n - 1))
            for mc in range(8):
                ctx.emit("t", lambda t, fc=fc, mc=mc, it=it, wd=wd: t.matmul(
                    psq[:, it, 0:wd], w1S[:, mc, fc * 128:(fc + 1) * 128],
                    bufT[:, mc, 0:wd], start=(mc == 0), stop=(mc == 7)),
                    wait=w2_ if mc == 0 else (),
                    inc=[(smm, 1)] if mc == 7 else [])
            mmw = smm.tick()
            ctx.emit("s", lambda s, fc=fc, it=it, wd=wd: s.activation(
                h1sb[:, fc, 0:wd], psq[:, it, 0:wd], AF.Silu),
                wait=[(smm, mmw)], inc=[(sma, 1)])
            sma.tick()
        h1_ok = sma.n
        for jj in range(tg):
            for oh in range(2):
                it2 = jj * 2 + oh
                par = it2 % 2
                w2_ = [(sma, h1_ok)]
                if smo.n >= 2:
                    w2_.append((smo, smo.n - 1))
                for fc in range(32):
                    ctx.emit("t", lambda t, jj=jj, oh=oh, fc=fc, par=par: t.matmul(
                        psO[:, par, :], h1sb[:, fc, jj * 128:(jj + 1) * 128],
                        w2S[:, fc, oh * 512:(oh + 1) * 512],
                        start=(fc == 0), stop=(fc == 31)),
                        wait=w2_ if fc == 0 else (),
                        inc=[(smm, 1)] if fc == 31 else [])
                mmw = smm.tick()
                sslot = it2 % 2
                ctx.emit("v", lambda v, sslot=sslot, oh=oh, par=par: v.tensor_copy(
                    orow[:, sslot, :], psO[:, par, :]),
                    wait=[(smm, mmw), (smst2[sslot], smst2[sslot].n)],
                    inc=[(smo, 1)])
                ow = smo.tick()
                j = j0 + jj
                if isinstance(out_d, list):
                    ck = 0 if j < 16 else (1 if j < 32 else 2)
                    odt, jl = out_d[ck], j - (0, 16, 32)[ck]
                else:
                    odt, jl = out_d, j
                ctx.dma("g", odt.rearrange("(a p) d -> p a d", p=128)[
                    :, jl, oh * 512:(oh + 1) * 512], orow[:, sslot, :],
                    wait=[(smo, ow)], inc=[(smst2[sslot], 16)])
                smst2[sslot].tick(16)
        if o_hook is not None:
            if gi in (3, 7):
                _ck[gi] = [(smst2[0], smst2[0].n), (smst2[1], smst2[1].n)]
            elif gi == 5:
                o_hook(0, _ck[3])
            elif gi == 9:
                o_hook(1, _ck[7])

    if o_hook is not None:
        o_hook(2, [(smst2[0], smst2[0].n), (smst2[1], smst2[1].n)])
    ctx.emit("y", None, wait=[(smst2[0], smst2[0].n), (smst2[1], smst2[1].n)])
    ctx.barrier(all_sems=True)
    ctx.free_to(_m_all)


def emit_moe_combine(ctx, ext):
    nc = ctx.nc
    ctx._eps_t = None
    _m_all = ctx.mark()
    x = ext["x"]
    h_in = ext["h_in"]
    einfo = ext["einfo"]
    counts_all = ext["counts_all"]
    cid = ext["cid"]
    out_all = ext["out_all"]
    xo = ext["xo"]
    c_ident = ctx.il0(np_ident(), "identsrc")
    c_iota8 = ctx.il0(
        np.tile(np.arange(8, dtype=np.float32)[None, :], (128, 1)), "iota8c")

    xv = x.rearrange("(n p) d -> p n d", p=128)
    hv = h_in.rearrange("(n p) d -> p n d", p=128)
    ev = einfo.rearrange("(n p) e -> p n e", p=128)
    xov = xo.rearrange("(n p) d -> p n d", p=128)

    identf = ctx.sb("identf", [128, 128], F32)
    iota8 = ctx.sb("iota8", [128, 8], F32)
    cnt = ctx.sb("cnt", [8, 8], F32)
    myoff = ctx.sb("myoff", [1, 8], F32)
    myoffB = ctx.sb("myoffB", [128, 8], F32)
    cid1 = ctx.sb("cid1", [1, 1], F32)
    onesf1 = ctx.sb("onesf1", [1, 128], F32)
    xt = ctx.sb("xt", [128, 2, 1024], F32)
    ht = ctx.sb("ht", [128, 2, 1024], BF16)
    et = ctx.sb("et", [128, 2, 8], F32)
    g0b = ctx.sb("g0b", [128, 2, 1024], BF16)
    g1b = ctx.sb("g1b", [128, 2, 1024], BF16)
    gcol = ctx.sb("gcol", [128, 4], F32)
    gcoli = ctx.sb("gcoli", [128, 2], I32)
    oh_ = ctx.sb("ohc2", [128, 8], F32)
    scr8 = ctx.sb("scr8", [128, 8], F32)
    psq = ctx.ps("psqC", [128, 2, 512], F32)

    smc = ctx.sem("consts")
    smt = ctx.sem("tps")
    sma = ctx.sem("acts")
    smld = [ctx.sem(f"lx{j}") for j in range(2)]
    smg = [ctx.sem(f"gt{j}") for j in range(2)]
    smst = [ctx.sem(f"sx{j}") for j in range(2)]
    smxe = ctx.sem("ev2")

    ctx.dma("g", identf[:, :], c_ident.ap()[:, :], inc=[(smc, 16)])
    smc.tick(16)
    ctx.dma("g", iota8[:, :], c_iota8.ap()[:, :], inc=[(smc, 16)])
    smc.tick(16)
    smcy = ctx.sem("lx2")
    ctx.dma("y", cnt[:, :], counts_all[:, :], inc=[(smcy, 16)])
    smcy.tick(16)
    ctx.dma("y", cid1[:, :], cid[None, :], inc=[(smcy, 16)])
    smcy.tick(16)
    consts_ok = smc.n
    ctx.emit("v", lambda v: v.memset(onesf1[:, :], 1.0), mark=True)
    ones_ok = ctx._last_inc["v"]

    iota8c2 = ctx.sb("iota8c2", [8, 1], F32)
    rowsel = ctx.sb("rowsel", [8, 1], BF16)
    c_iotac2 = ctx.il0(np.arange(8, dtype=np.float32)[:, None], "iota8cc2")
    ctx.dma("g", iota8c2[:, :], c_iotac2.ap()[:, :], inc=[(smc, 16)])
    smc.tick(16)
    cst2 = smc.n
    ctx.emit("t", lambda t: t.matmul(psq[0:8, 0, 0:1], onesf1[0:1, 0:8],
                                     cid1[:, :], start=True, stop=True),
             wait=[(smc, cst2), (smcy, smcy.n), ones_ok], inc=[(smt, 1)])
    smt.tick()
    ctx.emit("v", lambda v: v.tensor_scalar(
        rowsel[:, :], iota8c2[:, :], psq[0:8, 0, 0:1], None, ALU.is_lt),
        wait=[(smt, smt.n)], inc=[(smt, 1)])
    smt.tick()
    rowself = ctx.sb("rowself", [8, 1], F32)
    ctx.emit("v", lambda v: v.tensor_copy(rowself[:, :], rowsel[:, :]),
             wait=[(smt, smt.n)], inc=[(smt, 1)])
    smt.tick()
    ctx.emit("t", lambda t: t.matmul(psq[0:1, 1, 0:8], rowself[:, :], cnt[:, :],
                                     start=True, stop=True),
             wait=[(smt, smt.n)], inc=[(smt, 1)])
    smt.tick()
    ctx.emit("v", lambda v: v.tensor_copy(myoff[:, :], psq[0:1, 1, 0:8]),
             wait=[(smt, smt.n)], inc=[(smt, 1)])
    smt.tick()
    ctx.emit("t", lambda t: t.matmul(psq[:, 0, 0:8], onesf1[:, :], myoff[:, :],
                                     start=True, stop=True),
             wait=[(smt, smt.n)], inc=[(smt, 1)])
    smt.tick()
    ctx.emit("v", lambda v: v.tensor_copy(myoffB[:, :], psq[:, 0, 0:8]),
             wait=[(smt, smt.n)], inc=[(sma, 1)])
    off_ok = sma.tick()

    for i in range(NTILE):
        slot = i % 2
        wl = [(smst[slot], smst[slot].n)] if i >= 2 else []
        ctx.dma("y", xt[:, slot, :], xv[:, i, :], wait=wl, inc=[(smld[slot], 16)])
        smld[slot].tick(16)
        ctx.dma("y", ht[:, slot, :], hv[:, i, :], inc=[(smld[slot], 16)])
        smld[slot].tick(16)
        ctx.dma("y", et[:, slot, :], ev[:, i, :], inc=[(smld[slot], 16)])
        ld_ok = smld[slot].tick(16)
        for kk in range(2):
            ctx.emit("v", lambda v, slot=slot, kk=kk: v.tensor_scalar(
                oh_[:, :], iota8[:, :], et[:, slot, kk:kk + 1], None,
                ALU.is_equal),
                wait=[(smld[slot], ld_ok), (sma, off_ok)], dep=True, mark=True)
            ctx.emit("v", lambda v: v.tensor_tensor(
                oh_[:, :], oh_[:, :], myoffB[:, :], ALU.mult),
                dep=True, mark=True)
            ctx.emit("v", lambda v, kk=kk: v.reduce_sum(
                gcol[:, kk:kk + 1], oh_[:, :], axis=AX.X), dep=True, mark=True)
            ctx.emit("v", lambda v, slot=slot, kk=kk: v.tensor_tensor(
                gcol[:, kk:kk + 1], gcol[:, kk:kk + 1],
                et[:, slot, 4 + kk:5 + kk], ALU.add), dep=True, mark=True)
            ctx.emit("v", lambda v, kk=kk: v.tensor_scalar(
                gcol[:, 2 + kk:3 + kk], gcol[:, kk:kk + 1], float(ECAP), None,
                ALU.is_ge), dep=True, mark=True)
            ctx.emit("v", lambda v, slot=slot, kk=kk: v.scalar_tensor_tensor(
                gcol[:, kk:kk + 1], et[:, slot, kk:kk + 1], float(EPAD),
                gcol[:, kk:kk + 1], ALU.mult, ALU.add), dep=True, mark=True)
            ctx.emit("v", lambda v, kk=kk: v.scalar_tensor_tensor(
                gcol[:, kk:kk + 1], gcol[:, 2 + kk:3 + kk], 1.0e7,
                gcol[:, kk:kk + 1], ALU.mult, ALU.add), dep=True, mark=True)
            ctx.emit("v", lambda v, kk=kk: v.tensor_copy(
                gcoli[:, kk:kk + 1], gcol[:, kk:kk + 1]), dep=True,
                inc=[(sma, 1)])
            gi_ok = sma.tick()
            gb = g0b if kk == 0 else g1b
            ctx.emit("v", lambda v, gb=gb, slot=slot: v.memset(
                gb[:, slot, :], 0.0), dep=True, inc=[(sma, 1)])
            gz = sma.tick()
            ctx.emit("g", lambda g, gb=gb, slot=slot, kk=kk: g.indirect_dma_start(
                gb[:, slot, :], None, out_all[:, :],
                IndirectOffsetOnAxis(gcoli[:, kk:kk + 1], 0),
                bounds_check=ctx.breg(g, NEXP * EPAD - 1), oob_is_err=False),
                wait=[(sma, gz)], inc=[(smg[slot], 16)])
            smg[slot].tick(16)
        g_ok = smg[slot].n
        ctx.emit("v", lambda v, slot=slot: v.tensor_tensor(
            xt[:, slot, :], xt[:, slot, :], ht[:, slot, :], ALU.add),
            wait=[(smg[slot], g_ok)], dep=True, mark=True)
        ctx.emit("v", lambda v, slot=slot: v.scalar_tensor_tensor(
            xt[:, slot, :], g0b[:, slot, :], et[:, slot, 2:3],
            xt[:, slot, :], ALU.mult, ALU.add), dep=True, mark=True)
        ctx.emit("v", lambda v, slot=slot: v.scalar_tensor_tensor(
            xt[:, slot, :], g1b[:, slot, :], et[:, slot, 3:4],
            xt[:, slot, :], ALU.mult, ALU.add), dep=True, inc=[(smxe, 1)])
        smxe.tick()
        ctx.dma("g", xov[:, i, :], xt[:, slot, :],
                wait=[(smxe, smxe.n)], inc=[(smst[slot], 16)])
        smst[slot].tick(16)

    ctx.emit("y", None, wait=[(smst[0], smst[0].n), (smst[1], smst[1].n)])
    ctx.barrier(all_sems=True)
    ctx.free_to(_m_all)


_NC_CACHE = {}


def _get(name, builder, *a):
    if name not in _NC_CACHE:
        _NC_CACHE[name] = builder(*a)
    return _NC_CACHE[name]


def _run(nc, ins):
    res = run_bass_kernel_spmd(nc, ins, core_ids=list(range(NCORE)))
    return res.results


def run_l0_attn_a(x_full, wq, wk, wv, wg, wo):
    nc = _get("l0aa", build_l0_attn_a)
    ins = [{"x": np.ascontiguousarray(x_full[c]), "wq": wq, "wk": wk,
            "wv": wv, "wg": wg, "wo": wo} for c in range(NCORE)]
    r = _run(nc, ins)
    return np.stack([r[c]["xo"] for c in range(NCORE)], 0)


def run_l0_ffn(x_full, Win, bin_, W1, b1, W2, b2, Wout, bout):
    nc = _get("l0ffn", build_l0_ffn)
    ins = [{"x": np.ascontiguousarray(x_full[c]), "Win": Win, "bin_": bin_,
            "W1": W1, "b1": b1, "W2": W2, "b2": b2, "Wout": Wout,
            "bout": bout} for c in range(NCORE)]
    r = _run(nc, ins)
    return np.stack([r[c]["xo"] for c in range(NCORE)], 0)


def run_l0_attn_b(x_full, W_qkvb, W_o, ln_g, ln_b):
    nc = _get("l0ab", build_l0_attn_b)
    ins = []
    for c in range(NCORE):
        xr = np.ascontiguousarray(
            x_full[:, c * 256:(c + 1) * 256, :].reshape(S, D))
        ins.append({"x": xr, "W_qkvb": W_qkvb, "W_o": W_o, "ln_g": ln_g,
                    "ln_b": ln_b})
    r = _run(nc, ins)
    out = np.empty_like(x_full)
    for c in range(NCORE):
        out[:, c * 256:(c + 1) * 256, :] = r[c]["xo"].reshape(B, 256, D)
    return out


def run_l1_attn(variant, x_full, **w):
    nc = _get("l1" + variant, build_l1_attn, variant)
    ins = [dict({"x": np.ascontiguousarray(x_full[c])}, **w)
           for c in range(NCORE)]
    r = _run(nc, ins)
    return np.stack([r[c]["xo"] for c in range(NCORE)], 0)


def run_moe(x_full, router_w, w1, w2):
    import ml_dtypes
    ncA = _get("moeA", build_moe_route)
    ins = [{"x": np.ascontiguousarray(x_full[c]), "rw": router_w,
            "cid": np.array([c], np.float32)} for c in range(NCORE)]
    rA = _run(ncA, ins)
    h_all = np.concatenate([rA[c]["h_out"] for c in range(NCORE)], 0)
    einfo = [rA[c]["einfo"] for c in range(NCORE)]
    counts = np.stack([np.asarray(rA[c]["counts"]).reshape(NEXP)
                       for c in range(NCORE)], 0).astype(np.float32)
    idxp = np.stack([rA[c]["idxp"] for c in range(NCORE)], 0)

    ncB = _get("moeB", build_moe_expert)
    ins = [{"h_all": h_all, "idxp_all": np.ascontiguousarray(idxp[:, c, :]),
            "counts_all": counts, "cid": np.array([c], np.float32),
            "w1": np.ascontiguousarray(w1[c]),
            "w2": np.ascontiguousarray(w2[c])} for c in range(NCORE)]
    rB = _run(ncB, ins)
    out_all = np.concatenate([rB[c]["out_d"] for c in range(NCORE)], 0)

    ncC = _get("moeC", build_moe_combine)
    ins = [{"x": np.ascontiguousarray(x_full[c]),
            "h_in": np.ascontiguousarray(h_all[c * S:(c + 1) * S]),
            "einfo": einfo[c], "counts_all": counts,
            "cid": np.array([c], np.float32), "out_all": out_all}
           for c in range(NCORE)]
    rC = _run(ncC, ins)
    return np.stack([rC[c]["xo"] for c in range(NCORE)], 0)


_FUSED = True


def kernel_fused(**inputs):
    f32 = lambda k: np.ascontiguousarray(np.asarray(inputs[k], np.float32))
    nc = _get("fused", build_fused)
    m = {
        "wq": f32("L0_attn_a_wq"), "wk": f32("L0_attn_a_wk"),
        "wv": f32("L0_attn_a_wv"), "wg": f32("L0_attn_a_wg"),
        "wo": f32("L0_attn_a_wo"),
        "Win": f32("L0_ffn_Win"), "bin_": f32("L0_ffn_bin_"),
        "W1": f32("L0_ffn_W1"), "b1": f32("L0_ffn_b1"),
        "W2": f32("L0_ffn_W2"), "b2": f32("L0_ffn_b2"),
        "Wout": f32("L0_ffn_Wout"), "bout": f32("L0_ffn_bout"),
        "W_qkvb": f32("L0_attn_b_W_qkvb"), "W_o": f32("L0_attn_b_W_o"),
        "ln_g": f32("L0_attn_b_ln_g"), "ln_b": f32("L0_attn_b_ln_b"),
        "Wq": f32("L1_attn_a_Wq"), "Wk": f32("L1_attn_a_Wk"),
        "Wv": f32("L1_attn_a_Wv"), "Wproj": f32("L1_attn_a_Wproj"),
        "bproj": f32("L1_attn_a_bproj"), "Wpre": f32("L1_attn_a_Wpre"),
        "bpre": f32("L1_attn_a_bpre"), "pi": f32("L1_attn_a_pi"),
        "W_attn": f32("L1_attn_b_W_attn"), "b_attn": f32("L1_attn_b_b_attn"),
        "W_proj": f32("L1_attn_b_W_proj"), "b_proj": f32("L1_attn_b_b_proj"),
        "rw": f32("L1_ffn_router_w"),
    }
    w1a = f32("L1_ffn_w1")
    w2a = f32("L1_ffn_w2")
    x = f32("L0_attn_a_x")
    ins = []
    for c in range(NCORE):
        d = dict(m)
        d["x0"] = np.ascontiguousarray(x[c])
        d["cid"] = np.array([c], np.float32)
        d["w1"] = np.ascontiguousarray(w1a[c])
        d["w2"] = np.ascontiguousarray(w2a[c])
        ins.append(d)
    r = _run(nc, ins)
    return np.stack([r[c]["xo"] for c in range(NCORE)], 0)


def kernel(**inputs):
    if _FUSED:
        return kernel_fused(**inputs)
    return kernel_multi(**inputs)


def kernel_multi(**inputs):
    f32 = lambda k: np.ascontiguousarray(np.asarray(inputs[k], np.float32))
    x = f32("L0_attn_a_x")
    x = run_l0_attn_a(x, f32("L0_attn_a_wq"), f32("L0_attn_a_wk"),
                      f32("L0_attn_a_wv"), f32("L0_attn_a_wg"),
                      f32("L0_attn_a_wo"))
    ffn_args = (f32("L0_ffn_Win"), f32("L0_ffn_bin_"), f32("L0_ffn_W1"),
                f32("L0_ffn_b1"), f32("L0_ffn_W2"), f32("L0_ffn_b2"),
                f32("L0_ffn_Wout"), f32("L0_ffn_bout"))
    x = run_l0_ffn(x, *ffn_args)
    x = run_l0_attn_b(x, f32("L0_attn_b_W_qkvb"), f32("L0_attn_b_W_o"),
                      f32("L0_attn_b_ln_g"), f32("L0_attn_b_ln_b"))
    x = run_l0_ffn(x, *ffn_args)
    x = run_l1_attn("a", x, Wq=f32("L1_attn_a_Wq"), Wk=f32("L1_attn_a_Wk"),
                    Wv=f32("L1_attn_a_Wv"), Wproj=f32("L1_attn_a_Wproj"),
                    bproj=f32("L1_attn_a_bproj"), Wpre=f32("L1_attn_a_Wpre"),
                    bpre=f32("L1_attn_a_bpre"), pi=f32("L1_attn_a_pi"))
    moe_args = (f32("L1_ffn_router_w"), f32("L1_ffn_w1"), f32("L1_ffn_w2"))
    x = run_moe(x, *moe_args)
    x = run_l1_attn("b", x, W_attn=f32("L1_attn_b_W_attn"),
                    b_attn=f32("L1_attn_b_b_attn"),
                    W_proj=f32("L1_attn_b_W_proj"),
                    b_proj=f32("L1_attn_b_b_proj"))
    x = run_moe(x, *moe_args)
    return x


def _ext_new(nc, specs):
    d = {}
    for n, s, dt_, kind in specs:
        d[n] = nc.dram_tensor(n, s, dt_, kind=kind).ap()
    return d


def build_l0_attn_a():
    nc = bass.Bass("TRN2", target_bir_lowering=False)
    ext = _ext_new(nc, [("x", [S, D], F32, "ExternalInput")] +
                   [(k, [D, D], F32, "ExternalInput")
                    for k in ("wq", "wk", "wv", "wg", "wo")] +
                   [("xo", [S, D], F32, "ExternalOutput")])
    ctx = Ctx(nc)
    emit_l0_attn_a(ctx, ext)
    ctx.finish(nc)
    return nc


def build_l0_ffn():
    nc = bass.Bass("TRN2", target_bir_lowering=False)
    ext = _ext_new(nc, [("x", [S, D], F32, "ExternalInput"),
                        ("Win", [D, 64], F32, "ExternalInput"),
                        ("bin_", [64], F32, "ExternalInput"),
                        ("W1", [64, 16], F32, "ExternalInput"),
                        ("b1", [64, 16], F32, "ExternalInput"),
                        ("W2", [64, 16], F32, "ExternalInput"),
                        ("b2", [64], F32, "ExternalInput"),
                        ("Wout", [64, D], F32, "ExternalInput"),
                        ("bout", [D], F32, "ExternalInput"),
                        ("xo", [S, D], F32, "ExternalOutput")])
    ctx = Ctx(nc)
    emit_l0_ffn(ctx, ext)
    ctx.finish(nc)
    return nc


def build_l1_attn(variant):
    nc = bass.Bass("TRN2", target_bir_lowering=False)
    if variant == "b":
        specs = [("x", [S, D], F32, "ExternalInput"),
                 ("W_attn", [D, 3 * D], F32, "ExternalInput"),
                 ("b_attn", [3 * D], F32, "ExternalInput"),
                 ("W_proj", [D, D], F32, "ExternalInput"),
                 ("b_proj", [D], F32, "ExternalInput")]
    else:
        specs = [("x", [S, D], F32, "ExternalInput")] + \
                [(k, [D, D], F32, "ExternalInput")
                 for k in ("Wq", "Wk", "Wv", "Wproj", "Wpre")] + \
                [("bproj", [D], F32, "ExternalInput"),
                 ("bpre", [D], F32, "ExternalInput"),
                 ("pi", [1], F32, "ExternalInput")]
    specs.append(("xo", [S, D], F32, "ExternalOutput"))
    ext = _ext_new(nc, specs)
    ctx = Ctx(nc)
    emit_l1_attn(ctx, ext, variant)
    ctx.finish(nc)
    return nc


def build_l0_attn_b():
    nc = bass.Bass("TRN2", target_bir_lowering=False)
    ext = _ext_new(nc, [("x", [S, D], F32, "ExternalInput"),
                        ("W_qkvb", [3088, D], F32, "ExternalInput"),
                        ("W_o", [D, D], F32, "ExternalInput"),
                        ("ln_g", [D], F32, "ExternalInput"),
                        ("ln_b", [D], F32, "ExternalInput"),
                        ("xo", [S, D], F32, "ExternalOutput")])
    ctx = Ctx(nc)
    emit_l0_attn_b(ctx, ext)
    ctx.finish(nc)
    return nc


def build_moe_route():
    nc = bass.Bass("TRN2", target_bir_lowering=False)
    ext = _ext_new(nc, [("x", [S, D], F32, "ExternalInput"),
                        ("rw", [NEXP, D], F32, "ExternalInput"),
                        ("cid", [1], F32, "ExternalInput"),
                        ("h_out", [S, D], BF16, "ExternalOutput"),
                        ("einfo", [S, 8], F32, "ExternalOutput"),
                        ("counts", [NEXP], F32, "ExternalOutput"),
                        ("idxp", [NEXP, EPAD + 4096], F32, "ExternalOutput")])
    ctx = Ctx(nc)
    emit_moe_route(ctx, ext)
    ctx.finish(nc)
    return nc


def build_moe_expert():
    nc = bass.Bass("TRN2", target_bir_lowering=False)
    ext = _ext_new(nc, [("h_all", [B * S, D], BF16, "ExternalInput"),
                        ("idxp_all", [NCORE, EPAD + 4096], F32, "ExternalInput"),
                        ("counts_all", [NCORE, NEXP], F32, "ExternalInput"),
                        ("cid", [1], F32, "ExternalInput"),
                        ("w1", [D, 4096], F32, "ExternalInput"),
                        ("w2", [4096, D], F32, "ExternalInput"),
                        ("out_d", [EPAD, D], BF16, "ExternalOutput")])
    ctx = Ctx(nc)
    emit_moe_expert(ctx, ext)
    ctx.finish(nc)
    return nc


def build_moe_combine():
    nc = bass.Bass("TRN2", target_bir_lowering=False)
    ext = _ext_new(nc, [("x", [S, D], F32, "ExternalInput"),
                        ("h_in", [S, D], BF16, "ExternalInput"),
                        ("einfo", [S, 8], F32, "ExternalInput"),
                        ("counts_all", [NCORE, NEXP], F32, "ExternalInput"),
                        ("cid", [1], F32, "ExternalInput"),
                        ("out_all", [NEXP * EPAD, D], BF16, "ExternalInput"),
                        ("xo", [S, D], F32, "ExternalOutput")])
    ctx = Ctx(nc)
    emit_moe_combine(ctx, ext)
    ctx.finish(nc)
    return nc


RG8 = [list(range(8))]


def emit_collective(ctx, kind, op, in_ap, out_ap, wait=()):
    sm = ctx.sem()
    ctx.emit("g", lambda g: g.collective_compute(
        kind, op, RG8, [in_ap], [out_ap]), wait=wait, inc=[(sm, 1)])
    sm.tick()
    return sm, sm.n


def emit_moe_fused(ctx, tag, x_in, x_out, W):
    nc = ctx.nc
    ctx.prefix = tag + "r_"
    h_loc = nc.dram_tensor(tag + "h_loc", [S, D], BF16, kind="Internal").ap()
    einfo_loc = nc.dram_tensor(tag + "einfo", [S, 8], F32, kind="Internal").ap()
    counts_loc = nc.dram_tensor(tag + "cnt", [NEXP], F32, kind="Internal").ap()
    idxp_loc = nc.dram_tensor(tag + "idxp", [NEXP, EPAD + 4096], F32,
                              kind="Internal").ap()
    h_all = nc.dram_tensor(tag + "h_all", [B * S, D], BF16, kind="Internal").ap()
    counts_all = nc.dram_tensor(tag + "cnt_all", [NCORE, NEXP], F32,
                                kind="Internal").ap()
    idxp_all = nc.dram_tensor(tag + "idxp_all", [NCORE * NEXP, EPAD + 4096],
                              F32, kind="Internal").ap()
    out_loc = nc.dram_tensor(tag + "out_loc", [EPAD, D], BF16,
                             kind="Internal").ap()
    out_all = nc.dram_tensor(tag + "out_all", [NEXP * EPAD, D], BF16,
                             kind="Internal").ap()
    def h_hook(waits):
        emit_collective(ctx, "AllGather", ALU.bypass, h_loc[:, :],
                        h_all[:, :], wait=waits)

    emit_moe_route(ctx, {"x": x_in, "rw": W["rw"], "cid": W["cid"],
                         "h_out": h_loc, "einfo": einfo_loc,
                         "counts": counts_loc, "idxp": idxp_loc,
                         "h_hook": h_hook})
    emit_collective(ctx, "AllGather", ALU.bypass, counts_loc[None, :],
                    counts_all[:, :])
    sm, smn = emit_collective(ctx, "AllGather", ALU.bypass, idxp_loc[:, :],
                              idxp_all[:, :])
    ctx.emit("g", None, wait=[(sm, smn)])
    ctx.barrier()
    ctx.prefix = tag + "e_"
    emit_moe_expert(ctx, {"h_all": h_all, "idxp_all": idxp_all,
                          "counts_all": counts_all, "cid": W["cid"],
                          "w1": W["w1"], "w2": W["w2"], "out_d": out_loc},
                    fused=True)
    emit_collective(ctx, "AllGather", ALU.bypass, out_loc[:, :], out_all[:, :])
    ctx.barrier()
    ctx.prefix = tag + "c_"
    emit_moe_combine(ctx, {"x": x_in, "h_in": h_loc, "einfo": einfo_loc,
                           "counts_all": counts_all, "cid": W["cid"],
                           "out_all": out_all, "xo": x_out})


def build_fused():
    nc = bass.Bass("TRN2", target_bir_lowering=False)
    E = {}
    E["x0"] = nc.dram_tensor("x0", [S, D], F32, kind="ExternalInput").ap()
    E["cid"] = nc.dram_tensor("cid", [1], F32, kind="ExternalInput").ap()
    names = [("wq", [D, D]), ("wk", [D, D]), ("wv", [D, D]), ("wg", [D, D]),
             ("wo", [D, D]),
             ("Win", [D, 64]), ("bin_", [64]), ("W1", [64, 16]),
             ("b1", [64, 16]), ("W2", [64, 16]), ("b2", [64]),
             ("Wout", [64, D]), ("bout", [D]),
             ("W_qkvb", [3088, D]), ("W_o", [D, D]), ("ln_g", [D]),
             ("ln_b", [D]),
             ("Wq", [D, D]), ("Wk", [D, D]), ("Wv", [D, D]),
             ("Wproj", [D, D]), ("bproj", [D]), ("Wpre", [D, D]),
             ("bpre", [D]), ("pi", [1]),
             ("W_attn", [D, 3 * D]), ("b_attn", [3 * D]),
             ("W_proj", [D, D]), ("b_proj", [D]),
             ("rw", [NEXP, D]), ("w1", [D, 4096]), ("w2", [4096, D])]
    for n, s in names:
        E[n] = nc.dram_tensor(n, s, F32, kind="ExternalInput").ap()
    xout = nc.dram_tensor("xo", [S, D], F32, kind="ExternalOutput").ap()
    xa = nc.dram_tensor("xa", [S, D], F32, kind="Internal").ap()
    xb = nc.dram_tensor("xb", [S, D], F32, kind="Internal").ap()
    xab = nc.dram_tensor("xab", [S, D], F32, kind="Internal").ap()
    xab2 = nc.dram_tensor("xab2", [S, D], F32, kind="Internal").ap()
    cidrow = nc.dram_tensor("cidrow", [1, 1], F32, kind="Internal").ap()

    ctx = Ctx(nc)

    ctx.prefix = "s1_"
    emit_l0_attn_a(ctx, {"x": E["x0"], "wq": E["wq"], "wk": E["wk"],
                         "wv": E["wv"], "wg": E["wg"], "wo": E["wo"],
                         "xo": xa})
    ctx.prefix = "s2_"
    emit_l0_ffn(ctx, {"x": xa, "Win": E["Win"], "bin_": E["bin_"],
                      "W1": E["W1"], "b1": E["b1"], "W2": E["W2"],
                      "b2": E["b2"], "Wout": E["Wout"], "bout": E["bout"],
                      "xo": xb})
    ctx.prefix = "s3_"
    sm, smn = emit_collective(ctx, "AllToAll", ALU.bypass, xb[:, :],
                              xab[:, :])
    ctx.emit("g", None, wait=[(sm, smn)])
    ctx.barrier()
    emit_l0_attn_b(ctx, {"x": xab, "W_qkvb": E["W_qkvb"], "W_o": E["W_o"],
                         "ln_g": E["ln_g"], "ln_b": E["ln_b"], "xo": xab2})
    sm2, smn2 = emit_collective(ctx, "AllToAll", ALU.bypass, xab2[:, :],
                                xa[:, :])
    ctx.emit("g", None, wait=[(sm2, smn2)])
    ctx.barrier()
    ctx.prefix = "s4_"
    emit_l0_ffn(ctx, {"x": xa, "Win": E["Win"], "bin_": E["bin_"],
                      "W1": E["W1"], "b1": E["b1"], "W2": E["W2"],
                      "b2": E["b2"], "Wout": E["Wout"], "bout": E["bout"],
                      "xo": xb})
    ctx.prefix = "s5_"
    emit_l1_attn(ctx, {"x": xb, "Wq": E["Wq"], "Wk": E["Wk"], "Wv": E["Wv"],
                       "Wproj": E["Wproj"], "bproj": E["bproj"],
                       "Wpre": E["Wpre"], "bpre": E["bpre"], "pi": E["pi"],
                       "xo": xa}, "a")
    emit_moe_fused(ctx, "m6_", xa, xb,
                   {"rw": E["rw"], "w1": E["w1"], "w2": E["w2"],
                    "cid": E["cid"]})
    ctx.prefix = "s7_"
    emit_l1_attn(ctx, {"x": xb, "W_attn": E["W_attn"], "b_attn": E["b_attn"],
                       "W_proj": E["W_proj"], "b_proj": E["b_proj"],
                       "xo": xa}, "b")
    emit_moe_fused(ctx, "m8_", xa, xout,
                   {"rw": E["rw"], "w1": E["w1"], "w2": E["w2"],
                    "cid": E["cid"]})
    ctx.finish(nc)
    return nc
